# version 56
# baseline (speedup 1.0000x reference)
import numpy as np
import ml_dtypes
from contextlib import ExitStack

import concourse.bass as bass
import concourse.mybir as mybir
from concourse.bass_utils import run_bass_kernel_spmd

F32 = mybir.dt.float32
BF16 = mybir.dt.bfloat16
AF = mybir.ActivationFunctionType
ALU = mybir.AluOpType
AX = mybir.AxisListType
BF = ml_dtypes.bfloat16

C = 1024
HID = 2048
IFF = 2816
NH = 16
HD = 64
QB = 16
C6 = 0.16666667163372039794921875
LN32 = 3.4657359027997265

ENGINES = ('sync', 'scalar', 'vector', 'tensor', 'gpsimd')


class Prog:

    def __init__(self, nc, es):
        self.nc = nc
        self.items = {e: [] for e in ENGINES}
        self.ecounts = {e: 0 for e in ENGINES}
        self.progsem = {e: es.enter_context(nc.semaphore(name=f'pg_{e}'))
                        for e in ENGINES}
        self.dma_sems = {}
        self.dma_chain = {}
        self.es = es

    def _dep_list(self, deps):
        out = []
        for d in deps:
            if d is None:
                continue
            if isinstance(d, (list, tuple)) and d and isinstance(d[0], (list, tuple)):
                out.extend(x for x in d if x is not None)
            else:
                out.append(d)
        return out

    def op(self, eng, emit, deps=()):
        idx = self.ecounts[eng]
        self.ecounts[eng] += 1
        self.items[eng].append({'emit': emit, 'deps': self._dep_list(deps),
                                'dma': None})
        return ('E', eng, idx)

    def dma(self, eng, emit, sem, deps=()):
        if sem not in self.dma_sems:
            self.dma_sems[sem] = [self.es.enter_context(
                self.nc.semaphore(name=f'dm_{sem}')), 0]
        self.dma_sems[sem][1] += 1
        k = self.dma_sems[sem][1]
        dl = self._dep_list(deps)
        prev = self.dma_chain.get(sem)
        if prev is not None:
            dl.append(prev)
        tok = ('D', sem, k)
        self.dma_chain[sem] = tok
        self.items[eng].append({'emit': emit, 'deps': dl, 'dma': (sem, k)})
        return tok

    def wait(self, eng, deps):
        self.items[eng].append({'emit': None, 'deps': self._dep_list(deps),
                                'dma': None})
        return None

    def barrier(self, engines=ENGINES):
        toks = []
        for e in engines:
            n = self.ecounts[e]
            if n:
                toks.append(('E', e, n - 1))
        for s, (h, cnt) in self.dma_sems.items():
            if cnt:
                toks.append(('D', s, cnt))
        for e in engines:
            self.wait(e, toks)
        return toks

    def finalize(self, block):
        referenced = set()
        for eng in ENGINES:
            for it in self.items[eng]:
                for d in it['deps']:
                    if d[0] == 'E' and d[1] == 'tensor' and eng != 'tensor':
                        referenced.add(d[2])
        tidx_sorted = sorted(referenced)
        import bisect
        tinc_of = {}
        for n_, i_ in enumerate(tidx_sorted):
            tinc_of[i_] = n_ + 1
        self._tensor_ref = referenced
        last_t = self.ecounts['tensor'] - 1
        total_tincs = len(tidx_sorted) + (0 if last_t in referenced else 1)

        def tensor_wait_count(i2):
            j = bisect.bisect_left(tidx_sorted, i2)
            if j >= len(tidx_sorted):
                return total_tincs
            return tinc_of[tidx_sorted[j]]

        for eng in ENGINES:
            items = self.items[eng]
            if not items:
                continue
            section = getattr(block, eng)

            def make_body(items=items, eng=eng):
                def body(E):
                    water = {}
                    tcount = 0
                    tseen = -1
                    for it in items:
                        for d in it['deps']:
                            kind = d[0]
                            if kind == 'E':
                                _, e2, i2 = d
                                if e2 == 'tensor':
                                    if eng == 'tensor':
                                        continue
                                    need = tensor_wait_count(i2)
                                else:
                                    need = i2 + 1
                                key = ('E', e2)
                                if water.get(key, 0) < need:
                                    E.wait_ge(self.progsem[e2], need)
                                    water[key] = need
                            else:
                                _, sname, k = d
                                need = 16 * k
                                key = ('D', sname)
                                if water.get(key, 0) < need:
                                    E.wait_ge(self.dma_sems[sname][0], need)
                                    water[key] = need
                        if it['emit'] is None:
                            continue
                        inst = it['emit'](E)
                        if it['dma'] is not None:
                            inst.then_inc(self.dma_sems[it['dma'][0]][0], 16)
                        elif eng == 'tensor':
                            tseen += 1
                            if tseen in self._tensor_ref or tseen == self.ecounts['tensor'] - 1:
                                inst.then_inc(self.progsem[eng], 1)
                                tcount += 1
                        else:
                            inst.then_inc(self.progsem[eng], 1)
                return body
            section(make_body())


class Buf:
    def __init__(self, ap, name=''):
        self.ap = ap
        self.name = name
        self.w = None
        self.readers = []

    def rdep(self):
        return [self.w] if self.w is not None else []

    def wdep(self):
        d = list(self.readers)
        if self.w is not None:
            d.append(self.w)
        return d

    def note_read(self, tok):
        if tok is not None:
            self.readers.append(tok)

    def note_write(self, tok):
        self.w = tok
        self.readers = []


def oprw(p, eng, emit, reads=(), writes=(), extra=()):
    deps = list(extra)
    for b in reads:
        deps += b.rdep()
    for b in writes:
        deps += b.wdep()
    tok = p.op(eng, emit, deps)
    for b in reads:
        b.note_read(tok)
    for b in writes:
        b.note_write(tok)
    return tok


def dmarw(p, eng, emit, sem, reads=(), writes=(), extra=()):
    deps = list(extra)
    for b in reads:
        deps += b.rdep()
    for b in writes:
        deps += b.wdep()
    tok = p.dma(eng, emit, sem, deps)
    for b in reads:
        b.note_read(tok)
    for b in writes:
        b.note_write(tok)
    return tok



def build_nc(S_batch, S_core, n_cores_per_batch):
    nc = bass.Bass('TRN2', target_bir_lowering=False)
    T_batch = S_batch // 128
    T_own = S_core // 128
    cpb = n_cores_per_batch

    d_x_all = nc.dram_tensor('x_all', [S_batch, C], F32, kind='ExternalInput').ap()
    d_x_own = nc.dram_tensor('x_own', [S_core, C], F32, kind='ExternalInput').ap()
    d_scale1 = nc.dram_tensor('scale1', [1, C], F32, kind='ExternalInput').ap()
    d_scale2 = nc.dram_tensor('scale2', [1, C], F32, kind='ExternalInput').ap()
    d_gamma = nc.dram_tensor('gamma', [1, C], F32, kind='ExternalInput').ap()
    d_pb = nc.dram_tensor('proj_b', [1, C], F32, kind='ExternalInput').ap()
    d_c1b = nc.dram_tensor('conv1_b', [1, HID], F32, kind='ExternalInput').ap()
    d_c2b = nc.dram_tensor('conv2_b', [1, C], F32, kind='ExternalInput').ap()
    d_qkvT = nc.dram_tensor('qkv_wT', [C, 3 * C], F32, kind='ExternalInput').ap()
    d_projT = nc.dram_tensor('proj_wT', [C, C], F32, kind='ExternalInput').ap()
    d_c1T = nc.dram_tensor('conv1_wT', [C, HID], F32, kind='ExternalInput').ap()
    d_c2T = nc.dram_tensor('conv2_wT', [HID, C], F32, kind='ExternalInput').ap()
    d_wg = nc.dram_tensor('w_gate', [IFF, C], F32, kind='ExternalInput').ap()
    d_wu = nc.dram_tensor('w_up', [IFF, C], F32, kind='ExternalInput').ap()
    d_wd = nc.dram_tensor('w_down', [C, IFF], F32, kind='ExternalInput').ap()
    d_ident = nc.dram_tensor('ident', [128, 128], BF16, kind='ExternalInput').ap()
    d_ones1 = nc.dram_tensor('ones1', [1, 128], BF16, kind='ExternalInput').ap()
    d_y = nc.dram_tensor('y', [S_core, C], F32, kind='ExternalOutput').ap()
    d_qspill = nc.dram_tensor('qspill', [S_core, C], BF16, kind='Internal').ap()
    d_hqspill = nc.dram_tensor('hqspill', [S_core, IFF], BF16, kind='Internal').ap()
    d_xqspill = nc.dram_tensor('xqspill', [S_core, C], BF16, kind='Internal').ap()
    d_y1spill = nc.dram_tensor('y1spill', [S_core, C], F32, kind='Internal').ap()
    d_xn2spill = nc.dram_tensor('xn2spill', [S_core, C], BF16, kind='Internal').ap()
    d_oaspill = nc.dram_tensor('oaspill', [S_core, C], F32, kind='Internal').ap()
    d_wgq = nc.dram_tensor('wgq', [IFF, C], BF16, kind='Internal').ap()
    d_wuq = nc.dram_tensor('wuq', [IFF, C], BF16, kind='Internal').ap()
    d_wdq = nc.dram_tensor('wdq', [C, IFF], BF16, kind='Internal').ap()

    es = ExitStack()
    esA = ExitStack()
    esQ = ExitStack()
    esW = ExitStack()
    esF1 = ExitStack()
    esF2 = ExitStack()
    esB = ExitStack()
    esB2 = ExitStack()

    def SB(name, shape, dt, stack=None):
        h = (stack or es).enter_context(nc.sbuf_tensor(name, shape, dt))
        return Buf(h, name)

    def PSUM(name, shape, dt):
        h = es.enter_context(nc.psum_tensor(name, shape, dt))
        return Buf(h, name)

    p = Prog(nc, es)

    ident = SB('identsb', [128, 128], BF16)
    ones1 = SB('ones1sb', [1, 128], BF16)
    s1col = SB('s1col', [128, 8], F32)
    s2col = SB('s2col', [128, 8], F32)
    gammar = SB('gammar', [128, C], F32)
    gpb = SB('gpb', [1, C], BF16)
    gc2b = SB('gc2b', [1, C], BF16)
    c1brow = SB('c1brow', [1, HID], BF16)
    stats = SB('stats', [128, 2 * T_own], F32)
    kv_b = SB('kv_b', [128, 8 * 130], BF16)

    PA = PSUM('PA', [128, 512], F32)
    PB = PSUM('PB', [128, 512], F32)
    PC = PSUM('PC', [128, 512], F32)
    PD = PSUM('PD', [128, 512], F32)
    KV1 = PSUM('KV1', [128, 512], F32)
    KV2 = PSUM('KV2', [128, 512], F32)
    KV3 = PSUM('KV3', [128, 512], F32)
    TR = PSUM('TR', [128, 1024], BF16)

    mvP = [SB('mva', [128, 16], F32), SB('mvb', [128, 16], F32), SB('mvc', [128, 16], F32)]

    q_sgn = SB('q_sgn', [128, 1024], F32, stack=esQ)
    q_ss = SB('q_ss', [128, 1024], F32, stack=esQ)
    q_lt = SB('q_lt', [128, 1024], BF16, stack=esQ)
    q_m1 = SB('q_m1', [128, 1024], BF16, stack=esQ)
    q_m2 = SB('q_m2', [128, 1024], BF16, stack=esQ)
    q_m3 = SB('q_m3', [128, 1024], BF16, stack=esQ)
    q_g2 = SB('q_g2', [128, 1024], BF16, stack=esQ)
    q_a = SB('q_a', [128, 1024], F32, stack=esQ)
    q2_a = SB('q2_a', [128, 1024], F32, stack=esQ)
    q_amax = SB('q_amax', [128, 64], F32, stack=esQ)
    q_scale = SB('q_scale', [128, 64], F32, stack=esQ)
    q_tmp = SB('q_tmp', [128, 64], F32, stack=esQ)
    q_hs = SB('q_hs', [128, 64], F32, stack=esQ)
    q_inv = SB('q_inv', [128, 64], F32, stack=esQ)
    q2_sgn = SB('q2_sgn', [128, 1024], F32, stack=esQ)
    q2_ss = SB('q2_ss', [128, 1024], F32, stack=esQ)
    q2_amax = SB('q2_amax', [128, 64], F32, stack=esQ)
    q2_scale = SB('q2_scale', [128, 64], F32, stack=esQ)
    q2_tmp = SB('q2_tmp', [128, 64], F32, stack=esQ)
    q2_hs = SB('q2_hs', [128, 64], F32, stack=esQ)
    q2_inv = SB('q2_inv', [128, 64], F32, stack=esQ)
    WJW = 1408
    wqld = [SB('wqld1', [128, WJW], F32, stack=esW),
            SB('wqld2', [128, WJW], F32, stack=esW)]
    wqout = SB('wqout', [128, WJW], BF16, stack=esW)
    rowtmp = SB('rowtmp', [1, C], F32, stack=esA)
    rowtmp2 = SB('rowtmp2', [1, C], F32, stack=esA)

    QSET0 = (q_sgn, q_amax, q_scale, q_tmp, q_hs, q_inv, q_ss, q_a)
    QSET1 = (q2_sgn, q2_amax, q2_scale, q2_tmp, q2_hs, q2_inv, q2_ss, q2_a)

    def emit_quant_pre_core(src, soff, W, qs, tok_extra=()):
        sgn, amax, scale, tmp, hs, inv, ssb, qa = qs
        NB = W // QB
        sa = src.ap[:, soff:soff + W]
        oprw(p, 'scalar', lambda sc, sa=sa, W=W, qa=qa: sc.activation(
            qa.ap[:, 0:W], sa, AF.Abs), reads=[src], writes=[qa], extra=tok_extra)
        oprw(p, 'scalar', lambda sc, sa=sa, W=W, sgn=sgn: sc.activation(
            sgn.ap[:, 0:W], sa, AF.Sign), reads=[src], writes=[sgn], extra=tok_extra)
        oprw(p, 'vector', lambda v, sa=sa, NB=NB, amax=amax: v.tensor_reduce(
            amax.ap[:, 0:NB].unsqueeze(2),
            sa.rearrange("p (a b) -> p a b", b=QB), AX.X, ALU.max,
            apply_absolute_value=True), reads=[src], writes=[amax], extra=tok_extra)
        oprw(p, 'vector', lambda v, NB=NB, amax=amax, scale=scale: v.tensor_scalar(
            scale.ap[:, 0:NB], amax.ap[:, 0:NB], C6, None, ALU.mult),
            reads=[amax], writes=[scale])
        oprw(p, 'vector', lambda v, NB=NB, amax=amax, scale=scale, tmp=tmp: v.scalar_tensor_tensor(
            tmp.ap[:, 0:NB], scale.ap[:, 0:NB], -6.0, amax.ap[:, 0:NB],
            ALU.mult, ALU.add), reads=[scale, amax], writes=[tmp])
        oprw(p, 'vector', lambda v, NB=NB, scale=scale, tmp=tmp: v.scalar_tensor_tensor(
            scale.ap[:, 0:NB], tmp.ap[:, 0:NB], C6, scale.ap[:, 0:NB],
            ALU.mult, ALU.add), reads=[tmp], writes=[scale])
        oprw(p, 'vector', lambda v, NB=NB, scale=scale: v.tensor_scalar(
            scale.ap[:, 0:NB], scale.ap[:, 0:NB], 1e-12, None, ALU.max),
            writes=[scale])
        oprw(p, 'vector', lambda v, NB=NB, scale=scale, hs=hs: v.tensor_scalar(
            hs.ap[:, 0:NB], scale.ap[:, 0:NB], 0.5, None, ALU.mult),
            reads=[scale], writes=[hs])
        oprw(p, 'vector', lambda v, NB=NB, scale=scale, inv=inv: v.reciprocal(
            inv.ap[:, 0:NB], scale.ap[:, 0:NB]), reads=[scale], writes=[inv])

    def emit_quant_pre_pool(W, qs, u_eng='gpsimd'):
        sgn, amax, scale, tmp, hs, inv, ssb, qa = qs
        NB = W // QB
        oprw(p, 'gpsimd', lambda g, NB=NB, W=W, sgn=sgn, hs=hs, ssb=ssb: g.tensor_tensor(
            ssb.ap[:, 0:W].rearrange("p (a b) -> p a b", b=QB),
            sgn.ap[:, 0:W].rearrange("p (a b) -> p a b", b=QB),
            hs.ap[:, 0:NB].unsqueeze(2).broadcast_to([128, NB, QB]),
            ALU.mult), reads=[sgn, hs], writes=[ssb])
        oprw(p, u_eng, lambda g, NB=NB, W=W, inv=inv, qa=qa: g.tensor_tensor(
            qa.ap[:, 0:W].rearrange("p (a b) -> p a b", b=QB),
            qa.ap[:, 0:W].rearrange("p (a b) -> p a b", b=QB),
            inv.ap[:, 0:NB].unsqueeze(2).broadcast_to([128, NB, QB]),
            ALU.mult), reads=[inv], writes=[qa])

    def emit_quant_pre(src, soff, W, qs, tok_extra=()):
        emit_quant_pre_core(src, soff, W, qs, tok_extra)
        emit_quant_pre_pool(W, qs)

    def emit_quant_main(dst, doff, W, qs):
        sgn, amax, scale, tmp, hs, inv, ssb, q_a = qs
        NB = W // QB
        oprw(p, 'vector', lambda g, W=W, q_a=q_a: g.tensor_scalar(
            q_lt.ap[:, 0:W], q_a.ap[:, 0:W], 0.25, None, ALU.is_gt),
            reads=[q_a], writes=[q_lt])
        oprw(p, 'vector', lambda g, W=W: g.tensor_scalar(
            q_m1.ap[:, 0:W], q_a.ap[:, 0:W], 0.75, None, ALU.is_gt),
            reads=[q_a], writes=[q_m1])
        oprw(p, 'vector', lambda g, W=W: g.tensor_scalar(
            q_m2.ap[:, 0:W], q_a.ap[:, 0:W], 1.25, None, ALU.is_gt),
            reads=[q_a], writes=[q_m2])
        oprw(p, 'vector', lambda g, W=W: g.tensor_scalar(
            q_m3.ap[:, 0:W], q_a.ap[:, 0:W], 1.75, None, ALU.is_gt),
            reads=[q_a], writes=[q_m3])
        oprw(p, 'vector', lambda g, W=W: g.tensor_tensor(
            q_lt.ap[:, 0:W], q_lt.ap[:, 0:W], q_m1.ap[:, 0:W], ALU.add),
            reads=[q_m1], writes=[q_lt])
        oprw(p, 'vector', lambda g, W=W: g.tensor_scalar(
            q_m1.ap[:, 0:W], q_a.ap[:, 0:W], 2.5, 2.0, ALU.is_gt, ALU.mult),
            reads=[q_a], writes=[q_m1])
        oprw(p, 'vector', lambda g, W=W: g.tensor_tensor(
            q_m2.ap[:, 0:W], q_m2.ap[:, 0:W], q_m3.ap[:, 0:W], ALU.add),
            reads=[q_m3], writes=[q_m2])
        oprw(p, 'vector', lambda g, W=W: g.tensor_scalar(
            q_m3.ap[:, 0:W], q_a.ap[:, 0:W], 3.5, 2.0, ALU.is_gt, ALU.mult),
            reads=[q_a], writes=[q_m3])
        oprw(p, 'vector', lambda g, W=W: g.tensor_tensor(
            q_lt.ap[:, 0:W], q_lt.ap[:, 0:W], q_m2.ap[:, 0:W], ALU.add),
            reads=[q_m2], writes=[q_lt])
        oprw(p, 'vector', lambda g, W=W: g.tensor_scalar(
            q_m2.ap[:, 0:W], q_a.ap[:, 0:W], 5.0, 4.0, ALU.is_gt, ALU.mult),
            reads=[q_a], writes=[q_m2])
        oprw(p, 'vector', lambda g, W=W: g.tensor_tensor(
            q_m1.ap[:, 0:W], q_m1.ap[:, 0:W], q_m3.ap[:, 0:W], ALU.add),
            reads=[q_m3], writes=[q_m1])
        oprw(p, 'vector', lambda g, W=W: g.tensor_tensor(
            q_m1.ap[:, 0:W], q_m1.ap[:, 0:W], q_m2.ap[:, 0:W], ALU.add),
            reads=[q_m2], writes=[q_m1])
        oprw(p, 'vector', lambda g, W=W: g.tensor_tensor(
            q_g2.ap[:, 0:W], q_lt.ap[:, 0:W], q_m1.ap[:, 0:W], ALU.add),
            reads=[q_lt, q_m1], writes=[q_g2])
        return oprw(p, 'vector', lambda v, W=W, doff=doff, ssb=ssb: v.tensor_tensor(
            dst.ap[:, doff:doff + W], q_g2.ap[:, 0:W], ssb.ap[:, 0:W],
            ALU.mult), reads=[q_g2, ssb], writes=[dst])

    def emit_quant(src, soff, dst, doff, W, tok_extra=(), eng='vector'):
        emit_quant_pre(src, soff, W, QSET0, tok_extra)
        return emit_quant_main(dst, doff, W, QSET0)

    dmarw(p, 'sync', lambda E: E.dma_start(ident.ap[:], d_ident[:]), 'win',
          writes=[ident])
    dmarw(p, 'sync', lambda E: E.dma_start(ones1.ap[:], d_ones1[:]), 'win',
          writes=[ones1])
    dmarw(p, 'sync', lambda E: E.dma_start(
        s1col.ap[:], d_scale1.rearrange("o (a p) -> (o p) a", p=128),
        allow_slow_non_contiguous=True), 'win', writes=[s1col])
    dmarw(p, 'sync', lambda E: E.dma_start(
        s2col.ap[:], d_scale2.rearrange("o (a p) -> (o p) a", p=128),
        allow_slow_non_contiguous=True), 'win', writes=[s2col])
    dmarw(p, 'sync', lambda E: E.dma_start(
        gammar.ap[:], d_gamma.partition_broadcast(128)), 'win', writes=[gammar])
    dmarw(p, 'sync', lambda E: E.dma_start(rowtmp.ap[:], d_pb[:]), 'win',
          writes=[rowtmp])
    oprw(p, 'vector', lambda v: v.tensor_tensor(
        gpb.ap[:], rowtmp.ap[:], gammar.ap[0:1, :], ALU.mult),
        reads=[rowtmp, gammar], writes=[gpb])
    dmarw(p, 'sync', lambda E: E.dma_start(rowtmp2.ap[:], d_c2b[:]), 'win',
          writes=[rowtmp2])
    oprw(p, 'vector', lambda v: v.tensor_tensor(
        gc2b.ap[:], rowtmp2.ap[:], gammar.ap[0:1, :], ALU.mult),
        reads=[rowtmp2, gammar], writes=[gc2b])
    dmarw(p, 'gpsimd', lambda E: E.dma_start(c1brow.ap[:], d_c1b[:]), 'gwin',
          writes=[c1brow])
    wTsh = esA.enter_context(nc.sbuf_tensor('wTs', [128, 8 * 3 * C], BF16))
    wTsC = [Buf(wTsh, f'wTs{kc}') for kc in range(8)]
    wld = SB('wld', [128, 2 * 3 * C], F32, stack=esA)
    for kc in range(8):
        half = kc % 2
        ld = dmarw(p, 'sync', lambda E, kc=kc, half=half: E.dma_start(
            wld.ap[:, half * 3 * C:(half + 1) * 3 * C],
            d_qkvT[kc * 128:(kc + 1) * 128, :]), 'wqkv', writes=[wld])
        oprw(p, 'vector', lambda v, kc=kc, half=half: v.tensor_scalar(
            wTsC[kc].ap[:, kc * 3 * C:(kc + 1) * 3 * C],
            wld.ap[:, half * 3 * C:(half + 1) * 3 * C],
            s1col.ap[:, kc:kc + 1], None, ALU.mult),
            reads=[wld, s1col], writes=[wTsC[kc]])

    xb = [SB('xb0', [128, C], F32, stack=esA), SB('xb1', [128, C], F32, stack=esA), SB('xb2', [128, C], F32, stack=esA)]
    xn1P = [SB('xn1a', [128, C], BF16, stack=esA), SB('xn1b', [128, C], BF16, stack=esA), SB('xn1c', [128, C], BF16, stack=esA)]
    xTP = [SB('xTa', [128, C], BF16, stack=esA), SB('xTb', [128, C], BF16, stack=esA)]
    ksq1 = SB('ksq1', [128, C], F32, stack=esA)
    ksqP = [ksq1, ksq1]
    kphiP = [SB('kphia', [128, C], BF16, stack=esA), SB('kphib', [128, C], BF16, stack=esA)]
    vextP = [SB('vexta', [128, NH * 65], BF16, stack=esA), SB('vextb', [128, NH * 65], BF16, stack=esA)]
    qphiP = [SB('qphia', [128, C], BF16, stack=esA), SB('qphib', [128, C], BF16, stack=esA)]

    for _ve in vextP:
        oprw(p, 'vector', lambda v, _ve=_ve: v.memset(_ve.ap[:], 1.0), writes=[_ve])
    oprw(p, 'vector', lambda v: v.memset(kv_b.ap[:], 0.0), writes=[kv_b])

    wjobs = []
    for r in range(IFF // 128):
        wjobs.append((d_wg, d_wgq, r, 0, C))
    for r in range(IFF // 128):
        wjobs.append((d_wu, d_wuq, r, 0, C))
    for r in range(C // 128):
        wjobs.append((d_wd, d_wdq, r, 0, WJW))
        wjobs.append((d_wd, d_wdq, r, WJW, WJW))
    wj_i = 0

    def emit_wjob_load(i):
        src, dst, r, co, W = wjobs[i]
        buf = wqld[i % 2]
        dmarw(p, 'gpsimd', lambda E, src=src, r=r, co=co, W=W, buf=buf: E.dma_start(
            buf.ap[:, 0:W], src[r * 128:(r + 1) * 128, co:co + W]),
            f'wstr{i % 2}', writes=[buf])

    def emit_wjob_prerest(i, qs=None):
        src, dst, r, co, W = wjobs[i]
        buf = wqld[i % 2]
        qs = qs or (QSET1 if i % 2 else QSET0)
        emit_quant_pre(buf, 0, min(1024, W), qs)

    def emit_wjob_pre(i, qs=None):
        emit_wjob_load(i)
        emit_wjob_prerest(i, qs)

    def emit_wjob_main(i, qs=None):
        src, dst, r, co, W = wjobs[i]
        buf = wqld[i % 2]
        qs = qs or (QSET1 if i % 2 else QSET0)
        emit_quant_main(wqout, 0, min(1024, W), qs)
        for off in range(1024, W, 1024):
            w = min(1024, W - off)
            emit_quant_pre(buf, off, w, qs)
            emit_quant_main(wqout, off, w, qs)
        dmarw(p, 'gpsimd', lambda E, dst=dst, r=r, co=co, W=W: E.dma_start(
            dst[r * 128:(r + 1) * 128, co:co + W], wqout.ap[:, 0:W]), 'wstw',
            reads=[wqout])

    def emit_wjob(i, qs=None):
        emit_wjob_pre(i, qs)
        emit_wjob_main(i, qs)

    n_w_per_tile = 1

    def stageA1a(t):
        x = xb[t % 3]
        mv = mvP[t % 3]
        dmarw(p, 'sync', lambda E, t=t, x=x: E.dma_start(
            x.ap[:], d_x_all[t * 128:(t + 1) * 128, :]), f'xinA{t % 3}',
            writes=[x])
        oprw(p, 'vector', lambda v, x=x, mv=mv: v.bn_stats(
            mv.ap[:, 0:6], x.ap[:, 0:512]), reads=[x], writes=[mv])
        oprw(p, 'vector', lambda v, x=x, mv=mv: v.bn_stats(
            mv.ap[:, 6:12], x.ap[:, 512:1024]), reads=[x], writes=[mv])
        oprw(p, 'vector', lambda v, mv=mv: v.bn_aggr(
            mv.ap[:, 12:14], mv.ap[:, 0:12]), writes=[mv])
        oprw(p, 'vector', lambda v, mv=mv: v.tensor_scalar(
            mv.ap[:, 14:15], mv.ap[:, 13:14], 1e-5, None, ALU.add), writes=[mv])
        oprw(p, 'vector', lambda v, mv=mv: v.tensor_copy(
            mv.ap[:, 15:16], mv.ap[:, 13:14]), writes=[mv])
        oprw(p, 'scalar', lambda sc, mv=mv: sc.activation(
            mv.ap[:, 14:16], mv.ap[:, 14:16], AF.Ln), writes=[mv])
        oprw(p, 'scalar', lambda sc, mv=mv: sc.activation(
            mv.ap[:, 14:16], mv.ap[:, 14:16], AF.Exp, scale=-0.5), writes=[mv])

    def stageA1b(t):
        x = xb[t % 3]
        xn1 = xn1P[t % 3]
        mv = mvP[t % 3]
        is_own = (t % cpb) == 0
        o = t // cpb
        if is_own:
            oprw(p, 'vector', lambda v, o=o, mv=mv: v.tensor_copy(
                stats.ap[:, 2 * o:2 * o + 1], mv.ap[:, 12:13]),
                reads=[mv], writes=[stats])
            oprw(p, 'vector', lambda v, o=o, mv=mv: v.tensor_copy(
                stats.ap[:, 2 * o + 1:2 * o + 2], mv.ap[:, 14:15]),
                reads=[mv], writes=[stats])
        oprw(p, 'vector', lambda v, x=x, mv=mv, xn1=xn1: v.tensor_scalar(
            xn1.ap[:], x.ap[:], mv.ap[:, 12:13], mv.ap[:, 15:16],
            ALU.subtract, ALU.mult), reads=[x, mv], writes=[xn1])

    def stageA2(t):
        xn1 = xn1P[t % 3]
        xT = xTP[t % 2]
        ksq = ksqP[t % 2]
        kphi = kphiP[t % 2]
        vext = vextP[t % 2]
        qphi = qphiP[t % 2]
        is_own = (t % cpb) == 0
        o = t // cpb
        for kc in range(8):
            oprw(p, 'tensor', lambda T, kc=kc, xn1=xn1: T.transpose(
                TR.ap[:, kc * 128:(kc + 1) * 128],
                xn1.ap[:, kc * 128:(kc + 1) * 128], ident.ap[:]),
                reads=[xn1, ident], writes=[TR])
        oprw(p, 'scalar' if t < 40 else 'vector',
             lambda e_, xT=xT: e_.copy(xT.ap[:], TR.ap[:]) if hasattr(e_, 'copy')
             else e_.tensor_copy(xT.ap[:], TR.ap[:]),
             reads=[TR], writes=[xT])
        for kc in range(8):
            for hh in range(2):
                oprw(p, 'tensor', lambda T, kc=kc, hh=hh, xT=xT: T.matmul(
                    (PA if hh == 0 else PB).ap[:],
                    xT.ap[:, kc * 128:(kc + 1) * 128],
                    wTsC[kc].ap[:, kc * 3 * C + C + hh * 512: kc * 3 * C + C + (hh + 1) * 512],
                    start=(kc == 0), stop=(kc == 7)),
                    reads=[xT, wTsC[kc]], writes=[PA if hh == 0 else PB])
        for kc in range(8):
            for hh in range(2):
                oprw(p, 'tensor', lambda T, kc=kc, hh=hh, xT=xT: T.matmul(
                    (PC if hh == 0 else PD).ap[:],
                    xT.ap[:, kc * 128:(kc + 1) * 128],
                    wTsC[kc].ap[:, kc * 3 * C + 2 * C + hh * 512: kc * 3 * C + 2 * C + (hh + 1) * 512],
                    start=(kc == 0), stop=(kc == 7)),
                    reads=[xT, wTsC[kc]], writes=[PC if hh == 0 else PD])
        oprw(p, 'scalar', lambda sc, ksq=ksq: sc.activation(
            ksq.ap[:, 0:512], PA.ap[:], AF.Square), reads=[PA], writes=[ksq])
        oprw(p, 'scalar', lambda sc, ksq=ksq: sc.activation(
            ksq.ap[:, 512:1024], PB.ap[:], AF.Square), reads=[PB], writes=[ksq])
        oprw(p, 'scalar', lambda sc, ksq=ksq, kphi=kphi: sc.activation(
            kphi.ap[:], ksq.ap[:], AF.Exp, scale=-0.5), reads=[ksq], writes=[kphi])
        if is_own:
            for kc in range(8):
                for hh in range(2):
                    oprw(p, 'tensor', lambda T, kc=kc, hh=hh, xT=xT: T.matmul(
                        (PA if hh == 0 else PB).ap[:],
                        xT.ap[:, kc * 128:(kc + 1) * 128],
                        wTsC[kc].ap[:, kc * 3 * C + hh * 512: kc * 3 * C + (hh + 1) * 512],
                        start=(kc == 0), stop=(kc == 7)),
                        reads=[xT, wTsC[kc]], writes=[PA if hh == 0 else PB])
        oprw(p, 'scalar', lambda sc, vext=vext: sc.copy(
            vext.ap[:].rearrange("p (h e) -> p h e", e=65)[:, 0:8, 0:64],
            PC.ap[:].rearrange("p (h e) -> p h e", e=64)[:, 0:8, :]),
            reads=[PC], writes=[vext])
        oprw(p, 'scalar', lambda sc, vext=vext: sc.copy(
            vext.ap[:].rearrange("p (h e) -> p h e", e=65)[:, 8:16, 0:64],
            PD.ap[:].rearrange("p (h e) -> p h e", e=64)[:, 0:8, :]),
            reads=[PD], writes=[vext])
        if is_own:
            oprw(p, 'scalar', lambda sc, ksq=ksq: sc.activation(
                ksq.ap[:, 0:512], PA.ap[:], AF.Square), reads=[PA], writes=[ksq])
            oprw(p, 'scalar', lambda sc, ksq=ksq: sc.activation(
                ksq.ap[:, 512:1024], PB.ap[:], AF.Square), reads=[PB], writes=[ksq])
            oprw(p, 'scalar', lambda sc, ksq=ksq, qphi=qphi: sc.activation(
                qphi.ap[:], ksq.ap[:], AF.Exp, scale=-0.5),
                reads=[ksq], writes=[qphi])
            dmarw(p, 'scalar', lambda E, o=o, qphi=qphi: E.dma_start(
                d_qspill[o * 128:(o + 1) * 128, :], qphi.ap[:]), 'qsp',
                reads=[qphi])

    def stageA2kv(t):
        kphi = kphiP[t % 2]
        vext = vextP[t % 2]
        for j in range(8):
            bank = (KV1, KV2, KV3)[j // 3]
            slot = j % 3
            st = (t == 0 and slot == 0)
            sp = (t == T_batch - 1 and j in (2, 5, 7))
            oprw(p, 'tensor', lambda T, j=j, bank=bank, slot=slot, st=st, sp=sp, kphi=kphi, vext=vext: T.matmul(
                bank.ap[:, slot * 130:(slot + 1) * 130],
                kphi.ap[:, j * 128:(j + 1) * 128],
                vext.ap[:, j * 130:(j + 1) * 130],
                start=st, stop=sp),
                reads=[kphi, vext], writes=[bank])

    stageA1a(0)
    stageA1b(0)
    stageA1a(1)
    w_pre = 0
    w_load = 0
    for t in range(T_batch):
        if t + 2 < T_batch:
            stageA1a(t + 2)
        if t + 1 < T_batch:
            stageA1b(t + 1)
        if (t < 34 or t % 2 == 0) and w_load < min(len(wjobs), 38):
            emit_wjob_load(w_load)
            w_load += 1
            if w_pre < w_load - 1:
                emit_wjob_prerest(w_pre)
                w_pre += 1
        stageA2(t)
        if t > 0:
            stageA2kv(t - 1)
        if wj_i < w_pre - 1:
            emit_wjob_main(wj_i)
            wj_i += 1
    stageA2kv(T_batch - 1)
    while w_pre < w_load:
        emit_wjob_prerest(w_pre)
        w_pre += 1
    while wj_i < w_pre:
        emit_wjob_main(wj_i)
        wj_i += 1

    for j in range(8):
        bank = (KV1, KV2, KV3)[j // 3]
        slot = j % 3
        oprw(p, 'vector', lambda v, j=j, bank=bank, slot=slot: v.tensor_copy(
            kv_b.ap[0:64, j * 130:j * 130 + 65],
            bank.ap[0:64, slot * 130:slot * 130 + 65]),
            reads=[bank], writes=[kv_b])
        oprw(p, 'vector', lambda v, j=j, bank=bank, slot=slot: v.tensor_copy(
            kv_b.ap[64:128, j * 130 + 65:j * 130 + 130],
            bank.ap[64:128, slot * 130 + 65:slot * 130 + 130]),
            reads=[bank], writes=[kv_b])

    p.barrier()
    esA.close()

    projTgh = esB.enter_context(nc.sbuf_tensor('projTg', [128, 8 * C], BF16))
    projTgC = [Buf(projTgh, f'projTg{kc}') for kc in range(8)]
    wldB = SB('wldB', [128, HID], F32, stack=esB)
    for kc in range(8):
        dmarw(p, 'sync', lambda E, kc=kc: E.dma_start(
            wldB.ap[:, 0:C],
            d_projT[kc * 128:(kc + 1) * 128, :]), 'wB', writes=[wldB])
        oprw(p, 'vector', lambda v, kc=kc: v.tensor_tensor(
            projTgC[kc].ap[:, kc * C:(kc + 1) * C],
            wldB.ap[:, 0:C], gammar.ap[:], ALU.mult),
            reads=[wldB, gammar], writes=[projTgC[kc]])

    qTallh = esB.enter_context(nc.sbuf_tensor('qTall', [128, 8 * S_core], BF16))
    qTallC = [Buf(qTallh, f'qTall{j}') for j in range(8)]
    for j in range(8):
        dmarw(p, 'sync', lambda E, j=j: E.dma_start(
            qTallC[j].ap[:, j * S_core:(j + 1) * S_core],
            d_qspill[:, j * 128:(j + 1) * 128],
            transpose=True), f'qrd{j % 2}', writes=[qTallC[j]])
    den = SB('den', [128, NH], F32, stack=esB)
    zz = SB('zz', [128, NH], F32, stack=esB)
    outzP = [SB('outza', [128, C], BF16, stack=esB), SB('outzb', [128, C], BF16, stack=esB)]
    ozTP = [SB('ozTa', [128, C], BF16, stack=esB), SB('ozTb', [128, C], BF16, stack=esB)]
    xn2P = [SB('xn2a', [128, C], BF16, stack=esB), SB('xn2b', [128, C], BF16, stack=esB)]
    xnBP = [SB('xnBa', [128, C], F32, stack=esB), SB('xnBb', [128, C], F32, stack=esB)]
    oaP = [SB('oaa', [128, C], F32, stack=esB), SB('oab', [128, C], F32, stack=esB)]
    xbB = [SB('xbBa', [128, C], F32, stack=esB), SB('xbBb', [128, C], F32, stack=esB)]
    x1P = [SB('x1a', [128, C], F32, stack=esB), SB('x1b', [128, C], F32, stack=esB)]
    xqBP = [SB('xqBa', [128, C], BF16, stack=esB), SB('xqBb', [128, C], BF16, stack=esB)]

    OUTB = (KV1, KV2, KV3)

    def stageBa(t):
        pr = t % 2
        x = xbB[pr]
        dmarw(p, 'sync', lambda E, t=t, x=x: E.dma_start(
            x.ap[:], d_x_own[t * 128:(t + 1) * 128, :]), f'xin{t % 2}',
            writes=[x])

    def stageBb(t):
        pr = t % 2
        x = xbB[pr]
        outz = outzP[pr]
        ozT = ozTP[pr]
        xn = xnBP[pr]
        x1 = x1P[pr]
        oacc = oaP[pr]
        xn2 = xn2P[pr]
        mv = mvP[pr]
        oprw(p, 'vector', lambda v, x=x, xn=xn, t=t: v.tensor_scalar(
            xn.ap[:], x.ap[:], stats.ap[:, 2 * t:2 * t + 1],
            stats.ap[:, 2 * t + 1:2 * t + 2], ALU.subtract, ALU.mult),
            reads=[x, stats], writes=[xn])
        for j in range(8):
            bank = OUTB[j // 3]
            slot = j % 3
            oprw(p, 'tensor', lambda T, j=j, bank=bank, slot=slot, t=t: T.matmul(
                bank.ap[:, slot * 130:(slot + 1) * 130],
                qTallC[j].ap[:, j * S_core + t * 128:j * S_core + (t + 1) * 128],
                kv_b.ap[:, j * 130:(j + 1) * 130],
                start=True, stop=True),
                reads=[qTallC[j], kv_b], writes=[bank])
        oprw(p, 'vector', lambda v: v.tensor_copy(
            den.ap[:, 0:6],
            OUTB[0].ap[:, 0:390].rearrange("p (h e) -> p h e", e=65)[:, :, 64]),
            reads=[OUTB[0]], writes=[den])
        oprw(p, 'vector', lambda v: v.tensor_copy(
            den.ap[:, 6:12],
            OUTB[1].ap[:, 0:390].rearrange("p (h e) -> p h e", e=65)[:, :, 64]),
            reads=[OUTB[1]], writes=[den])
        oprw(p, 'vector', lambda v: v.tensor_copy(
            den.ap[:, 12:16],
            OUTB[2].ap[:, 0:260].rearrange("p (h e) -> p h e", e=65)[:, :, 64]),
            reads=[OUTB[2]], writes=[den])
        oprw(p, 'vector', lambda v: v.tensor_scalar(
            den.ap[:], den.ap[:], 1e-6, None, ALU.add), writes=[den])
        oprw(p, 'vector', lambda v: v.reciprocal(zz.ap[:], den.ap[:]),
             reads=[den], writes=[zz])
        for bi, (b0, h0, nh) in enumerate(((OUTB[0], 0, 6), (OUTB[1], 6, 6),
                                           (OUTB[2], 12, 4))):
            oprw(p, 'vector', lambda v, b0=b0, h0=h0, nh=nh, outz=outz: v.tensor_tensor(
                outz.ap[:, h0 * HD:(h0 + nh) * HD].rearrange(
                    "p (h e) -> p h e", e=HD),
                b0.ap[:, 0:nh * 65].rearrange("p (h e) -> p h e", e=65)[:, :, 0:HD],
                zz.ap[:, h0:h0 + nh].unsqueeze(2).broadcast_to([128, nh, HD]),
                ALU.mult), reads=[b0, zz], writes=[outz])
        for kc in range(8):
            oprw(p, 'tensor', lambda T, kc=kc, outz=outz: T.transpose(
                TR.ap[:, kc * 128:(kc + 1) * 128],
                outz.ap[:, kc * 128:(kc + 1) * 128], ident.ap[:]),
                reads=[outz, ident], writes=[TR])
        oprw(p, 'scalar', lambda sc, ozT=ozT: sc.copy(ozT.ap[:], TR.ap[:]),
             reads=[TR], writes=[ozT])
        prA, prB = (PA, PB) if t % 2 == 0 else (PC, PD)
        for hh in range(2):
            pbk = prA if hh == 0 else prB
            for kc in range(8):
                oprw(p, 'tensor', lambda T, kc=kc, hh=hh, pbk=pbk, ozT=ozT: T.matmul(
                    pbk.ap[:], ozT.ap[:, kc * 128:(kc + 1) * 128],
                    projTgC[kc].ap[:, kc * C + hh * 512: kc * C + (hh + 1) * 512],
                    start=(kc == 0), stop=False),
                    reads=[ozT, projTgC[kc]], writes=[pbk])
            oprw(p, 'tensor', lambda T, hh=hh, pbk=pbk: T.matmul(
                pbk.ap[:], ones1.ap[:],
                gpb.ap[:, hh * 512:(hh + 1) * 512], start=False, stop=True),
                reads=[ones1, gpb], writes=[pbk])
    def stageBb2(t):
        pr = t % 2
        x = xbB[pr]
        xn = xnBP[pr]
        x1 = x1P[pr]
        oacc = oaP[pr]
        xn2 = xn2P[pr]
        mv = mvP[pr]
        prA, prB = (PA, PB) if t % 2 == 0 else (PC, PD)
        for hh, pbk in ((0, prA), (1, prB)):
            oprw(p, 'vector', lambda v, hh=hh, pbk=pbk, x1=x1, xn=xn: v.tensor_tensor(
                x1.ap[:, hh * 512:(hh + 1) * 512], pbk.ap[:],
                xn.ap[:, hh * 512:(hh + 1) * 512], ALU.add),
                reads=[pbk, xn], writes=[x1])
            oprw(p, 'vector', lambda v, hh=hh, oacc=oacc, x1=x1, x=x: v.tensor_tensor(
                oacc.ap[:, hh * 512:(hh + 1) * 512],
                x1.ap[:, hh * 512:(hh + 1) * 512],
                x.ap[:, hh * 512:(hh + 1) * 512], ALU.add),
                reads=[x1, x], writes=[oacc])
        dmarw(p, 'scalar', lambda E, t=t, oacc=oacc: E.dma_start(
            d_oaspill[t * 128:(t + 1) * 128, :], oacc.ap[:]), 'oas', reads=[oacc])
        oprw(p, 'vector', lambda v, mv=mv, x1=x1: v.bn_stats(
            mv.ap[:, 0:6], x1.ap[:, 0:512]), reads=[x1], writes=[mv])
        oprw(p, 'vector', lambda v, mv=mv, x1=x1: v.bn_stats(
            mv.ap[:, 6:12], x1.ap[:, 512:1024]), reads=[x1], writes=[mv])
        oprw(p, 'vector', lambda v, mv=mv: v.bn_aggr(
            mv.ap[:, 12:14], mv.ap[:, 0:12]), writes=[mv])
        oprw(p, 'vector', lambda v, mv=mv: v.tensor_scalar(
            mv.ap[:, 14:15], mv.ap[:, 12:13], mv.ap[:, 12:13], None,
            ALU.mult), writes=[mv])
        oprw(p, 'vector', lambda v, mv=mv: v.tensor_tensor(
            mv.ap[:, 14:15], mv.ap[:, 14:15], mv.ap[:, 13:14], ALU.add),
            writes=[mv])
        oprw(p, 'scalar', lambda sc, mv=mv: sc.activation(
            mv.ap[:, 14:15], mv.ap[:, 14:15], AF.Ln, scale=1024.0), writes=[mv])
        oprw(p, 'scalar', lambda sc, mv=mv: sc.activation(
            mv.ap[:, 14:15], mv.ap[:, 14:15], AF.Exp, scale=-0.5), writes=[mv])
        oprw(p, 'vector', lambda v, mv=mv, x1=x1, xn2=xn2: v.tensor_scalar(
            xn2.ap[:], x1.ap[:], mv.ap[:, 14:15], 32.0, ALU.mult, ALU.mult),
            reads=[x1, mv], writes=[xn2])
        dmarw(p, 'scalar', lambda E, t=t, xn2=xn2: E.dma_start(
            d_xn2spill[t * 128:(t + 1) * 128, :], xn2.ap[:]), 'xn2s', reads=[xn2])
        if t > 0:
            xqp = xqBP[(t - 1) % 2]
            emit_quant_main(xqp, 0, C, QSET0)
            dmarw(p, 'scalar', lambda E, t=t, xqp=xqp: E.dma_start(
                d_xqspill[(t - 1) * 128:t * 128, :], xqp.ap[:]), 'xqs',
                reads=[xqp])
        emit_quant_pre(xn, 0, C, QSET0)

    stageBa(0)
    for t in range(T_own):
        stageBb(t)
        if t > 0:
            stageBb2(t - 1)
        if t + 1 < T_own:
            stageBa(t + 1)
        if t % 3 == 2 and wj_i < min(len(wjobs), 40):
            emit_wjob(wj_i, QSET1)
            wj_i += 1
    stageBb2(T_own - 1)
    xqp = xqBP[(T_own - 1) % 2]
    emit_quant_main(xqp, 0, C, QSET0)
    dmarw(p, 'scalar', lambda E, xqp=xqp: E.dma_start(
        d_xqspill[(T_own - 1) * 128:T_own * 128, :], xqp.ap[:]), 'xqs',
        reads=[xqp])

    p.barrier()
    esB.close()

    c1Tsh = esB2.enter_context(nc.sbuf_tensor('c1Ts', [128, 8 * HID], BF16))
    c1TsC = [Buf(c1Tsh, f'c1Ts{kc}') for kc in range(8)]
    c2Tgh = esB2.enter_context(nc.sbuf_tensor('c2Tg', [128, 16 * C], BF16))
    c2TgC = [Buf(c2Tgh, f'c2Tg{kc}') for kc in range(16)]
    wldB2 = SB('wldB2', [128, C], F32, stack=esB2)
    for kc in range(8):
        for hf in range(2):
            dmarw(p, 'sync', lambda E, kc=kc, hf=hf: E.dma_start(
                wldB2.ap[:],
                d_c1T[kc * 128:(kc + 1) * 128, hf * C:(hf + 1) * C]), 'wB',
                writes=[wldB2])
            oprw(p, 'vector', lambda v, kc=kc, hf=hf: v.tensor_scalar(
                c1TsC[kc].ap[:, kc * HID + hf * C:kc * HID + (hf + 1) * C],
                wldB2.ap[:],
                s2col.ap[:, kc:kc + 1], None, ALU.mult),
                reads=[wldB2, s2col], writes=[c1TsC[kc]])
    for kc in range(16):
        dmarw(p, 'sync', lambda E, kc=kc: E.dma_start(
            wldB2.ap[:, 0:C],
            d_c2T[kc * 128:(kc + 1) * 128, :]), 'wB', writes=[wldB2])
        oprw(p, 'vector', lambda v, kc=kc: v.tensor_tensor(
            c2TgC[kc].ap[:, kc * C:(kc + 1) * C],
            wldB2.ap[:, 0:C], gammar.ap[:], ALU.mult),
            reads=[wldB2, gammar], writes=[c2TgC[kc]])

    x2Tallh = esB2.enter_context(nc.sbuf_tensor('x2Tall', [128, 8 * S_core], BF16))
    x2TallC = [Buf(x2Tallh, f'x2Tall{j}') for j in range(8)]
    for j in range(8):
        dmarw(p, 'sync', lambda E, j=j: E.dma_start(
            x2TallC[j].ap[:, j * S_core:(j + 1) * S_core],
            d_xn2spill[:, j * 128:(j + 1) * 128],
            transpose=True), f'xn2r{j % 2}', writes=[x2TallC[j]])
    hddP = [SB('hdda', [128, HID], BF16, stack=esB2), SB('hddb', [128, HID], BF16, stack=esB2)]
    hdTP = [SB('hdTa', [128, HID], BF16, stack=esB2), SB('hdTb', [128, HID], BF16, stack=esB2)]
    oaiP = [SB('oaia', [128, C], F32, stack=esB2), SB('oaib', [128, C], F32, stack=esB2)]

    def stageB2oai(t):
        oai = oaiP[t % 2]
        dmarw(p, 'sync', lambda E, t=t, oai=oai: E.dma_start(
            oai.ap[:], d_oaspill[t * 128:(t + 1) * 128, :]), 'oar', writes=[oai])

    def stageB2m(t):
        pr = t % 2
        hdd = hddP[pr]
        C1B = (PA, PB, PC, PD)
        for hh in range(4):
            pbk = C1B[hh]
            for kc in range(8):
                oprw(p, 'tensor', lambda T, kc=kc, hh=hh, pbk=pbk, t=t: T.matmul(
                    pbk.ap[:], x2TallC[kc].ap[:, kc * S_core + t * 128:kc * S_core + (t + 1) * 128],
                    c1TsC[kc].ap[:, kc * HID + hh * 512: kc * HID + (hh + 1) * 512],
                    start=(kc == 0), stop=False),
                    reads=[x2TallC[kc], c1TsC[kc]], writes=[pbk])
            oprw(p, 'tensor', lambda T, hh=hh, pbk=pbk: T.matmul(
                pbk.ap[:], ones1.ap[:],
                c1brow.ap[:, hh * 512:(hh + 1) * 512], start=False, stop=True),
                reads=[ones1, c1brow], writes=[pbk])
        for hh in range(4):
            pbk = C1B[hh]
            oprw(p, 'scalar', lambda sc, hh=hh, pbk=pbk, hdd=hdd: sc.activation(
                hdd.ap[:, hh * 512:(hh + 1) * 512], pbk.ap[:], AF.Gelu),
                reads=[pbk], writes=[hdd])

    def stageB2c(t):
        pr = t % 2
        hdd = hddP[pr]
        hdT = hdTP[pr]
        oai = oaiP[pr]
        yb = oai
        for rnd in range(2):
            for kc in range(rnd * 8, rnd * 8 + 8):
                oprw(p, 'tensor', lambda T, kc=kc, rnd=rnd, hdd=hdd: T.transpose(
                    TR.ap[:, (kc - rnd * 8) * 128:(kc - rnd * 8 + 1) * 128],
                    hdd.ap[:, kc * 128:(kc + 1) * 128], ident.ap[:]),
                    reads=[hdd, ident], writes=[TR])
            oprw(p, 'scalar', lambda sc, rnd=rnd, hdT=hdT: sc.copy(
                hdT.ap[:, rnd * 1024:(rnd + 1) * 1024], TR.ap[:]),
                reads=[TR], writes=[hdT])
        for hh in range(2):
            pbk = KV1 if hh == 0 else KV2
            for kc in range(16):
                oprw(p, 'tensor', lambda T, kc=kc, hh=hh, pbk=pbk, hdT=hdT: T.matmul(
                    pbk.ap[:], hdT.ap[:, kc * 128:(kc + 1) * 128],
                    c2TgC[kc].ap[:, kc * C + hh * 512: kc * C + (hh + 1) * 512],
                    start=(kc == 0), stop=False),
                    reads=[hdT, c2TgC[kc]], writes=[pbk])
            oprw(p, 'tensor', lambda T, hh=hh, pbk=pbk: T.matmul(
                pbk.ap[:], ones1.ap[:],
                gc2b.ap[:, hh * 512:(hh + 1) * 512], start=False, stop=True),
                reads=[ones1, gc2b], writes=[pbk])
        for hh, pbk in ((0, KV1), (1, KV2)):
            oprw(p, 'vector', lambda v, hh=hh, pbk=pbk, yb=yb, oai=oai: v.tensor_tensor(
                yb.ap[:, hh * 512:(hh + 1) * 512], pbk.ap[:],
                oai.ap[:, hh * 512:(hh + 1) * 512], ALU.add),
                reads=[pbk, oai], writes=[yb])
        dmarw(p, 'sync', lambda E, t=t, yb=yb: E.dma_start(
            d_y1spill[t * 128:(t + 1) * 128, :], yb.ap[:]), 'y1s', reads=[yb])

    stageB2oai(0)
    for t in range(T_own):
        stageB2m(t)
        if wj_i < len(wjobs):
            emit_wjob(wj_i)
            wj_i += 1
        if t > 0:
            stageB2c(t - 1)
        if t + 1 < T_own:
            stageB2oai(t + 1)
    stageB2c(T_own - 1)

    while wj_i < len(wjobs):
        emit_wjob(wj_i)
        wj_i += 1

    p.barrier()
    esB2.close()
    esW.close()


    wgTh = esF1.enter_context(nc.sbuf_tensor('wgT', [128, 8 * IFF], BF16))
    wgTC = [Buf(wgTh, f'wgT{kc}') for kc in range(8)]
    wuTh = esF1.enter_context(nc.sbuf_tensor('wuT', [128, 8 * IFF], BF16))
    wuTC = [Buf(wuTh, f'wuT{kc}') for kc in range(8)]
    for kc in range(8):
        dmarw(p, 'sync', lambda E, kc=kc: E.dma_start(
            wgTC[kc].ap[:, kc * IFF:(kc + 1) * IFF],
            d_wgq[:, kc * 128:(kc + 1) * 128], transpose=True), f'wgt{kc % 2}',
            writes=[wgTC[kc]])
        dmarw(p, 'sync', lambda E, kc=kc: E.dma_start(
            wuTC[kc].ap[:, kc * IFF:(kc + 1) * IFF],
            d_wuq[:, kc * 128:(kc + 1) * 128], transpose=True), f'wut{kc % 2}',
            writes=[wuTC[kc]])

    xqTallh = esF1.enter_context(nc.sbuf_tensor('xqTall', [128, 8 * S_core], BF16))
    xqTallC = [Buf(xqTallh, f'xqTall{j}') for j in range(8)]
    for j in range(8):
        dmarw(p, 'sync', lambda E, j=j: E.dma_start(
            xqTallC[j].ap[:, j * S_core:(j + 1) * S_core],
            d_xqspill[:, j * 128:(j + 1) * 128],
            transpose=True), f'xqr{j % 2}', writes=[xqTallC[j]])
    hbufP = [SB('hbufa', [128, IFF], F32, stack=esF1), SB('hbufb', [128, IFF], F32, stack=esF1)]
    hsilP = [SB('hsila', [128, 512], F32, stack=esF1), SB('hsilb', [128, 512], F32, stack=esF1)]
    hq1 = SB('hq1', [128, IFF], BF16, stack=esF1)
    hqP = [hq1, hq1]

    def stageF1m(t):
        hbuf = hbufP[t % 2]
        for occ in range(6):
            w = 512 if occ < 5 else IFF - 5 * 512
            pg = PA if occ % 2 == 0 else PC
            pu = PB if occ % 2 == 0 else PD
            for kc in range(8):
                oprw(p, 'tensor', lambda T, kc=kc, occ=occ, w=w, pg=pg, t=t: T.matmul(
                    pg.ap[:, 0:w],
                    xqTallC[kc].ap[:, kc * S_core + t * 128:kc * S_core + (t + 1) * 128],
                    wgTC[kc].ap[:, kc * IFF + occ * 512: kc * IFF + occ * 512 + w],
                    start=(kc == 0), stop=(kc == 7)),
                    reads=[xqTallC[kc], wgTC[kc]], writes=[pg])
            for kc in range(8):
                oprw(p, 'tensor', lambda T, kc=kc, occ=occ, w=w, pu=pu, t=t: T.matmul(
                    pu.ap[:, 0:w],
                    xqTallC[kc].ap[:, kc * S_core + t * 128:kc * S_core + (t + 1) * 128],
                    wuTC[kc].ap[:, kc * IFF + occ * 512: kc * IFF + occ * 512 + w],
                    start=(kc == 0), stop=(kc == 7)),
                    reads=[xqTallC[kc], wuTC[kc]], writes=[pu])
            hsil = hsilP[occ % 2]
            oprw(p, 'scalar', lambda sc, occ=occ, w=w, pg=pg, hsil=hsil: sc.activation(
                hsil.ap[:, 0:w], pg.ap[:, 0:w], AF.Silu), reads=[pg], writes=[hsil])
            oprw(p, 'scalar', lambda sc, occ=occ, w=w, pu=pu, hbuf=hbuf: sc.copy(
                hbuf.ap[:, occ * 512:occ * 512 + w], pu.ap[:, 0:w]),
                reads=[pu], writes=[hbuf])
            oprw(p, 'gpsimd', lambda g, occ=occ, w=w, hsil=hsil, hbuf=hbuf: g.tensor_tensor(
                hbuf.ap[:, occ * 512:occ * 512 + w], hsil.ap[:, 0:w],
                hbuf.ap[:, occ * 512:occ * 512 + w], ALU.mult),
                reads=[hsil], writes=[hbuf])

    def stageF1q(t):
        hbuf = hbufP[t % 2]
        hq = hqP[t % 2]
        emit_quant_pre_core(hbuf, 0, 1024, QSET0)
        emit_quant_pre_pool(1024, QSET0, u_eng='vector')
        emit_quant_pre_core(hbuf, 1024, 1024, QSET1)
        emit_quant_pre_pool(1024, QSET1, u_eng='vector')
        emit_quant_main(hq, 0, 1024, QSET0)
        emit_quant_pre_core(hbuf, 2048, IFF - 2048, QSET0)
        emit_quant_pre_pool(IFF - 2048, QSET0, u_eng='vector')
        emit_quant_main(hq, 1024, 1024, QSET1)
        emit_quant_main(hq, 2048, IFF - 2048, QSET0)
        dmarw(p, 'scalar', lambda E, t=t, hq=hq: E.dma_start(
            d_hqspill[t * 128:(t + 1) * 128, :], hq.ap[:]), 'hqs', reads=[hq])

    for t in range(T_own):
        stageF1m(t)
        if t > 0:
            stageF1q(t - 1)
    stageF1q(T_own - 1)

    p.barrier()
    esF1.close()
    esQ.close()

    wdTh = esF2.enter_context(nc.sbuf_tensor('wdT', [128, 22 * C], BF16))
    wdTC = [Buf(wdTh, f'wdT{kc}') for kc in range(22)]
    for kc in range(22):
        dmarw(p, 'sync', lambda E, kc=kc: E.dma_start(
            wdTC[kc].ap[:, kc * C:(kc + 1) * C],
            d_wdq[:, kc * 128:(kc + 1) * 128], transpose=True), f'wdt{kc % 3}',
            writes=[wdTC[kc]])
    hqTallh = esF2.enter_context(nc.sbuf_tensor('hqTall', [128, 22 * S_core], BF16))
    hqTallC = [Buf(hqTallh, f'hqTall{j}') for j in range(22)]
    for j in range(22):
        dmarw(p, 'sync', lambda E, j=j: E.dma_start(
            hqTallC[j].ap[:, j * S_core:(j + 1) * S_core],
            d_hqspill[:, j * 128:(j + 1) * 128],
            transpose=True), f'hqr{j % 3}', writes=[hqTallC[j]])
    fbufP = [SB('fbufa', [128, C], F32, stack=esF2), SB('fbufb', [128, C], F32, stack=esF2)]
    xbF = [SB('xbFa', [128, C], F32, stack=esF2), SB('xbFb', [128, C], F32, stack=esF2)]

    def stageF2a(t):
        xfb = xbF[t % 2]
        dmarw(p, 'sync', lambda E, t=t, xfb=xfb: E.dma_start(
            xfb.ap[:], d_y1spill[t * 128:(t + 1) * 128, :]), f'xin{t % 2}',
            writes=[xfb])

    def stageF2b(t):
        fbuf = fbufP[t % 2]
        xfb = xbF[t % 2]
        pa, pb = (PA, PB) if t % 2 == 0 else (PC, PD)
        for kc in range(22):
            for hh in range(2):
                oprw(p, 'tensor', lambda T, kc=kc, hh=hh, t=t, pa=pa, pb=pb: T.matmul(
                    (pa if hh == 0 else pb).ap[:],
                    hqTallC[kc].ap[:, kc * S_core + t * 128:kc * S_core + (t + 1) * 128],
                    wdTC[kc].ap[:, kc * C + hh * 512: kc * C + (hh + 1) * 512],
                    start=(kc == 0), stop=(kc == 21)),
                    reads=[hqTallC[kc], wdTC[kc]], writes=[pa if hh == 0 else pb])
        oprw(p, 'vector', lambda v, xfb=xfb, fbuf=fbuf, pa=pa: v.tensor_tensor(
            fbuf.ap[:, 0:512], pa.ap[:], xfb.ap[:, 0:512], ALU.add),
            reads=[pa, xfb], writes=[fbuf])
        oprw(p, 'vector', lambda v, xfb=xfb, fbuf=fbuf, pb=pb: v.tensor_tensor(
            fbuf.ap[:, 512:1024], pb.ap[:], xfb.ap[:, 512:1024], ALU.add),
            reads=[pb, xfb], writes=[fbuf])
        dmarw(p, 'sync', lambda E, t=t, fbuf=fbuf: E.dma_start(
            d_y[t * 128:(t + 1) * 128, :], fbuf.ap[:]), 'yout', reads=[fbuf])

    stageF2a(0)
    for t in range(T_own):
        if t + 1 < T_own:
            stageF2a(t + 1)
        stageF2b(t)

    p.barrier()
    esF2.close()

    p.wait('sync', [('D', 'yout', p.dma_sems['yout'][1])])

    esB.close()
    with nc.Block() as block:
        p.finalize(block)
    es.close()
    return nc


def interleave_x(xb_full, q, S_core, cpb):
    S, C_ = xb_full.shape
    nb = S // 128
    T_own = S_core // 128
    blocks = xb_full.reshape(nb, 128, C_)
    own_ids = list(range(q * T_own, (q + 1) * T_own))
    own_set = set(own_ids)
    other_ids = [i for i in range(nb) if i not in own_set]
    order = []
    oi = ri = 0
    for t in range(nb):
        if t % cpb == 0:
            order.append(own_ids[oi])
            oi += 1
        else:
            order.append(other_ids[ri])
            ri += 1
    return np.ascontiguousarray(blocks[order].reshape(S, C_))


_NC_CACHE = {}


def _get_nc(S_batch, S_core):
    key = (S_batch, S_core)
    if key not in _NC_CACHE:
        _NC_CACHE[key] = build_nc(S_batch, S_core, S_batch // S_core)
    return _NC_CACHE[key]


def kernel(L0_attn_x, L0_attn_scale1, L0_attn_scale2, L0_attn_gamma,
           L0_attn_qkv_w, L0_attn_proj_w, L0_attn_proj_b, L0_attn_conv1_w,
           L0_attn_conv1_b, L0_attn_conv2_w, L0_attn_conv2_b, L0_ffn_w_gate,
           L0_ffn_w_up, L0_ffn_w_down):
    x = np.ascontiguousarray(np.asarray(L0_attn_x, np.float32))
    B, S, _ = x.shape
    n_cores = 8
    cpb = n_cores // B
    S_core = S // cpb
    nc = _get_nc(S, S_core)

    row = lambda a: np.ascontiguousarray(np.asarray(a, np.float32)).reshape(1, -1)
    wT = lambda w: np.ascontiguousarray(np.asarray(w, np.float32).T)
    nat = lambda w: np.ascontiguousarray(np.asarray(w, np.float32))
    common = {
        'scale1': row(L0_attn_scale1), 'scale2': row(L0_attn_scale2),
        'gamma': row(L0_attn_gamma), 'proj_b': row(L0_attn_proj_b),
        'conv1_b': row(L0_attn_conv1_b), 'conv2_b': row(L0_attn_conv2_b),
        'qkv_wT': wT(L0_attn_qkv_w), 'proj_wT': wT(L0_attn_proj_w),
        'conv1_wT': wT(L0_attn_conv1_w), 'conv2_wT': wT(L0_attn_conv2_w),
        'w_gate': nat(L0_ffn_w_gate), 'w_up': nat(L0_ffn_w_up),
        'w_down': nat(L0_ffn_w_down),
        'ident': np.eye(128, dtype=BF),
        'ones1': np.ones((1, 128), dtype=BF),
    }
    in_maps = []
    for c in range(n_cores):
        b = c // cpb
        q = c % cpb
        xb_full = x[b]
        own = np.ascontiguousarray(xb_full[q * S_core:(q + 1) * S_core])
        m = dict(common)
        m['x_all'] = interleave_x(xb_full, q, S_core, cpb)
        m['x_own'] = own
        in_maps.append(m)

    res = run_bass_kernel_spmd(nc, in_maps, core_ids=list(range(n_cores)))
    shards = [res.results[c]['y'] for c in range(n_cores)]
    out = np.empty((B, S, C), np.float32)
    for c in range(n_cores):
        b = c // cpb
        q = c % cpb
        out[b, q * S_core:(q + 1) * S_core] = shards[c]
    return out


# version 57
# speedup vs baseline: 1.0073x; 1.0073x over previous
import numpy as np
import ml_dtypes
from contextlib import ExitStack

import concourse.bass as bass
import concourse.mybir as mybir
from concourse.bass_utils import run_bass_kernel_spmd

F32 = mybir.dt.float32
BF16 = mybir.dt.bfloat16
AF = mybir.ActivationFunctionType
ALU = mybir.AluOpType
AX = mybir.AxisListType
BF = ml_dtypes.bfloat16

C = 1024
HID = 2048
IFF = 2816
NH = 16
HD = 64
QB = 16
C6 = 0.16666667163372039794921875
LN32 = 3.4657359027997265

ENGINES = ('sync', 'scalar', 'vector', 'tensor', 'gpsimd')


class Prog:

    def __init__(self, nc, es):
        self.nc = nc
        self.items = {e: [] for e in ENGINES}
        self.ecounts = {e: 0 for e in ENGINES}
        self.progsem = {e: es.enter_context(nc.semaphore(name=f'pg_{e}'))
                        for e in ENGINES}
        self.dma_sems = {}
        self.dma_chain = {}
        self.es = es

    def _dep_list(self, deps):
        out = []
        for d in deps:
            if d is None:
                continue
            if isinstance(d, (list, tuple)) and d and isinstance(d[0], (list, tuple)):
                out.extend(x for x in d if x is not None)
            else:
                out.append(d)
        return out

    def op(self, eng, emit, deps=()):
        idx = self.ecounts[eng]
        self.ecounts[eng] += 1
        self.items[eng].append({'emit': emit, 'deps': self._dep_list(deps),
                                'dma': None})
        return ('E', eng, idx)

    def dma(self, eng, emit, sem, deps=()):
        if sem not in self.dma_sems:
            self.dma_sems[sem] = [self.es.enter_context(
                self.nc.semaphore(name=f'dm_{sem}')), 0]
        self.dma_sems[sem][1] += 1
        k = self.dma_sems[sem][1]
        dl = self._dep_list(deps)
        prev = self.dma_chain.get(sem)
        if prev is not None:
            dl.append(prev)
        tok = ('D', sem, k)
        self.dma_chain[sem] = tok
        self.items[eng].append({'emit': emit, 'deps': dl, 'dma': (sem, k)})
        return tok

    def wait(self, eng, deps):
        self.items[eng].append({'emit': None, 'deps': self._dep_list(deps),
                                'dma': None})
        return None

    def barrier(self, engines=ENGINES):
        toks = []
        for e in engines:
            n = self.ecounts[e]
            if n:
                toks.append(('E', e, n - 1))
        for s, (h, cnt) in self.dma_sems.items():
            if cnt:
                toks.append(('D', s, cnt))
        for e in engines:
            self.wait(e, toks)
        return toks

    def finalize(self, block):
        referenced = set()
        for eng in ENGINES:
            for it in self.items[eng]:
                for d in it['deps']:
                    if d[0] == 'E' and d[1] == 'tensor' and eng != 'tensor':
                        referenced.add(d[2])
        tidx_sorted = sorted(referenced)
        import bisect
        tinc_of = {}
        for n_, i_ in enumerate(tidx_sorted):
            tinc_of[i_] = n_ + 1
        self._tensor_ref = referenced
        last_t = self.ecounts['tensor'] - 1
        total_tincs = len(tidx_sorted) + (0 if last_t in referenced else 1)

        def tensor_wait_count(i2):
            j = bisect.bisect_left(tidx_sorted, i2)
            if j >= len(tidx_sorted):
                return total_tincs
            return tinc_of[tidx_sorted[j]]

        for eng in ENGINES:
            items = self.items[eng]
            if not items:
                continue
            section = getattr(block, eng)

            def make_body(items=items, eng=eng):
                def body(E):
                    water = {}
                    tcount = 0
                    tseen = -1
                    for it in items:
                        for d in it['deps']:
                            kind = d[0]
                            if kind == 'E':
                                _, e2, i2 = d
                                if e2 == 'tensor':
                                    if eng == 'tensor':
                                        continue
                                    need = tensor_wait_count(i2)
                                else:
                                    need = i2 + 1
                                key = ('E', e2)
                                if water.get(key, 0) < need:
                                    E.wait_ge(self.progsem[e2], need)
                                    water[key] = need
                            else:
                                _, sname, k = d
                                need = 16 * k
                                key = ('D', sname)
                                if water.get(key, 0) < need:
                                    E.wait_ge(self.dma_sems[sname][0], need)
                                    water[key] = need
                        if it['emit'] is None:
                            continue
                        inst = it['emit'](E)
                        if it['dma'] is not None:
                            inst.then_inc(self.dma_sems[it['dma'][0]][0], 16)
                        elif eng == 'tensor':
                            tseen += 1
                            if tseen in self._tensor_ref or tseen == self.ecounts['tensor'] - 1:
                                inst.then_inc(self.progsem[eng], 1)
                                tcount += 1
                        else:
                            inst.then_inc(self.progsem[eng], 1)
                return body
            section(make_body())


class Buf:
    def __init__(self, ap, name=''):
        self.ap = ap
        self.name = name
        self.w = None
        self.readers = []

    def rdep(self):
        return [self.w] if self.w is not None else []

    def wdep(self):
        d = list(self.readers)
        if self.w is not None:
            d.append(self.w)
        return d

    def note_read(self, tok):
        if tok is not None:
            self.readers.append(tok)

    def note_write(self, tok):
        self.w = tok
        self.readers = []


def oprw(p, eng, emit, reads=(), writes=(), extra=()):
    deps = list(extra)
    for b in reads:
        deps += b.rdep()
    for b in writes:
        deps += b.wdep()
    tok = p.op(eng, emit, deps)
    for b in reads:
        b.note_read(tok)
    for b in writes:
        b.note_write(tok)
    return tok


def dmarw(p, eng, emit, sem, reads=(), writes=(), extra=()):
    deps = list(extra)
    for b in reads:
        deps += b.rdep()
    for b in writes:
        deps += b.wdep()
    tok = p.dma(eng, emit, sem, deps)
    for b in reads:
        b.note_read(tok)
    for b in writes:
        b.note_write(tok)
    return tok



def build_nc(S_batch, S_core, n_cores_per_batch):
    nc = bass.Bass('TRN2', target_bir_lowering=False)
    T_batch = S_batch // 128
    T_own = S_core // 128
    cpb = n_cores_per_batch

    d_x_all = nc.dram_tensor('x_all', [S_batch, C], F32, kind='ExternalInput').ap()
    d_x_own = nc.dram_tensor('x_own', [S_core, C], F32, kind='ExternalInput').ap()
    d_scale1 = nc.dram_tensor('scale1', [1, C], F32, kind='ExternalInput').ap()
    d_scale2 = nc.dram_tensor('scale2', [1, C], F32, kind='ExternalInput').ap()
    d_gamma = nc.dram_tensor('gamma', [1, C], F32, kind='ExternalInput').ap()
    d_pb = nc.dram_tensor('proj_b', [1, C], F32, kind='ExternalInput').ap()
    d_c1b = nc.dram_tensor('conv1_b', [1, HID], F32, kind='ExternalInput').ap()
    d_c2b = nc.dram_tensor('conv2_b', [1, C], F32, kind='ExternalInput').ap()
    d_qkvT = nc.dram_tensor('qkv_wT', [C, 3 * C], F32, kind='ExternalInput').ap()
    d_projT = nc.dram_tensor('proj_wT', [C, C], F32, kind='ExternalInput').ap()
    d_c1T = nc.dram_tensor('conv1_wT', [C, HID], F32, kind='ExternalInput').ap()
    d_c2T = nc.dram_tensor('conv2_wT', [HID, C], F32, kind='ExternalInput').ap()
    d_wg = nc.dram_tensor('w_gate', [IFF, C], F32, kind='ExternalInput').ap()
    d_wu = nc.dram_tensor('w_up', [IFF, C], F32, kind='ExternalInput').ap()
    d_wd = nc.dram_tensor('w_down', [C, IFF], F32, kind='ExternalInput').ap()
    d_ident = nc.dram_tensor('ident', [128, 128], BF16, kind='ExternalInput').ap()
    d_ones1 = nc.dram_tensor('ones1', [1, 128], BF16, kind='ExternalInput').ap()
    d_y = nc.dram_tensor('y', [S_core, C], F32, kind='ExternalOutput').ap()
    d_qspill = nc.dram_tensor('qspill', [S_core, C], BF16, kind='Internal').ap()
    d_hqspill = nc.dram_tensor('hqspill', [S_core, IFF], BF16, kind='Internal').ap()
    d_xqspill = nc.dram_tensor('xqspill', [S_core, C], BF16, kind='Internal').ap()
    d_y1spill = nc.dram_tensor('y1spill', [S_core, C], F32, kind='Internal').ap()
    d_xn2spill = nc.dram_tensor('xn2spill', [S_core, C], BF16, kind='Internal').ap()
    d_oaspill = nc.dram_tensor('oaspill', [S_core, C], F32, kind='Internal').ap()
    d_wgq = nc.dram_tensor('wgq', [IFF, C], BF16, kind='Internal').ap()
    d_wuq = nc.dram_tensor('wuq', [IFF, C], BF16, kind='Internal').ap()
    d_wdq = nc.dram_tensor('wdq', [C, IFF], BF16, kind='Internal').ap()

    es = ExitStack()
    esA = ExitStack()
    esQ = ExitStack()
    esW = ExitStack()
    esF1 = ExitStack()
    esF2 = ExitStack()
    esB = ExitStack()
    esB2 = ExitStack()

    def SB(name, shape, dt, stack=None):
        h = (stack or es).enter_context(nc.sbuf_tensor(name, shape, dt))
        return Buf(h, name)

    def PSUM(name, shape, dt):
        h = es.enter_context(nc.psum_tensor(name, shape, dt))
        return Buf(h, name)

    p = Prog(nc, es)

    ident = SB('identsb', [128, 128], BF16)
    ones1 = SB('ones1sb', [1, 128], BF16)
    s1col = SB('s1col', [128, 8], F32)
    s2col = SB('s2col', [128, 8], F32)
    gammar = SB('gammar', [128, C], F32)
    gpb = SB('gpb', [1, C], BF16)
    gc2b = SB('gc2b', [1, C], BF16)
    stats = SB('stats', [128, 2 * T_own], F32)


    PA = PSUM('PA', [128, 512], F32)
    PB = PSUM('PB', [128, 512], F32)
    PC = PSUM('PC', [128, 512], F32)
    PD = PSUM('PD', [128, 512], F32)
    KV1 = PSUM('KV1', [128, 512], F32)
    KV2 = PSUM('KV2', [128, 512], F32)
    KV3 = PSUM('KV3', [128, 512], F32)
    TR = PSUM('TR', [128, 1024], BF16)

    mvP = [SB('mva', [128, 16], F32), SB('mvb', [128, 16], F32), SB('mvc', [128, 16], F32)]

    q_sgn = SB('q_sgn', [128, 1024], F32, stack=esQ)
    q_ss = SB('q_ss', [128, 1024], F32, stack=esQ)
    q_lt = SB('q_lt', [128, 1024], BF16, stack=esQ)
    q_m1 = SB('q_m1', [128, 1024], BF16, stack=esQ)
    q_m2 = SB('q_m2', [128, 1024], BF16, stack=esQ)
    q_m3 = SB('q_m3', [128, 1024], BF16, stack=esQ)
    q_g2 = SB('q_g2', [128, 1024], BF16, stack=esQ)
    q_a = SB('q_a', [128, 1024], F32, stack=esQ)
    q2_a = SB('q2_a', [128, 1024], F32, stack=esQ)
    q_amax = SB('q_amax', [128, 64], F32, stack=esQ)
    q_scale = SB('q_scale', [128, 64], F32, stack=esQ)
    q_hs = SB('q_hs', [128, 64], F32, stack=esQ)
    q_inv = SB('q_inv', [128, 64], F32, stack=esQ)
    q2_sgn = SB('q2_sgn', [128, 1024], F32, stack=esQ)
    q2_ss = SB('q2_ss', [128, 1024], F32, stack=esQ)
    q2_amax = SB('q2_amax', [128, 64], F32, stack=esQ)
    q2_scale = SB('q2_scale', [128, 64], F32, stack=esQ)
    q2_hs = SB('q2_hs', [128, 64], F32, stack=esQ)
    q2_inv = SB('q2_inv', [128, 64], F32, stack=esQ)
    WJW = 1408
    wqld = [SB('wqld1', [128, WJW], F32, stack=esW),
            SB('wqld2', [128, WJW], F32, stack=esW)]
    wqout = SB('wqout', [128, WJW], BF16, stack=esW)
    rowtmp = SB('rowtmp', [1, C], F32, stack=esA)
    rowtmp2 = SB('rowtmp2', [1, C], F32, stack=esA)

    QSET0 = (q_sgn, q_amax, q_scale, q_amax, q_hs, q_inv, q_ss, q_a)
    QSET1 = (q2_sgn, q2_amax, q2_scale, q2_amax, q2_hs, q2_inv, q2_ss, q2_a)

    def emit_quant_pre_core(src, soff, W, qs, tok_extra=()):
        sgn, amax, scale, tmp, hs, inv, ssb, qa = qs
        NB = W // QB
        sa = src.ap[:, soff:soff + W]
        oprw(p, 'scalar', lambda sc, sa=sa, W=W, qa=qa: sc.activation(
            qa.ap[:, 0:W], sa, AF.Abs), reads=[src], writes=[qa], extra=tok_extra)
        oprw(p, 'scalar', lambda sc, sa=sa, W=W, sgn=sgn: sc.activation(
            sgn.ap[:, 0:W], sa, AF.Sign), reads=[src], writes=[sgn], extra=tok_extra)
        oprw(p, 'vector', lambda v, sa=sa, NB=NB, amax=amax: v.tensor_reduce(
            amax.ap[:, 0:NB].unsqueeze(2),
            sa.rearrange("p (a b) -> p a b", b=QB), AX.X, ALU.max,
            apply_absolute_value=True), reads=[src], writes=[amax], extra=tok_extra)
        oprw(p, 'vector', lambda v, NB=NB, amax=amax, scale=scale: v.tensor_scalar(
            scale.ap[:, 0:NB], amax.ap[:, 0:NB], C6, None, ALU.mult),
            reads=[amax], writes=[scale])
        oprw(p, 'vector', lambda v, NB=NB, amax=amax, scale=scale, tmp=tmp: v.scalar_tensor_tensor(
            tmp.ap[:, 0:NB], scale.ap[:, 0:NB], -6.0, amax.ap[:, 0:NB],
            ALU.mult, ALU.add), reads=[scale, amax], writes=[tmp])
        oprw(p, 'vector', lambda v, NB=NB, scale=scale, tmp=tmp: v.scalar_tensor_tensor(
            scale.ap[:, 0:NB], tmp.ap[:, 0:NB], C6, scale.ap[:, 0:NB],
            ALU.mult, ALU.add), reads=[tmp], writes=[scale])
        oprw(p, 'vector', lambda v, NB=NB, scale=scale: v.tensor_scalar(
            scale.ap[:, 0:NB], scale.ap[:, 0:NB], 1e-12, None, ALU.max),
            writes=[scale])
        oprw(p, 'vector', lambda v, NB=NB, scale=scale, hs=hs: v.tensor_scalar(
            hs.ap[:, 0:NB], scale.ap[:, 0:NB], 0.5, None, ALU.mult),
            reads=[scale], writes=[hs])
        oprw(p, 'vector', lambda v, NB=NB, scale=scale, inv=inv: v.reciprocal(
            inv.ap[:, 0:NB], scale.ap[:, 0:NB]), reads=[scale], writes=[inv])

    def emit_quant_pre_pool(W, qs, u_eng='gpsimd'):
        sgn, amax, scale, tmp, hs, inv, ssb, qa = qs
        NB = W // QB
        oprw(p, 'gpsimd', lambda g, NB=NB, W=W, sgn=sgn, hs=hs, ssb=ssb: g.tensor_tensor(
            ssb.ap[:, 0:W].rearrange("p (a b) -> p a b", b=QB),
            sgn.ap[:, 0:W].rearrange("p (a b) -> p a b", b=QB),
            hs.ap[:, 0:NB].unsqueeze(2).broadcast_to([128, NB, QB]),
            ALU.mult), reads=[sgn, hs], writes=[ssb])
        oprw(p, u_eng, lambda g, NB=NB, W=W, inv=inv, qa=qa: g.tensor_tensor(
            qa.ap[:, 0:W].rearrange("p (a b) -> p a b", b=QB),
            qa.ap[:, 0:W].rearrange("p (a b) -> p a b", b=QB),
            inv.ap[:, 0:NB].unsqueeze(2).broadcast_to([128, NB, QB]),
            ALU.mult), reads=[inv], writes=[qa])

    def emit_quant_pre(src, soff, W, qs, tok_extra=()):
        emit_quant_pre_core(src, soff, W, qs, tok_extra)
        emit_quant_pre_pool(W, qs)

    def emit_quant_main(dst, doff, W, qs):
        sgn, amax, scale, tmp, hs, inv, ssb, q_a = qs
        NB = W // QB
        oprw(p, 'vector', lambda g, W=W, q_a=q_a: g.tensor_scalar(
            q_lt.ap[:, 0:W], q_a.ap[:, 0:W], 0.25, None, ALU.is_gt),
            reads=[q_a], writes=[q_lt])
        oprw(p, 'vector', lambda g, W=W: g.tensor_scalar(
            q_m1.ap[:, 0:W], q_a.ap[:, 0:W], 0.75, None, ALU.is_gt),
            reads=[q_a], writes=[q_m1])
        oprw(p, 'vector', lambda g, W=W: g.tensor_scalar(
            q_m2.ap[:, 0:W], q_a.ap[:, 0:W], 1.25, None, ALU.is_gt),
            reads=[q_a], writes=[q_m2])
        oprw(p, 'vector', lambda g, W=W: g.tensor_scalar(
            q_m3.ap[:, 0:W], q_a.ap[:, 0:W], 1.75, None, ALU.is_gt),
            reads=[q_a], writes=[q_m3])
        oprw(p, 'vector', lambda g, W=W: g.tensor_tensor(
            q_lt.ap[:, 0:W], q_lt.ap[:, 0:W], q_m1.ap[:, 0:W], ALU.add),
            reads=[q_m1], writes=[q_lt])
        oprw(p, 'vector', lambda g, W=W: g.tensor_scalar(
            q_m1.ap[:, 0:W], q_a.ap[:, 0:W], 2.5, 2.0, ALU.is_gt, ALU.mult),
            reads=[q_a], writes=[q_m1])
        oprw(p, 'vector', lambda g, W=W: g.tensor_tensor(
            q_m2.ap[:, 0:W], q_m2.ap[:, 0:W], q_m3.ap[:, 0:W], ALU.add),
            reads=[q_m3], writes=[q_m2])
        oprw(p, 'vector', lambda g, W=W: g.tensor_scalar(
            q_m3.ap[:, 0:W], q_a.ap[:, 0:W], 3.5, 2.0, ALU.is_gt, ALU.mult),
            reads=[q_a], writes=[q_m3])
        oprw(p, 'vector', lambda g, W=W: g.tensor_tensor(
            q_lt.ap[:, 0:W], q_lt.ap[:, 0:W], q_m2.ap[:, 0:W], ALU.add),
            reads=[q_m2], writes=[q_lt])
        oprw(p, 'vector', lambda g, W=W: g.tensor_scalar(
            q_m2.ap[:, 0:W], q_a.ap[:, 0:W], 5.0, 4.0, ALU.is_gt, ALU.mult),
            reads=[q_a], writes=[q_m2])
        oprw(p, 'vector', lambda g, W=W: g.tensor_tensor(
            q_m1.ap[:, 0:W], q_m1.ap[:, 0:W], q_m3.ap[:, 0:W], ALU.add),
            reads=[q_m3], writes=[q_m1])
        oprw(p, 'vector', lambda g, W=W: g.tensor_tensor(
            q_m1.ap[:, 0:W], q_m1.ap[:, 0:W], q_m2.ap[:, 0:W], ALU.add),
            reads=[q_m2], writes=[q_m1])
        oprw(p, 'vector', lambda g, W=W: g.tensor_tensor(
            q_g2.ap[:, 0:W], q_lt.ap[:, 0:W], q_m1.ap[:, 0:W], ALU.add),
            reads=[q_lt, q_m1], writes=[q_g2])
        return oprw(p, 'vector', lambda v, W=W, doff=doff, ssb=ssb: v.tensor_tensor(
            dst.ap[:, doff:doff + W], q_g2.ap[:, 0:W], ssb.ap[:, 0:W],
            ALU.mult), reads=[q_g2, ssb], writes=[dst])

    def emit_quant(src, soff, dst, doff, W, tok_extra=(), eng='vector'):
        emit_quant_pre(src, soff, W, QSET0, tok_extra)
        return emit_quant_main(dst, doff, W, QSET0)

    dmarw(p, 'sync', lambda E: E.dma_start(ident.ap[:], d_ident[:]), 'win',
          writes=[ident])
    dmarw(p, 'sync', lambda E: E.dma_start(ones1.ap[:], d_ones1[:]), 'win',
          writes=[ones1])
    dmarw(p, 'sync', lambda E: E.dma_start(
        s1col.ap[:], d_scale1.rearrange("o (a p) -> (o p) a", p=128),
        allow_slow_non_contiguous=True), 'win', writes=[s1col])
    dmarw(p, 'sync', lambda E: E.dma_start(
        s2col.ap[:], d_scale2.rearrange("o (a p) -> (o p) a", p=128),
        allow_slow_non_contiguous=True), 'win', writes=[s2col])
    dmarw(p, 'sync', lambda E: E.dma_start(
        gammar.ap[:], d_gamma.partition_broadcast(128)), 'win', writes=[gammar])
    dmarw(p, 'sync', lambda E: E.dma_start(rowtmp.ap[:], d_pb[:]), 'win',
          writes=[rowtmp])
    oprw(p, 'vector', lambda v: v.tensor_tensor(
        gpb.ap[:], rowtmp.ap[:], gammar.ap[0:1, :], ALU.mult),
        reads=[rowtmp, gammar], writes=[gpb])
    dmarw(p, 'sync', lambda E: E.dma_start(rowtmp2.ap[:], d_c2b[:]), 'win',
          writes=[rowtmp2])
    oprw(p, 'vector', lambda v: v.tensor_tensor(
        gc2b.ap[:], rowtmp2.ap[:], gammar.ap[0:1, :], ALU.mult),
        reads=[rowtmp2, gammar], writes=[gc2b])
    wTsh = esA.enter_context(nc.sbuf_tensor('wTs', [128, 8 * 3 * C], BF16))
    wTsC = [Buf(wTsh, f'wTs{kc}') for kc in range(8)]
    wld = SB('wld', [128, 2 * 3 * C], F32, stack=esA)
    for kc in range(8):
        half = kc % 2
        ld = dmarw(p, 'sync', lambda E, kc=kc, half=half: E.dma_start(
            wld.ap[:, half * 3 * C:(half + 1) * 3 * C],
            d_qkvT[kc * 128:(kc + 1) * 128, :]), 'wqkv', writes=[wld])
        oprw(p, 'vector', lambda v, kc=kc, half=half: v.tensor_scalar(
            wTsC[kc].ap[:, kc * 3 * C:(kc + 1) * 3 * C],
            wld.ap[:, half * 3 * C:(half + 1) * 3 * C],
            s1col.ap[:, kc:kc + 1], None, ALU.mult),
            reads=[wld, s1col], writes=[wTsC[kc]])

    xb = [SB('xb0', [128, C], F32, stack=esA), SB('xb1', [128, C], F32, stack=esA), SB('xb2', [128, C], F32, stack=esA)]
    xn1P = [SB('xn1a', [128, C], BF16, stack=esA), SB('xn1b', [128, C], BF16, stack=esA), SB('xn1c', [128, C], BF16, stack=esA)]
    xTP = [SB('xTa', [128, C], BF16, stack=esA), SB('xTb', [128, C], BF16, stack=esA)]
    ksq1 = SB('ksq1', [128, C], F32, stack=esA)
    ksqP = [ksq1, ksq1]
    kphiP = [SB('kphia', [128, C], BF16, stack=esA), SB('kphib', [128, C], BF16, stack=esA)]
    vextP = [SB('vexta', [128, NH * 65], BF16, stack=esA), SB('vextb', [128, NH * 65], BF16, stack=esA)]
    qphiP = [SB('qphia', [128, C], BF16, stack=esA), SB('qphib', [128, C], BF16, stack=esA)]

    for _ve in vextP:
        oprw(p, 'vector', lambda v, _ve=_ve: v.memset(_ve.ap[:], 1.0), writes=[_ve])

    wjobs = []
    for r in range(IFF // 128):
        wjobs.append((d_wg, d_wgq, r, 0, C))
    for r in range(IFF // 128):
        wjobs.append((d_wu, d_wuq, r, 0, C))
    for r in range(C // 128):
        wjobs.append((d_wd, d_wdq, r, 0, WJW))
        wjobs.append((d_wd, d_wdq, r, WJW, WJW))
    wj_i = 0

    def emit_wjob_load(i):
        src, dst, r, co, W = wjobs[i]
        buf = wqld[i % 2]
        dmarw(p, 'gpsimd', lambda E, src=src, r=r, co=co, W=W, buf=buf: E.dma_start(
            buf.ap[:, 0:W], src[r * 128:(r + 1) * 128, co:co + W]),
            f'wstr{i % 2}', writes=[buf])

    def emit_wjob_prerest(i, qs=None):
        src, dst, r, co, W = wjobs[i]
        buf = wqld[i % 2]
        qs = qs or (QSET1 if i % 2 else QSET0)
        emit_quant_pre(buf, 0, min(1024, W), qs)

    def emit_wjob_pre(i, qs=None):
        emit_wjob_load(i)
        emit_wjob_prerest(i, qs)

    def emit_wjob_main(i, qs=None):
        src, dst, r, co, W = wjobs[i]
        buf = wqld[i % 2]
        qs = qs or (QSET1 if i % 2 else QSET0)
        emit_quant_main(wqout, 0, min(1024, W), qs)
        for off in range(1024, W, 1024):
            w = min(1024, W - off)
            emit_quant_pre(buf, off, w, qs)
            emit_quant_main(wqout, off, w, qs)
        dmarw(p, 'gpsimd', lambda E, dst=dst, r=r, co=co, W=W: E.dma_start(
            dst[r * 128:(r + 1) * 128, co:co + W], wqout.ap[:, 0:W]), 'wstw',
            reads=[wqout])

    def emit_wjob(i, qs=None):
        emit_wjob_pre(i, qs)
        emit_wjob_main(i, qs)

    n_w_per_tile = 1

    def stageA1a(t):
        x = xb[t % 3]
        mv = mvP[t % 3]
        dmarw(p, 'sync', lambda E, t=t, x=x: E.dma_start(
            x.ap[:], d_x_all[t * 128:(t + 1) * 128, :]), f'xinA{t % 3}',
            writes=[x])
        oprw(p, 'vector', lambda v, x=x, mv=mv: v.bn_stats(
            mv.ap[:, 0:6], x.ap[:, 0:512]), reads=[x], writes=[mv])
        oprw(p, 'vector', lambda v, x=x, mv=mv: v.bn_stats(
            mv.ap[:, 6:12], x.ap[:, 512:1024]), reads=[x], writes=[mv])
        oprw(p, 'vector', lambda v, mv=mv: v.bn_aggr(
            mv.ap[:, 12:14], mv.ap[:, 0:12]), writes=[mv])
        oprw(p, 'vector', lambda v, mv=mv: v.tensor_scalar(
            mv.ap[:, 14:15], mv.ap[:, 13:14], 1e-5, None, ALU.add), writes=[mv])
        oprw(p, 'vector', lambda v, mv=mv: v.tensor_copy(
            mv.ap[:, 15:16], mv.ap[:, 13:14]), writes=[mv])
        oprw(p, 'scalar', lambda sc, mv=mv: sc.activation(
            mv.ap[:, 14:16], mv.ap[:, 14:16], AF.Ln), writes=[mv])
        oprw(p, 'scalar', lambda sc, mv=mv: sc.activation(
            mv.ap[:, 14:16], mv.ap[:, 14:16], AF.Exp, scale=-0.5), writes=[mv])

    def stageA1b(t):
        x = xb[t % 3]
        xn1 = xn1P[t % 3]
        mv = mvP[t % 3]
        is_own = (t % cpb) == 0
        o = t // cpb
        if is_own:
            oprw(p, 'vector', lambda v, o=o, mv=mv: v.tensor_copy(
                stats.ap[:, 2 * o:2 * o + 1], mv.ap[:, 12:13]),
                reads=[mv], writes=[stats])
            oprw(p, 'vector', lambda v, o=o, mv=mv: v.tensor_copy(
                stats.ap[:, 2 * o + 1:2 * o + 2], mv.ap[:, 14:15]),
                reads=[mv], writes=[stats])
        oprw(p, 'vector', lambda v, x=x, mv=mv, xn1=xn1: v.tensor_scalar(
            xn1.ap[:], x.ap[:], mv.ap[:, 12:13], mv.ap[:, 15:16],
            ALU.subtract, ALU.mult), reads=[x, mv], writes=[xn1])

    def stageA2(t):
        xn1 = xn1P[t % 3]
        xT = xTP[t % 2]
        ksq = ksqP[t % 2]
        kphi = kphiP[t % 2]
        vext = vextP[t % 2]
        qphi = qphiP[t % 2]
        is_own = (t % cpb) == 0
        o = t // cpb
        for kc in range(8):
            oprw(p, 'tensor', lambda T, kc=kc, xn1=xn1: T.transpose(
                TR.ap[:, kc * 128:(kc + 1) * 128],
                xn1.ap[:, kc * 128:(kc + 1) * 128], ident.ap[:]),
                reads=[xn1, ident], writes=[TR])
        oprw(p, 'scalar' if t < 40 else 'vector',
             lambda e_, xT=xT: e_.copy(xT.ap[:], TR.ap[:]) if hasattr(e_, 'copy')
             else e_.tensor_copy(xT.ap[:], TR.ap[:]),
             reads=[TR], writes=[xT])
        for kc in range(8):
            for hh in range(2):
                oprw(p, 'tensor', lambda T, kc=kc, hh=hh, xT=xT: T.matmul(
                    (PA if hh == 0 else PB).ap[:],
                    xT.ap[:, kc * 128:(kc + 1) * 128],
                    wTsC[kc].ap[:, kc * 3 * C + C + hh * 512: kc * 3 * C + C + (hh + 1) * 512],
                    start=(kc == 0), stop=(kc == 7)),
                    reads=[xT, wTsC[kc]], writes=[PA if hh == 0 else PB])
        for kc in range(8):
            for hh in range(2):
                oprw(p, 'tensor', lambda T, kc=kc, hh=hh, xT=xT: T.matmul(
                    (PC if hh == 0 else PD).ap[:],
                    xT.ap[:, kc * 128:(kc + 1) * 128],
                    wTsC[kc].ap[:, kc * 3 * C + 2 * C + hh * 512: kc * 3 * C + 2 * C + (hh + 1) * 512],
                    start=(kc == 0), stop=(kc == 7)),
                    reads=[xT, wTsC[kc]], writes=[PC if hh == 0 else PD])
        oprw(p, 'scalar', lambda sc, ksq=ksq: sc.activation(
            ksq.ap[:, 0:512], PA.ap[:], AF.Square), reads=[PA], writes=[ksq])
        oprw(p, 'scalar', lambda sc, ksq=ksq: sc.activation(
            ksq.ap[:, 512:1024], PB.ap[:], AF.Square), reads=[PB], writes=[ksq])
        oprw(p, 'scalar', lambda sc, ksq=ksq, kphi=kphi: sc.activation(
            kphi.ap[:], ksq.ap[:], AF.Exp, scale=-0.5), reads=[ksq], writes=[kphi])
        if is_own:
            for kc in range(8):
                for hh in range(2):
                    oprw(p, 'tensor', lambda T, kc=kc, hh=hh, xT=xT: T.matmul(
                        (PA if hh == 0 else PB).ap[:],
                        xT.ap[:, kc * 128:(kc + 1) * 128],
                        wTsC[kc].ap[:, kc * 3 * C + hh * 512: kc * 3 * C + (hh + 1) * 512],
                        start=(kc == 0), stop=(kc == 7)),
                        reads=[xT, wTsC[kc]], writes=[PA if hh == 0 else PB])
        oprw(p, 'scalar', lambda sc, vext=vext: sc.copy(
            vext.ap[:].rearrange("p (h e) -> p h e", e=65)[:, 0:8, 0:64],
            PC.ap[:].rearrange("p (h e) -> p h e", e=64)[:, 0:8, :]),
            reads=[PC], writes=[vext])
        oprw(p, 'scalar', lambda sc, vext=vext: sc.copy(
            vext.ap[:].rearrange("p (h e) -> p h e", e=65)[:, 8:16, 0:64],
            PD.ap[:].rearrange("p (h e) -> p h e", e=64)[:, 0:8, :]),
            reads=[PD], writes=[vext])
        if is_own:
            oprw(p, 'scalar', lambda sc, ksq=ksq: sc.activation(
                ksq.ap[:, 0:512], PA.ap[:], AF.Square), reads=[PA], writes=[ksq])
            oprw(p, 'scalar', lambda sc, ksq=ksq: sc.activation(
                ksq.ap[:, 512:1024], PB.ap[:], AF.Square), reads=[PB], writes=[ksq])
            oprw(p, 'scalar', lambda sc, ksq=ksq, qphi=qphi: sc.activation(
                qphi.ap[:], ksq.ap[:], AF.Exp, scale=-0.5),
                reads=[ksq], writes=[qphi])
            dmarw(p, 'scalar', lambda E, o=o, qphi=qphi: E.dma_start(
                d_qspill[o * 128:(o + 1) * 128, :], qphi.ap[:]), 'qsp',
                reads=[qphi])

    def stageA2kv(t):
        kphi = kphiP[t % 2]
        vext = vextP[t % 2]
        for j in range(8):
            bank = (KV1, KV2, KV3)[j // 3]
            slot = j % 3
            st = (t == 0 and slot == 0)
            sp = (t == T_batch - 1 and j in (2, 5, 7))
            oprw(p, 'tensor', lambda T, j=j, bank=bank, slot=slot, st=st, sp=sp, kphi=kphi, vext=vext: T.matmul(
                bank.ap[:, slot * 130:(slot + 1) * 130],
                kphi.ap[:, j * 128:(j + 1) * 128],
                vext.ap[:, j * 130:(j + 1) * 130],
                start=st, stop=sp),
                reads=[kphi, vext], writes=[bank])

    stageA1a(0)
    stageA1b(0)
    stageA1a(1)
    w_pre = 0
    w_load = 0
    for t in range(T_batch):
        if t + 2 < T_batch:
            stageA1a(t + 2)
        if t + 1 < T_batch:
            stageA1b(t + 1)
        if (t < 34 or t % 2 == 0) and w_load < min(len(wjobs), 38):
            emit_wjob_load(w_load)
            w_load += 1
            if w_pre < w_load - 1:
                emit_wjob_prerest(w_pre)
                w_pre += 1
        stageA2(t)
        if t > 0:
            stageA2kv(t - 1)
        if wj_i < w_pre - 1:
            emit_wjob_main(wj_i)
            wj_i += 1
    stageA2kv(T_batch - 1)
    while w_pre < w_load:
        emit_wjob_prerest(w_pre)
        w_pre += 1
    while wj_i < w_pre:
        emit_wjob_main(wj_i)
        wj_i += 1

    p.barrier()
    esA.close()

    projTgh = esB.enter_context(nc.sbuf_tensor('projTg', [128, 8 * C], BF16))
    projTgC = [Buf(projTgh, f'projTg{kc}') for kc in range(8)]
    wldB = SB('wldB', [128, HID], F32, stack=esB)
    for kc in range(8):
        dmarw(p, 'sync', lambda E, kc=kc: E.dma_start(
            wldB.ap[:, 0:C],
            d_projT[kc * 128:(kc + 1) * 128, :]), 'wB', writes=[wldB])
        oprw(p, 'vector', lambda v, kc=kc: v.tensor_tensor(
            projTgC[kc].ap[:, kc * C:(kc + 1) * C],
            wldB.ap[:, 0:C], gammar.ap[:], ALU.mult),
            reads=[wldB, gammar], writes=[projTgC[kc]])

    qTallh = esB.enter_context(nc.sbuf_tensor('qTall', [128, 8 * S_core], BF16))
    qTallC = [Buf(qTallh, f'qTall{j}') for j in range(8)]
    for j in range(8):
        dmarw(p, 'sync', lambda E, j=j: E.dma_start(
            qTallC[j].ap[:, j * S_core:(j + 1) * S_core],
            d_qspill[:, j * 128:(j + 1) * 128],
            transpose=True), f'qrd{j % 2}', writes=[qTallC[j]])
    kv_b = SB('kv_b', [128, 8 * 130], BF16, stack=esB)
    oprw(p, 'vector', lambda v: v.memset(kv_b.ap[:], 0.0), writes=[kv_b])
    for j in range(8):
        bank = (KV1, KV2, KV3)[j // 3]
        slot = j % 3
        oprw(p, 'vector', lambda v, j=j, bank=bank, slot=slot: v.tensor_copy(
            kv_b.ap[0:64, j * 130:j * 130 + 65],
            bank.ap[0:64, slot * 130:slot * 130 + 65]),
            reads=[bank], writes=[kv_b])
        oprw(p, 'vector', lambda v, j=j, bank=bank, slot=slot: v.tensor_copy(
            kv_b.ap[64:128, j * 130 + 65:j * 130 + 130],
            bank.ap[64:128, slot * 130 + 65:slot * 130 + 130]),
            reads=[bank], writes=[kv_b])
    den = SB('den', [128, NH], F32, stack=esB)
    zz = SB('zz', [128, NH], F32, stack=esB)
    outzP = [SB('outza', [128, C], BF16, stack=esB), SB('outzb', [128, C], BF16, stack=esB)]
    ozTP = [SB('ozTa', [128, C], BF16, stack=esB), SB('ozTb', [128, C], BF16, stack=esB)]
    xn2P = [SB('xn2a', [128, C], BF16, stack=esB), SB('xn2b', [128, C], BF16, stack=esB)]
    xnBP = [SB('xnBa', [128, C], F32, stack=esB), SB('xnBb', [128, C], F32, stack=esB)]
    oaP = [SB('oaa', [128, C], F32, stack=esB), SB('oab', [128, C], F32, stack=esB)]
    xbB = [SB('xbBa', [128, C], F32, stack=esB), SB('xbBb', [128, C], F32, stack=esB)]
    x1P = [SB('x1a', [128, C], F32, stack=esB), SB('x1b', [128, C], F32, stack=esB)]
    xqBP = [SB('xqBa', [128, C], BF16, stack=esB), SB('xqBb', [128, C], BF16, stack=esB)]

    OUTB = (KV1, KV2, KV3)

    def stageBa(t):
        pr = t % 2
        x = xbB[pr]
        dmarw(p, 'sync', lambda E, t=t, x=x: E.dma_start(
            x.ap[:], d_x_own[t * 128:(t + 1) * 128, :]), f'xin{t % 2}',
            writes=[x])

    def stageBb(t):
        pr = t % 2
        x = xbB[pr]
        outz = outzP[pr]
        ozT = ozTP[pr]
        xn = xnBP[pr]
        x1 = x1P[pr]
        oacc = oaP[pr]
        xn2 = xn2P[pr]
        mv = mvP[pr]
        oprw(p, 'vector', lambda v, x=x, xn=xn, t=t: v.tensor_scalar(
            xn.ap[:], x.ap[:], stats.ap[:, 2 * t:2 * t + 1],
            stats.ap[:, 2 * t + 1:2 * t + 2], ALU.subtract, ALU.mult),
            reads=[x, stats], writes=[xn])
        for j in range(8):
            bank = OUTB[j // 3]
            slot = j % 3
            oprw(p, 'tensor', lambda T, j=j, bank=bank, slot=slot, t=t: T.matmul(
                bank.ap[:, slot * 130:(slot + 1) * 130],
                qTallC[j].ap[:, j * S_core + t * 128:j * S_core + (t + 1) * 128],
                kv_b.ap[:, j * 130:(j + 1) * 130],
                start=True, stop=True),
                reads=[qTallC[j], kv_b], writes=[bank])
        oprw(p, 'vector', lambda v: v.tensor_copy(
            den.ap[:, 0:6],
            OUTB[0].ap[:, 0:390].rearrange("p (h e) -> p h e", e=65)[:, :, 64]),
            reads=[OUTB[0]], writes=[den])
        oprw(p, 'vector', lambda v: v.tensor_copy(
            den.ap[:, 6:12],
            OUTB[1].ap[:, 0:390].rearrange("p (h e) -> p h e", e=65)[:, :, 64]),
            reads=[OUTB[1]], writes=[den])
        oprw(p, 'vector', lambda v: v.tensor_copy(
            den.ap[:, 12:16],
            OUTB[2].ap[:, 0:260].rearrange("p (h e) -> p h e", e=65)[:, :, 64]),
            reads=[OUTB[2]], writes=[den])
        oprw(p, 'vector', lambda v: v.tensor_scalar(
            den.ap[:], den.ap[:], 1e-6, None, ALU.add), writes=[den])
        oprw(p, 'vector', lambda v: v.reciprocal(zz.ap[:], den.ap[:]),
             reads=[den], writes=[zz])
        for bi, (b0, h0, nh) in enumerate(((OUTB[0], 0, 6), (OUTB[1], 6, 6),
                                           (OUTB[2], 12, 4))):
            oprw(p, 'vector', lambda v, b0=b0, h0=h0, nh=nh, outz=outz: v.tensor_tensor(
                outz.ap[:, h0 * HD:(h0 + nh) * HD].rearrange(
                    "p (h e) -> p h e", e=HD),
                b0.ap[:, 0:nh * 65].rearrange("p (h e) -> p h e", e=65)[:, :, 0:HD],
                zz.ap[:, h0:h0 + nh].unsqueeze(2).broadcast_to([128, nh, HD]),
                ALU.mult), reads=[b0, zz], writes=[outz])
        for kc in range(8):
            oprw(p, 'tensor', lambda T, kc=kc, outz=outz: T.transpose(
                TR.ap[:, kc * 128:(kc + 1) * 128],
                outz.ap[:, kc * 128:(kc + 1) * 128], ident.ap[:]),
                reads=[outz, ident], writes=[TR])
        oprw(p, 'scalar', lambda sc, ozT=ozT: sc.copy(ozT.ap[:], TR.ap[:]),
             reads=[TR], writes=[ozT])
        prA, prB = (PA, PB) if t % 2 == 0 else (PC, PD)
        for hh in range(2):
            pbk = prA if hh == 0 else prB
            for kc in range(8):
                oprw(p, 'tensor', lambda T, kc=kc, hh=hh, pbk=pbk, ozT=ozT: T.matmul(
                    pbk.ap[:], ozT.ap[:, kc * 128:(kc + 1) * 128],
                    projTgC[kc].ap[:, kc * C + hh * 512: kc * C + (hh + 1) * 512],
                    start=(kc == 0), stop=False),
                    reads=[ozT, projTgC[kc]], writes=[pbk])
            oprw(p, 'tensor', lambda T, hh=hh, pbk=pbk: T.matmul(
                pbk.ap[:], ones1.ap[:],
                gpb.ap[:, hh * 512:(hh + 1) * 512], start=False, stop=True),
                reads=[ones1, gpb], writes=[pbk])
    def stageBb2(t):
        pr = t % 2
        x = xbB[pr]
        xn = xnBP[pr]
        x1 = x1P[pr]
        oacc = oaP[pr]
        xn2 = xn2P[pr]
        mv = mvP[pr]
        prA, prB = (PA, PB) if t % 2 == 0 else (PC, PD)
        for hh, pbk in ((0, prA), (1, prB)):
            oprw(p, 'vector', lambda v, hh=hh, pbk=pbk, x1=x1, xn=xn: v.tensor_tensor(
                x1.ap[:, hh * 512:(hh + 1) * 512], pbk.ap[:],
                xn.ap[:, hh * 512:(hh + 1) * 512], ALU.add),
                reads=[pbk, xn], writes=[x1])
            oprw(p, 'vector', lambda v, hh=hh, oacc=oacc, x1=x1, x=x: v.tensor_tensor(
                oacc.ap[:, hh * 512:(hh + 1) * 512],
                x1.ap[:, hh * 512:(hh + 1) * 512],
                x.ap[:, hh * 512:(hh + 1) * 512], ALU.add),
                reads=[x1, x], writes=[oacc])
        dmarw(p, 'scalar', lambda E, t=t, oacc=oacc: E.dma_start(
            d_oaspill[t * 128:(t + 1) * 128, :], oacc.ap[:]), 'oas', reads=[oacc])
        oprw(p, 'vector', lambda v, mv=mv, x1=x1: v.bn_stats(
            mv.ap[:, 0:6], x1.ap[:, 0:512]), reads=[x1], writes=[mv])
        oprw(p, 'vector', lambda v, mv=mv, x1=x1: v.bn_stats(
            mv.ap[:, 6:12], x1.ap[:, 512:1024]), reads=[x1], writes=[mv])
        oprw(p, 'vector', lambda v, mv=mv: v.bn_aggr(
            mv.ap[:, 12:14], mv.ap[:, 0:12]), writes=[mv])
        oprw(p, 'vector', lambda v, mv=mv: v.tensor_scalar(
            mv.ap[:, 14:15], mv.ap[:, 12:13], mv.ap[:, 12:13], None,
            ALU.mult), writes=[mv])
        oprw(p, 'vector', lambda v, mv=mv: v.tensor_tensor(
            mv.ap[:, 14:15], mv.ap[:, 14:15], mv.ap[:, 13:14], ALU.add),
            writes=[mv])
        oprw(p, 'scalar', lambda sc, mv=mv: sc.activation(
            mv.ap[:, 14:15], mv.ap[:, 14:15], AF.Ln, scale=1024.0), writes=[mv])
        oprw(p, 'scalar', lambda sc, mv=mv: sc.activation(
            mv.ap[:, 14:15], mv.ap[:, 14:15], AF.Exp, scale=-0.5), writes=[mv])
        oprw(p, 'vector', lambda v, mv=mv, x1=x1, xn2=xn2: v.tensor_scalar(
            xn2.ap[:], x1.ap[:], mv.ap[:, 14:15], 32.0, ALU.mult, ALU.mult),
            reads=[x1, mv], writes=[xn2])
        dmarw(p, 'scalar', lambda E, t=t, xn2=xn2: E.dma_start(
            d_xn2spill[t * 128:(t + 1) * 128, :], xn2.ap[:]), 'xn2s', reads=[xn2])
        if t > 0:
            xqp = xqBP[(t - 1) % 2]
            emit_quant_main(xqp, 0, C, QSET0)
            dmarw(p, 'scalar', lambda E, t=t, xqp=xqp: E.dma_start(
                d_xqspill[(t - 1) * 128:t * 128, :], xqp.ap[:]), 'xqs',
                reads=[xqp])
        emit_quant_pre(xn, 0, C, QSET0)

    stageBa(0)
    for t in range(T_own):
        stageBb(t)
        if t > 0:
            stageBb2(t - 1)
        if t + 1 < T_own:
            stageBa(t + 1)
        if t % 3 == 2 and wj_i < min(len(wjobs), 40):
            emit_wjob(wj_i, QSET1)
            wj_i += 1
    stageBb2(T_own - 1)
    xqp = xqBP[(T_own - 1) % 2]
    emit_quant_main(xqp, 0, C, QSET0)
    dmarw(p, 'scalar', lambda E, xqp=xqp: E.dma_start(
        d_xqspill[(T_own - 1) * 128:T_own * 128, :], xqp.ap[:]), 'xqs',
        reads=[xqp])

    p.barrier()
    esB.close()

    c1Tsh = esB2.enter_context(nc.sbuf_tensor('c1Ts', [128, 8 * HID], BF16))
    c1TsC = [Buf(c1Tsh, f'c1Ts{kc}') for kc in range(8)]
    c2Tgh = esB2.enter_context(nc.sbuf_tensor('c2Tg', [128, 16 * C], BF16))
    c2TgC = [Buf(c2Tgh, f'c2Tg{kc}') for kc in range(16)]
    wldB2 = SB('wldB2', [128, C], F32, stack=esB2)
    for kc in range(8):
        for hf in range(2):
            dmarw(p, 'sync', lambda E, kc=kc, hf=hf: E.dma_start(
                wldB2.ap[:],
                d_c1T[kc * 128:(kc + 1) * 128, hf * C:(hf + 1) * C]), 'wB',
                writes=[wldB2])
            oprw(p, 'vector', lambda v, kc=kc, hf=hf: v.tensor_scalar(
                c1TsC[kc].ap[:, kc * HID + hf * C:kc * HID + (hf + 1) * C],
                wldB2.ap[:],
                s2col.ap[:, kc:kc + 1], None, ALU.mult),
                reads=[wldB2, s2col], writes=[c1TsC[kc]])
    for kc in range(16):
        dmarw(p, 'sync', lambda E, kc=kc: E.dma_start(
            wldB2.ap[:, 0:C],
            d_c2T[kc * 128:(kc + 1) * 128, :]), 'wB', writes=[wldB2])
        oprw(p, 'vector', lambda v, kc=kc: v.tensor_tensor(
            c2TgC[kc].ap[:, kc * C:(kc + 1) * C],
            wldB2.ap[:, 0:C], gammar.ap[:], ALU.mult),
            reads=[wldB2, gammar], writes=[c2TgC[kc]])

    x2Tallh = esB2.enter_context(nc.sbuf_tensor('x2Tall', [128, 8 * S_core], BF16))
    x2TallC = [Buf(x2Tallh, f'x2Tall{j}') for j in range(8)]
    for j in range(8):
        dmarw(p, 'sync', lambda E, j=j: E.dma_start(
            x2TallC[j].ap[:, j * S_core:(j + 1) * S_core],
            d_xn2spill[:, j * 128:(j + 1) * 128],
            transpose=True), f'xn2r{j % 2}', writes=[x2TallC[j]])
    hddP = [SB('hdda', [128, HID], BF16, stack=esB2), SB('hddb', [128, HID], BF16, stack=esB2)]
    hdTP = [SB('hdTa', [128, HID], BF16, stack=esB2), SB('hdTb', [128, HID], BF16, stack=esB2)]
    oaiP = [SB('oaia', [128, C], F32, stack=esB2), SB('oaib', [128, C], F32, stack=esB2)]
    c1brow = SB('c1brow', [1, HID], BF16, stack=esB2)
    dmarw(p, 'gpsimd', lambda E: E.dma_start(c1brow.ap[:], d_c1b[:]), 'gwin',
          writes=[c1brow])

    def stageB2oai(t):
        oai = oaiP[t % 2]
        dmarw(p, 'sync', lambda E, t=t, oai=oai: E.dma_start(
            oai.ap[:], d_oaspill[t * 128:(t + 1) * 128, :]), 'oar', writes=[oai])

    def stageB2m(t):
        pr = t % 2
        hdd = hddP[pr]
        C1B = (PA, PB, PC, PD)
        for hh in range(4):
            pbk = C1B[hh]
            for kc in range(8):
                oprw(p, 'tensor', lambda T, kc=kc, hh=hh, pbk=pbk, t=t: T.matmul(
                    pbk.ap[:], x2TallC[kc].ap[:, kc * S_core + t * 128:kc * S_core + (t + 1) * 128],
                    c1TsC[kc].ap[:, kc * HID + hh * 512: kc * HID + (hh + 1) * 512],
                    start=(kc == 0), stop=False),
                    reads=[x2TallC[kc], c1TsC[kc]], writes=[pbk])
            oprw(p, 'tensor', lambda T, hh=hh, pbk=pbk: T.matmul(
                pbk.ap[:], ones1.ap[:],
                c1brow.ap[:, hh * 512:(hh + 1) * 512], start=False, stop=True),
                reads=[ones1, c1brow], writes=[pbk])
        for hh in range(4):
            pbk = C1B[hh]
            oprw(p, 'scalar', lambda sc, hh=hh, pbk=pbk, hdd=hdd: sc.activation(
                hdd.ap[:, hh * 512:(hh + 1) * 512], pbk.ap[:], AF.Gelu),
                reads=[pbk], writes=[hdd])

    def stageB2c(t):
        pr = t % 2
        hdd = hddP[pr]
        hdT = hdTP[pr]
        oai = oaiP[pr]
        yb = oai
        for rnd in range(2):
            for kc in range(rnd * 8, rnd * 8 + 8):
                oprw(p, 'tensor', lambda T, kc=kc, rnd=rnd, hdd=hdd: T.transpose(
                    TR.ap[:, (kc - rnd * 8) * 128:(kc - rnd * 8 + 1) * 128],
                    hdd.ap[:, kc * 128:(kc + 1) * 128], ident.ap[:]),
                    reads=[hdd, ident], writes=[TR])
            oprw(p, 'scalar', lambda sc, rnd=rnd, hdT=hdT: sc.copy(
                hdT.ap[:, rnd * 1024:(rnd + 1) * 1024], TR.ap[:]),
                reads=[TR], writes=[hdT])
        for hh in range(2):
            pbk = KV1 if hh == 0 else KV2
            for kc in range(16):
                oprw(p, 'tensor', lambda T, kc=kc, hh=hh, pbk=pbk, hdT=hdT: T.matmul(
                    pbk.ap[:], hdT.ap[:, kc * 128:(kc + 1) * 128],
                    c2TgC[kc].ap[:, kc * C + hh * 512: kc * C + (hh + 1) * 512],
                    start=(kc == 0), stop=False),
                    reads=[hdT, c2TgC[kc]], writes=[pbk])
            oprw(p, 'tensor', lambda T, hh=hh, pbk=pbk: T.matmul(
                pbk.ap[:], ones1.ap[:],
                gc2b.ap[:, hh * 512:(hh + 1) * 512], start=False, stop=True),
                reads=[ones1, gc2b], writes=[pbk])
        for hh, pbk in ((0, KV1), (1, KV2)):
            oprw(p, 'vector', lambda v, hh=hh, pbk=pbk, yb=yb, oai=oai: v.tensor_tensor(
                yb.ap[:, hh * 512:(hh + 1) * 512], pbk.ap[:],
                oai.ap[:, hh * 512:(hh + 1) * 512], ALU.add),
                reads=[pbk, oai], writes=[yb])
        dmarw(p, 'sync', lambda E, t=t, yb=yb: E.dma_start(
            d_y1spill[t * 128:(t + 1) * 128, :], yb.ap[:]), 'y1s', reads=[yb])

    stageB2oai(0)
    for t in range(T_own):
        stageB2m(t)
        if t > 0:
            stageB2c(t - 1)
        if t + 1 < T_own:
            stageB2oai(t + 1)
        if wj_i < len(wjobs):
            emit_wjob(wj_i)
            wj_i += 1
    stageB2c(T_own - 1)

    while wj_i < len(wjobs):
        emit_wjob(wj_i)
        wj_i += 1

    p.barrier()
    esB2.close()
    esW.close()


    wgTh = esF1.enter_context(nc.sbuf_tensor('wgT', [128, 8 * IFF], BF16))
    wgTC = [Buf(wgTh, f'wgT{kc}') for kc in range(8)]
    wuTh = esF1.enter_context(nc.sbuf_tensor('wuT', [128, 8 * IFF], BF16))
    wuTC = [Buf(wuTh, f'wuT{kc}') for kc in range(8)]
    for kc in range(8):
        dmarw(p, 'sync', lambda E, kc=kc: E.dma_start(
            wgTC[kc].ap[:, kc * IFF:(kc + 1) * IFF],
            d_wgq[:, kc * 128:(kc + 1) * 128], transpose=True), f'wgt{kc % 2}',
            writes=[wgTC[kc]])
        dmarw(p, 'sync', lambda E, kc=kc: E.dma_start(
            wuTC[kc].ap[:, kc * IFF:(kc + 1) * IFF],
            d_wuq[:, kc * 128:(kc + 1) * 128], transpose=True), f'wut{kc % 2}',
            writes=[wuTC[kc]])

    xqTallh = esF1.enter_context(nc.sbuf_tensor('xqTall', [128, 8 * S_core], BF16))
    xqTallC = [Buf(xqTallh, f'xqTall{j}') for j in range(8)]
    for j in range(8):
        dmarw(p, 'sync', lambda E, j=j: E.dma_start(
            xqTallC[j].ap[:, j * S_core:(j + 1) * S_core],
            d_xqspill[:, j * 128:(j + 1) * 128],
            transpose=True), f'xqr{j % 2}', writes=[xqTallC[j]])
    hbufP = [SB('hbufa', [128, IFF], F32, stack=esF1), SB('hbufb', [128, IFF], F32, stack=esF1),
             SB('hbufc', [128, IFF], F32, stack=esF1)]
    hsilP = [SB('hsila', [128, 512], F32, stack=esF1), SB('hsilb', [128, 512], F32, stack=esF1)]
    hq1 = SB('hq1', [128, IFF], BF16, stack=esF1)
    hqP = [hq1, hq1]

    def stageF1m(t):
        hbuf = hbufP[t % 3]
        for occ in range(6):
            w = 512 if occ < 5 else IFF - 5 * 512
            pg = PA if occ % 2 == 0 else PC
            pu = PB if occ % 2 == 0 else PD
            for kc in range(8):
                oprw(p, 'tensor', lambda T, kc=kc, occ=occ, w=w, pg=pg, t=t: T.matmul(
                    pg.ap[:, 0:w],
                    xqTallC[kc].ap[:, kc * S_core + t * 128:kc * S_core + (t + 1) * 128],
                    wgTC[kc].ap[:, kc * IFF + occ * 512: kc * IFF + occ * 512 + w],
                    start=(kc == 0), stop=(kc == 7)),
                    reads=[xqTallC[kc], wgTC[kc]], writes=[pg])
            for kc in range(8):
                oprw(p, 'tensor', lambda T, kc=kc, occ=occ, w=w, pu=pu, t=t: T.matmul(
                    pu.ap[:, 0:w],
                    xqTallC[kc].ap[:, kc * S_core + t * 128:kc * S_core + (t + 1) * 128],
                    wuTC[kc].ap[:, kc * IFF + occ * 512: kc * IFF + occ * 512 + w],
                    start=(kc == 0), stop=(kc == 7)),
                    reads=[xqTallC[kc], wuTC[kc]], writes=[pu])
            hsil = hsilP[occ % 2]
            oprw(p, 'scalar', lambda sc, occ=occ, w=w, pg=pg, hsil=hsil: sc.activation(
                hsil.ap[:, 0:w], pg.ap[:, 0:w], AF.Silu), reads=[pg], writes=[hsil])
            oprw(p, 'scalar', lambda sc, occ=occ, w=w, pu=pu, hbuf=hbuf: sc.copy(
                hbuf.ap[:, occ * 512:occ * 512 + w], pu.ap[:, 0:w]),
                reads=[pu], writes=[hbuf])
            oprw(p, 'gpsimd', lambda g, occ=occ, w=w, hsil=hsil, hbuf=hbuf: g.tensor_tensor(
                hbuf.ap[:, occ * 512:occ * 512 + w], hsil.ap[:, 0:w],
                hbuf.ap[:, occ * 512:occ * 512 + w], ALU.mult),
                reads=[hsil], writes=[hbuf])

    def stageF1q(t):
        hbuf = hbufP[t % 3]
        hq = hqP[t % 2]
        emit_quant_pre_core(hbuf, 0, 1024, QSET0)
        emit_quant_pre_pool(1024, QSET0, u_eng='vector')
        emit_quant_pre_core(hbuf, 1024, 1024, QSET1)
        emit_quant_pre_pool(1024, QSET1, u_eng='vector')
        emit_quant_main(hq, 0, 1024, QSET0)
        emit_quant_pre_core(hbuf, 2048, IFF - 2048, QSET0)
        emit_quant_pre_pool(IFF - 2048, QSET0, u_eng='vector')
        emit_quant_main(hq, 1024, 1024, QSET1)
        emit_quant_main(hq, 2048, IFF - 2048, QSET0)
        dmarw(p, 'scalar', lambda E, t=t, hq=hq: E.dma_start(
            d_hqspill[t * 128:(t + 1) * 128, :], hq.ap[:]), 'hqs', reads=[hq])

    for t in range(T_own):
        stageF1m(t)
        if t > 0:
            stageF1q(t - 1)
    stageF1q(T_own - 1)

    p.barrier()
    esF1.close()
    esQ.close()

    wdTh = esF2.enter_context(nc.sbuf_tensor('wdT', [128, 22 * C], BF16))
    wdTC = [Buf(wdTh, f'wdT{kc}') for kc in range(22)]
    for kc in range(22):
        dmarw(p, 'sync', lambda E, kc=kc: E.dma_start(
            wdTC[kc].ap[:, kc * C:(kc + 1) * C],
            d_wdq[:, kc * 128:(kc + 1) * 128], transpose=True), f'wdt{kc % 3}',
            writes=[wdTC[kc]])
    hqTallh = esF2.enter_context(nc.sbuf_tensor('hqTall', [128, 22 * S_core], BF16))
    hqTallC = [Buf(hqTallh, f'hqTall{j}') for j in range(22)]
    for j in range(22):
        dmarw(p, 'sync', lambda E, j=j: E.dma_start(
            hqTallC[j].ap[:, j * S_core:(j + 1) * S_core],
            d_hqspill[:, j * 128:(j + 1) * 128],
            transpose=True), f'hqr{j % 3}', writes=[hqTallC[j]])
    fbufP = [SB('fbufa', [128, C], F32, stack=esF2), SB('fbufb', [128, C], F32, stack=esF2)]
    xbF = [SB('xbFa', [128, C], F32, stack=esF2), SB('xbFb', [128, C], F32, stack=esF2)]

    def stageF2a(t):
        xfb = xbF[t % 2]
        dmarw(p, 'sync', lambda E, t=t, xfb=xfb: E.dma_start(
            xfb.ap[:], d_y1spill[t * 128:(t + 1) * 128, :]), f'xin{t % 2}',
            writes=[xfb])

    def stageF2b(t):
        fbuf = fbufP[t % 2]
        xfb = xbF[t % 2]
        pa, pb = (PA, PB) if t % 2 == 0 else (PC, PD)
        for kc in range(22):
            for hh in range(2):
                oprw(p, 'tensor', lambda T, kc=kc, hh=hh, t=t, pa=pa, pb=pb: T.matmul(
                    (pa if hh == 0 else pb).ap[:],
                    hqTallC[kc].ap[:, kc * S_core + t * 128:kc * S_core + (t + 1) * 128],
                    wdTC[kc].ap[:, kc * C + hh * 512: kc * C + (hh + 1) * 512],
                    start=(kc == 0), stop=(kc == 21)),
                    reads=[hqTallC[kc], wdTC[kc]], writes=[pa if hh == 0 else pb])
        oprw(p, 'vector', lambda v, xfb=xfb, fbuf=fbuf, pa=pa: v.tensor_tensor(
            fbuf.ap[:, 0:512], pa.ap[:], xfb.ap[:, 0:512], ALU.add),
            reads=[pa, xfb], writes=[fbuf])
        oprw(p, 'vector', lambda v, xfb=xfb, fbuf=fbuf, pb=pb: v.tensor_tensor(
            fbuf.ap[:, 512:1024], pb.ap[:], xfb.ap[:, 512:1024], ALU.add),
            reads=[pb, xfb], writes=[fbuf])
        dmarw(p, 'sync', lambda E, t=t, fbuf=fbuf: E.dma_start(
            d_y[t * 128:(t + 1) * 128, :], fbuf.ap[:]), 'yout', reads=[fbuf])

    stageF2a(0)
    for t in range(T_own):
        if t + 1 < T_own:
            stageF2a(t + 1)
        stageF2b(t)

    p.barrier()
    esF2.close()

    p.wait('sync', [('D', 'yout', p.dma_sems['yout'][1])])

    esB.close()
    with nc.Block() as block:
        p.finalize(block)
    es.close()
    return nc


def interleave_x(xb_full, q, S_core, cpb):
    S, C_ = xb_full.shape
    nb = S // 128
    T_own = S_core // 128
    blocks = xb_full.reshape(nb, 128, C_)
    own_ids = list(range(q * T_own, (q + 1) * T_own))
    own_set = set(own_ids)
    other_ids = [i for i in range(nb) if i not in own_set]
    order = []
    oi = ri = 0
    for t in range(nb):
        if t % cpb == 0:
            order.append(own_ids[oi])
            oi += 1
        else:
            order.append(other_ids[ri])
            ri += 1
    return np.ascontiguousarray(blocks[order].reshape(S, C_))


_NC_CACHE = {}


def _get_nc(S_batch, S_core):
    key = (S_batch, S_core)
    if key not in _NC_CACHE:
        _NC_CACHE[key] = build_nc(S_batch, S_core, S_batch // S_core)
    return _NC_CACHE[key]


def kernel(L0_attn_x, L0_attn_scale1, L0_attn_scale2, L0_attn_gamma,
           L0_attn_qkv_w, L0_attn_proj_w, L0_attn_proj_b, L0_attn_conv1_w,
           L0_attn_conv1_b, L0_attn_conv2_w, L0_attn_conv2_b, L0_ffn_w_gate,
           L0_ffn_w_up, L0_ffn_w_down):
    x = np.ascontiguousarray(np.asarray(L0_attn_x, np.float32))
    B, S, _ = x.shape
    n_cores = 8
    cpb = n_cores // B
    S_core = S // cpb
    nc = _get_nc(S, S_core)

    row = lambda a: np.ascontiguousarray(np.asarray(a, np.float32)).reshape(1, -1)
    wT = lambda w: np.ascontiguousarray(np.asarray(w, np.float32).T)
    nat = lambda w: np.ascontiguousarray(np.asarray(w, np.float32))
    common = {
        'scale1': row(L0_attn_scale1), 'scale2': row(L0_attn_scale2),
        'gamma': row(L0_attn_gamma), 'proj_b': row(L0_attn_proj_b),
        'conv1_b': row(L0_attn_conv1_b), 'conv2_b': row(L0_attn_conv2_b),
        'qkv_wT': wT(L0_attn_qkv_w), 'proj_wT': wT(L0_attn_proj_w),
        'conv1_wT': wT(L0_attn_conv1_w), 'conv2_wT': wT(L0_attn_conv2_w),
        'w_gate': nat(L0_ffn_w_gate), 'w_up': nat(L0_ffn_w_up),
        'w_down': nat(L0_ffn_w_down),
        'ident': np.eye(128, dtype=BF),
        'ones1': np.ones((1, 128), dtype=BF),
    }
    in_maps = []
    for c in range(n_cores):
        b = c // cpb
        q = c % cpb
        xb_full = x[b]
        own = np.ascontiguousarray(xb_full[q * S_core:(q + 1) * S_core])
        m = dict(common)
        m['x_all'] = interleave_x(xb_full, q, S_core, cpb)
        m['x_own'] = own
        in_maps.append(m)

    res = run_bass_kernel_spmd(nc, in_maps, core_ids=list(range(n_cores)))
    shards = [res.results[c]['y'] for c in range(n_cores)]
    out = np.empty((B, S, C), np.float32)
    for c in range(n_cores):
        b = c // cpb
        q = c % cpb
        out[b, q * S_core:(q + 1) * S_core] = shards[c]
    return out


# version 58
# speedup vs baseline: 1.0233x; 1.0159x over previous
import numpy as np
import ml_dtypes
from contextlib import ExitStack

import concourse.bass as bass
import concourse.mybir as mybir
from concourse.bass_utils import run_bass_kernel_spmd

F32 = mybir.dt.float32
BF16 = mybir.dt.bfloat16
AF = mybir.ActivationFunctionType
ALU = mybir.AluOpType
AX = mybir.AxisListType
BF = ml_dtypes.bfloat16

C = 1024
HID = 2048
IFF = 2816
NH = 16
HD = 64
QB = 16
C6 = 0.16666667163372039794921875
LN32 = 3.4657359027997265

ENGINES = ('sync', 'scalar', 'vector', 'tensor', 'gpsimd')


class Prog:

    def __init__(self, nc, es):
        self.nc = nc
        self.items = {e: [] for e in ENGINES}
        self.ecounts = {e: 0 for e in ENGINES}
        self.progsem = {e: es.enter_context(nc.semaphore(name=f'pg_{e}'))
                        for e in ENGINES}
        self.dma_sems = {}
        self.dma_chain = {}
        self.es = es

    def _dep_list(self, deps):
        out = []
        for d in deps:
            if d is None:
                continue
            if isinstance(d, (list, tuple)) and d and isinstance(d[0], (list, tuple)):
                out.extend(x for x in d if x is not None)
            else:
                out.append(d)
        return out

    def op(self, eng, emit, deps=()):
        idx = self.ecounts[eng]
        self.ecounts[eng] += 1
        self.items[eng].append({'emit': emit, 'deps': self._dep_list(deps),
                                'dma': None})
        return ('E', eng, idx)

    def dma(self, eng, emit, sem, deps=()):
        if sem not in self.dma_sems:
            self.dma_sems[sem] = [self.es.enter_context(
                self.nc.semaphore(name=f'dm_{sem}')), 0]
        self.dma_sems[sem][1] += 1
        k = self.dma_sems[sem][1]
        dl = self._dep_list(deps)
        prev = self.dma_chain.get(sem)
        if prev is not None:
            dl.append(prev)
        tok = ('D', sem, k)
        self.dma_chain[sem] = tok
        self.items[eng].append({'emit': emit, 'deps': dl, 'dma': (sem, k)})
        return tok

    def wait(self, eng, deps):
        self.items[eng].append({'emit': None, 'deps': self._dep_list(deps),
                                'dma': None})
        return None

    def barrier(self, engines=ENGINES):
        toks = []
        for e in engines:
            n = self.ecounts[e]
            if n:
                toks.append(('E', e, n - 1))
        for s, (h, cnt) in self.dma_sems.items():
            if cnt:
                toks.append(('D', s, cnt))
        for e in engines:
            self.wait(e, toks)
        return toks

    def finalize(self, block):
        referenced = set()
        for eng in ENGINES:
            for it in self.items[eng]:
                for d in it['deps']:
                    if d[0] == 'E' and d[1] == 'tensor' and eng != 'tensor':
                        referenced.add(d[2])
        tidx_sorted = sorted(referenced)
        import bisect
        tinc_of = {}
        for n_, i_ in enumerate(tidx_sorted):
            tinc_of[i_] = n_ + 1
        self._tensor_ref = referenced
        last_t = self.ecounts['tensor'] - 1
        total_tincs = len(tidx_sorted) + (0 if last_t in referenced else 1)

        def tensor_wait_count(i2):
            j = bisect.bisect_left(tidx_sorted, i2)
            if j >= len(tidx_sorted):
                return total_tincs
            return tinc_of[tidx_sorted[j]]

        for eng in ENGINES:
            items = self.items[eng]
            if not items:
                continue
            section = getattr(block, eng)

            def make_body(items=items, eng=eng):
                def body(E):
                    water = {}
                    tcount = 0
                    tseen = -1
                    for it in items:
                        for d in it['deps']:
                            kind = d[0]
                            if kind == 'E':
                                _, e2, i2 = d
                                if e2 == 'tensor':
                                    if eng == 'tensor':
                                        continue
                                    need = tensor_wait_count(i2)
                                else:
                                    need = i2 + 1
                                key = ('E', e2)
                                if water.get(key, 0) < need:
                                    E.wait_ge(self.progsem[e2], need)
                                    water[key] = need
                            else:
                                _, sname, k = d
                                need = 16 * k
                                key = ('D', sname)
                                if water.get(key, 0) < need:
                                    E.wait_ge(self.dma_sems[sname][0], need)
                                    water[key] = need
                        if it['emit'] is None:
                            continue
                        inst = it['emit'](E)
                        if it['dma'] is not None:
                            inst.then_inc(self.dma_sems[it['dma'][0]][0], 16)
                        elif eng == 'tensor':
                            tseen += 1
                            if tseen in self._tensor_ref or tseen == self.ecounts['tensor'] - 1:
                                inst.then_inc(self.progsem[eng], 1)
                                tcount += 1
                        else:
                            inst.then_inc(self.progsem[eng], 1)
                return body
            section(make_body())


class Buf:
    def __init__(self, ap, name=''):
        self.ap = ap
        self.name = name
        self.w = None
        self.readers = []

    def rdep(self):
        return [self.w] if self.w is not None else []

    def wdep(self):
        d = list(self.readers)
        if self.w is not None:
            d.append(self.w)
        return d

    def note_read(self, tok):
        if tok is not None:
            self.readers.append(tok)

    def note_write(self, tok):
        self.w = tok
        self.readers = []


def oprw(p, eng, emit, reads=(), writes=(), extra=()):
    deps = list(extra)
    for b in reads:
        deps += b.rdep()
    for b in writes:
        deps += b.wdep()
    tok = p.op(eng, emit, deps)
    for b in reads:
        b.note_read(tok)
    for b in writes:
        b.note_write(tok)
    return tok


def dmarw(p, eng, emit, sem, reads=(), writes=(), extra=()):
    deps = list(extra)
    for b in reads:
        deps += b.rdep()
    for b in writes:
        deps += b.wdep()
    tok = p.dma(eng, emit, sem, deps)
    for b in reads:
        b.note_read(tok)
    for b in writes:
        b.note_write(tok)
    return tok



def build_nc(S_batch, S_core, n_cores_per_batch):
    nc = bass.Bass('TRN2', target_bir_lowering=False)
    T_batch = S_batch // 128
    T_own = S_core // 128
    cpb = n_cores_per_batch

    d_x_all = nc.dram_tensor('x_all', [S_batch, C], F32, kind='ExternalInput').ap()
    d_x_own = nc.dram_tensor('x_own', [S_core, C], F32, kind='ExternalInput').ap()
    d_scale1 = nc.dram_tensor('scale1', [1, C], F32, kind='ExternalInput').ap()
    d_scale2 = nc.dram_tensor('scale2', [1, C], F32, kind='ExternalInput').ap()
    d_gamma = nc.dram_tensor('gamma', [1, C], F32, kind='ExternalInput').ap()
    d_pb = nc.dram_tensor('proj_b', [1, C], F32, kind='ExternalInput').ap()
    d_c1b = nc.dram_tensor('conv1_b', [1, HID], F32, kind='ExternalInput').ap()
    d_c2b = nc.dram_tensor('conv2_b', [1, C], F32, kind='ExternalInput').ap()
    d_qkvT = nc.dram_tensor('qkv_wT', [C, 3 * C], F32, kind='ExternalInput').ap()
    d_projT = nc.dram_tensor('proj_wT', [C, C], F32, kind='ExternalInput').ap()
    d_c1T = nc.dram_tensor('conv1_wT', [C, HID], F32, kind='ExternalInput').ap()
    d_c2T = nc.dram_tensor('conv2_wT', [HID, C], F32, kind='ExternalInput').ap()
    d_wg = nc.dram_tensor('w_gate', [IFF, C], F32, kind='ExternalInput').ap()
    d_wu = nc.dram_tensor('w_up', [IFF, C], F32, kind='ExternalInput').ap()
    d_wd = nc.dram_tensor('w_down', [C, IFF], F32, kind='ExternalInput').ap()
    d_ident = nc.dram_tensor('ident', [128, 128], BF16, kind='ExternalInput').ap()
    d_ones1 = nc.dram_tensor('ones1', [1, 128], BF16, kind='ExternalInput').ap()
    d_y = nc.dram_tensor('y', [S_core, C], F32, kind='ExternalOutput').ap()
    d_qspill = nc.dram_tensor('qspill', [S_core, C], BF16, kind='Internal').ap()
    d_hqspill = nc.dram_tensor('hqspill', [S_core, IFF], BF16, kind='Internal').ap()
    d_xqspill = nc.dram_tensor('xqspill', [S_core, C], BF16, kind='Internal').ap()
    d_y1spill = nc.dram_tensor('y1spill', [S_core, C], F32, kind='Internal').ap()
    d_xn2spill = nc.dram_tensor('xn2spill', [S_core, C], BF16, kind='Internal').ap()
    d_oaspill = nc.dram_tensor('oaspill', [S_core, C], F32, kind='Internal').ap()
    d_wgq = nc.dram_tensor('wgq', [IFF, C], BF16, kind='Internal').ap()
    d_wuq = nc.dram_tensor('wuq', [IFF, C], BF16, kind='Internal').ap()
    d_wdq = nc.dram_tensor('wdq', [C, IFF], BF16, kind='Internal').ap()

    es = ExitStack()
    esA = ExitStack()
    esQ = ExitStack()
    esW = ExitStack()
    esF1 = ExitStack()
    esF2 = ExitStack()
    esB = ExitStack()
    esB2 = ExitStack()

    def SB(name, shape, dt, stack=None):
        h = (stack or es).enter_context(nc.sbuf_tensor(name, shape, dt))
        return Buf(h, name)

    def PSUM(name, shape, dt):
        h = es.enter_context(nc.psum_tensor(name, shape, dt))
        return Buf(h, name)

    p = Prog(nc, es)

    ident = SB('identsb', [128, 128], BF16)
    ones1 = SB('ones1sb', [1, 128], BF16)
    s1col = SB('s1col', [128, 8], F32)
    s2col = SB('s2col', [128, 8], F32)
    gammar = SB('gammar', [128, C], F32)
    gpb = SB('gpb', [1, C], BF16)
    gc2b = SB('gc2b', [1, C], BF16)
    c1brow = SB('c1brow', [1, HID], BF16)
    stats = SB('stats', [128, 2 * T_own], F32)
    kv_b = SB('kv_b', [128, 8 * 130], BF16)

    PA = PSUM('PA', [128, 512], F32)
    PB = PSUM('PB', [128, 512], F32)
    PC = PSUM('PC', [128, 512], F32)
    PD = PSUM('PD', [128, 512], F32)
    KV1 = PSUM('KV1', [128, 512], F32)
    KV2 = PSUM('KV2', [128, 512], F32)
    KV3 = PSUM('KV3', [128, 512], F32)
    TR = PSUM('TR', [128, 1024], BF16)

    mvP = [SB('mva', [128, 16], F32), SB('mvb', [128, 16], F32), SB('mvc', [128, 16], F32)]

    q_sgn = SB('q_sgn', [128, 1024], F32, stack=esQ)
    q_ss = SB('q_ss', [128, 1024], F32, stack=esQ)
    q_lt = SB('q_lt', [128, 1024], BF16, stack=esQ)
    q_m1 = SB('q_m1', [128, 1024], BF16, stack=esQ)
    q_m2 = SB('q_m2', [128, 1024], BF16, stack=esQ)
    q_m3 = SB('q_m3', [128, 1024], BF16, stack=esQ)
    q_g2 = SB('q_g2', [128, 1024], BF16, stack=esQ)
    q_a = SB('q_a', [128, 1024], F32, stack=esQ)
    q2_a = SB('q2_a', [128, 1024], F32, stack=esQ)
    q_amax = SB('q_amax', [128, 64], F32, stack=esQ)
    q_scale = SB('q_scale', [128, 64], F32, stack=esQ)
    q_tmp = SB('q_tmp', [128, 64], F32, stack=esQ)
    q_hs = SB('q_hs', [128, 64], F32, stack=esQ)
    q_inv = SB('q_inv', [128, 64], F32, stack=esQ)
    q2_sgn = SB('q2_sgn', [128, 1024], F32, stack=esQ)
    q2_ss = SB('q2_ss', [128, 1024], F32, stack=esQ)
    q2_amax = SB('q2_amax', [128, 64], F32, stack=esQ)
    q2_scale = SB('q2_scale', [128, 64], F32, stack=esQ)
    q2_tmp = SB('q2_tmp', [128, 64], F32, stack=esQ)
    q2_hs = SB('q2_hs', [128, 64], F32, stack=esQ)
    q2_inv = SB('q2_inv', [128, 64], F32, stack=esQ)
    WJW = 1408
    wqld = [SB('wqld1', [128, WJW], F32, stack=esW),
            SB('wqld2', [128, WJW], F32, stack=esW)]
    wqout = SB('wqout', [128, WJW], BF16, stack=esW)
    rowtmp = SB('rowtmp', [1, C], F32, stack=esA)
    rowtmp2 = SB('rowtmp2', [1, C], F32, stack=esA)

    QSET0 = (q_sgn, q_amax, q_scale, q_tmp, q_hs, q_inv, q_ss, q_a)
    QSET1 = (q2_sgn, q2_amax, q2_scale, q2_tmp, q2_hs, q2_inv, q2_ss, q2_a)

    def emit_quant_pre_core(src, soff, W, qs, tok_extra=()):
        sgn, amax, scale, tmp, hs, inv, ssb, qa = qs
        NB = W // QB
        sa = src.ap[:, soff:soff + W]
        oprw(p, 'scalar', lambda sc, sa=sa, W=W, qa=qa: sc.activation(
            qa.ap[:, 0:W], sa, AF.Abs), reads=[src], writes=[qa], extra=tok_extra)
        oprw(p, 'scalar', lambda sc, sa=sa, W=W, sgn=sgn: sc.activation(
            sgn.ap[:, 0:W], sa, AF.Sign), reads=[src], writes=[sgn], extra=tok_extra)
        oprw(p, 'vector', lambda v, sa=sa, NB=NB, amax=amax: v.tensor_reduce(
            amax.ap[:, 0:NB].unsqueeze(2),
            sa.rearrange("p (a b) -> p a b", b=QB), AX.X, ALU.max,
            apply_absolute_value=True), reads=[src], writes=[amax], extra=tok_extra)
        oprw(p, 'vector', lambda v, NB=NB, amax=amax, scale=scale: v.tensor_scalar(
            scale.ap[:, 0:NB], amax.ap[:, 0:NB], C6, None, ALU.mult),
            reads=[amax], writes=[scale])
        oprw(p, 'vector', lambda v, NB=NB, amax=amax, scale=scale, tmp=tmp: v.scalar_tensor_tensor(
            tmp.ap[:, 0:NB], scale.ap[:, 0:NB], -6.0, amax.ap[:, 0:NB],
            ALU.mult, ALU.add), reads=[scale, amax], writes=[tmp])
        oprw(p, 'vector', lambda v, NB=NB, scale=scale, tmp=tmp: v.scalar_tensor_tensor(
            scale.ap[:, 0:NB], tmp.ap[:, 0:NB], C6, scale.ap[:, 0:NB],
            ALU.mult, ALU.add), reads=[tmp], writes=[scale])
        oprw(p, 'vector', lambda v, NB=NB, scale=scale: v.tensor_scalar(
            scale.ap[:, 0:NB], scale.ap[:, 0:NB], 1e-12, None, ALU.max),
            writes=[scale])
        oprw(p, 'vector', lambda v, NB=NB, scale=scale, hs=hs: v.tensor_scalar(
            hs.ap[:, 0:NB], scale.ap[:, 0:NB], 0.5, None, ALU.mult),
            reads=[scale], writes=[hs])
        oprw(p, 'vector', lambda v, NB=NB, scale=scale, inv=inv: v.reciprocal(
            inv.ap[:, 0:NB], scale.ap[:, 0:NB]), reads=[scale], writes=[inv])

    def emit_quant_pre_pool(W, qs, u_eng='gpsimd'):
        sgn, amax, scale, tmp, hs, inv, ssb, qa = qs
        NB = W // QB
        oprw(p, 'gpsimd', lambda g, NB=NB, W=W, sgn=sgn, hs=hs, ssb=ssb: g.tensor_tensor(
            ssb.ap[:, 0:W].rearrange("p (a b) -> p a b", b=QB),
            sgn.ap[:, 0:W].rearrange("p (a b) -> p a b", b=QB),
            hs.ap[:, 0:NB].unsqueeze(2).broadcast_to([128, NB, QB]),
            ALU.mult), reads=[sgn, hs], writes=[ssb])
        oprw(p, u_eng, lambda g, NB=NB, W=W, inv=inv, qa=qa: g.tensor_tensor(
            qa.ap[:, 0:W].rearrange("p (a b) -> p a b", b=QB),
            qa.ap[:, 0:W].rearrange("p (a b) -> p a b", b=QB),
            inv.ap[:, 0:NB].unsqueeze(2).broadcast_to([128, NB, QB]),
            ALU.mult), reads=[inv], writes=[qa])

    def emit_quant_pre(src, soff, W, qs, tok_extra=()):
        emit_quant_pre_core(src, soff, W, qs, tok_extra)
        emit_quant_pre_pool(W, qs)

    def emit_quant_main(dst, doff, W, qs):
        sgn, amax, scale, tmp, hs, inv, ssb, q_a = qs
        NB = W // QB
        oprw(p, 'vector', lambda g, W=W, q_a=q_a: g.tensor_scalar(
            q_lt.ap[:, 0:W], q_a.ap[:, 0:W], 0.25, None, ALU.is_gt),
            reads=[q_a], writes=[q_lt])
        oprw(p, 'vector', lambda g, W=W: g.tensor_scalar(
            q_m1.ap[:, 0:W], q_a.ap[:, 0:W], 0.75, None, ALU.is_gt),
            reads=[q_a], writes=[q_m1])
        oprw(p, 'vector', lambda g, W=W: g.tensor_scalar(
            q_m2.ap[:, 0:W], q_a.ap[:, 0:W], 1.25, None, ALU.is_gt),
            reads=[q_a], writes=[q_m2])
        oprw(p, 'vector', lambda g, W=W: g.tensor_scalar(
            q_m3.ap[:, 0:W], q_a.ap[:, 0:W], 1.75, None, ALU.is_gt),
            reads=[q_a], writes=[q_m3])
        oprw(p, 'vector', lambda g, W=W: g.tensor_tensor(
            q_lt.ap[:, 0:W], q_lt.ap[:, 0:W], q_m1.ap[:, 0:W], ALU.add),
            reads=[q_m1], writes=[q_lt])
        oprw(p, 'vector', lambda g, W=W: g.tensor_scalar(
            q_m1.ap[:, 0:W], q_a.ap[:, 0:W], 2.5, 2.0, ALU.is_gt, ALU.mult),
            reads=[q_a], writes=[q_m1])
        oprw(p, 'vector', lambda g, W=W: g.tensor_tensor(
            q_m2.ap[:, 0:W], q_m2.ap[:, 0:W], q_m3.ap[:, 0:W], ALU.add),
            reads=[q_m3], writes=[q_m2])
        oprw(p, 'vector', lambda g, W=W: g.tensor_scalar(
            q_m3.ap[:, 0:W], q_a.ap[:, 0:W], 3.5, 2.0, ALU.is_gt, ALU.mult),
            reads=[q_a], writes=[q_m3])
        oprw(p, 'vector', lambda g, W=W: g.tensor_tensor(
            q_lt.ap[:, 0:W], q_lt.ap[:, 0:W], q_m2.ap[:, 0:W], ALU.add),
            reads=[q_m2], writes=[q_lt])
        oprw(p, 'vector', lambda g, W=W: g.tensor_scalar(
            q_m2.ap[:, 0:W], q_a.ap[:, 0:W], 5.0, 4.0, ALU.is_gt, ALU.mult),
            reads=[q_a], writes=[q_m2])
        oprw(p, 'vector', lambda g, W=W: g.tensor_tensor(
            q_m1.ap[:, 0:W], q_m1.ap[:, 0:W], q_m3.ap[:, 0:W], ALU.add),
            reads=[q_m3], writes=[q_m1])
        oprw(p, 'vector', lambda g, W=W: g.tensor_tensor(
            q_m1.ap[:, 0:W], q_m1.ap[:, 0:W], q_m2.ap[:, 0:W], ALU.add),
            reads=[q_m2], writes=[q_m1])
        oprw(p, 'vector', lambda g, W=W: g.tensor_tensor(
            q_g2.ap[:, 0:W], q_lt.ap[:, 0:W], q_m1.ap[:, 0:W], ALU.add),
            reads=[q_lt, q_m1], writes=[q_g2])
        return oprw(p, 'vector', lambda v, W=W, doff=doff, ssb=ssb: v.tensor_tensor(
            dst.ap[:, doff:doff + W], q_g2.ap[:, 0:W], ssb.ap[:, 0:W],
            ALU.mult), reads=[q_g2, ssb], writes=[dst])

    def emit_quant(src, soff, dst, doff, W, tok_extra=(), eng='vector'):
        emit_quant_pre(src, soff, W, QSET0, tok_extra)
        return emit_quant_main(dst, doff, W, QSET0)

    dmarw(p, 'sync', lambda E: E.dma_start(ident.ap[:], d_ident[:]), 'win',
          writes=[ident])
    dmarw(p, 'sync', lambda E: E.dma_start(ones1.ap[:], d_ones1[:]), 'win',
          writes=[ones1])
    dmarw(p, 'sync', lambda E: E.dma_start(
        s1col.ap[:], d_scale1.rearrange("o (a p) -> (o p) a", p=128),
        allow_slow_non_contiguous=True), 'win', writes=[s1col])
    dmarw(p, 'sync', lambda E: E.dma_start(
        s2col.ap[:], d_scale2.rearrange("o (a p) -> (o p) a", p=128),
        allow_slow_non_contiguous=True), 'win', writes=[s2col])
    dmarw(p, 'sync', lambda E: E.dma_start(
        gammar.ap[:], d_gamma.partition_broadcast(128)), 'win', writes=[gammar])
    dmarw(p, 'sync', lambda E: E.dma_start(rowtmp.ap[:], d_pb[:]), 'win',
          writes=[rowtmp])
    oprw(p, 'vector', lambda v: v.tensor_tensor(
        gpb.ap[:], rowtmp.ap[:], gammar.ap[0:1, :], ALU.mult),
        reads=[rowtmp, gammar], writes=[gpb])
    dmarw(p, 'sync', lambda E: E.dma_start(rowtmp2.ap[:], d_c2b[:]), 'win',
          writes=[rowtmp2])
    oprw(p, 'vector', lambda v: v.tensor_tensor(
        gc2b.ap[:], rowtmp2.ap[:], gammar.ap[0:1, :], ALU.mult),
        reads=[rowtmp2, gammar], writes=[gc2b])
    dmarw(p, 'gpsimd', lambda E: E.dma_start(c1brow.ap[:], d_c1b[:]), 'gwin',
          writes=[c1brow])
    wTsh = esA.enter_context(nc.sbuf_tensor('wTs', [128, 8 * 3 * C], BF16))
    wTsC = [Buf(wTsh, f'wTs{kc}') for kc in range(8)]
    wld = SB('wld', [128, 2 * 3 * C], F32, stack=esA)
    for kc in range(8):
        half = kc % 2
        ld = dmarw(p, 'sync', lambda E, kc=kc, half=half: E.dma_start(
            wld.ap[:, half * 3 * C:(half + 1) * 3 * C],
            d_qkvT[kc * 128:(kc + 1) * 128, :]), 'wqkv', writes=[wld])
        oprw(p, 'vector', lambda v, kc=kc, half=half: v.tensor_scalar(
            wTsC[kc].ap[:, kc * 3 * C:(kc + 1) * 3 * C],
            wld.ap[:, half * 3 * C:(half + 1) * 3 * C],
            s1col.ap[:, kc:kc + 1], None, ALU.mult),
            reads=[wld, s1col], writes=[wTsC[kc]])

    xb = [SB('xb0', [128, C], F32, stack=esA), SB('xb1', [128, C], F32, stack=esA), SB('xb2', [128, C], F32, stack=esA)]
    xn1P = [SB('xn1a', [128, C], BF16, stack=esA), SB('xn1b', [128, C], BF16, stack=esA), SB('xn1c', [128, C], BF16, stack=esA)]
    xTP = [SB('xTa', [128, C], BF16, stack=esA), SB('xTb', [128, C], BF16, stack=esA)]
    ksq1 = SB('ksq1', [128, C], F32, stack=esA)
    ksqP = [ksq1, ksq1]
    kphiP = [SB('kphia', [128, C], BF16, stack=esA), SB('kphib', [128, C], BF16, stack=esA)]
    vextP = [SB('vexta', [128, NH * 65], BF16, stack=esA), SB('vextb', [128, NH * 65], BF16, stack=esA)]
    qphiP = [SB('qphia', [128, C], BF16, stack=esA), SB('qphib', [128, C], BF16, stack=esA)]

    for _ve in vextP:
        oprw(p, 'vector', lambda v, _ve=_ve: v.memset(_ve.ap[:], 1.0), writes=[_ve])
    oprw(p, 'vector', lambda v: v.memset(kv_b.ap[:], 0.0), writes=[kv_b])

    wjobs = []
    for r in range(IFF // 128):
        wjobs.append((d_wg, d_wgq, r, 0, C))
    for r in range(IFF // 128):
        wjobs.append((d_wu, d_wuq, r, 0, C))
    for r in range(C // 128):
        wjobs.append((d_wd, d_wdq, r, 0, WJW))
        wjobs.append((d_wd, d_wdq, r, WJW, WJW))
    wj_i = 0

    def emit_wjob_load(i):
        src, dst, r, co, W = wjobs[i]
        buf = wqld[i % 2]
        dmarw(p, 'gpsimd', lambda E, src=src, r=r, co=co, W=W, buf=buf: E.dma_start(
            buf.ap[:, 0:W], src[r * 128:(r + 1) * 128, co:co + W]),
            f'wstr{i % 2}', writes=[buf])

    def emit_wjob_prerest(i, qs=None):
        src, dst, r, co, W = wjobs[i]
        buf = wqld[i % 2]
        qs = qs or (QSET1 if i % 2 else QSET0)
        emit_quant_pre(buf, 0, min(1024, W), qs)

    def emit_wjob_pre(i, qs=None):
        emit_wjob_load(i)
        emit_wjob_prerest(i, qs)

    def emit_wjob_main(i, qs=None):
        src, dst, r, co, W = wjobs[i]
        buf = wqld[i % 2]
        qs = qs or (QSET1 if i % 2 else QSET0)
        emit_quant_main(wqout, 0, min(1024, W), qs)
        for off in range(1024, W, 1024):
            w = min(1024, W - off)
            emit_quant_pre(buf, off, w, qs)
            emit_quant_main(wqout, off, w, qs)
        dmarw(p, 'gpsimd', lambda E, dst=dst, r=r, co=co, W=W: E.dma_start(
            dst[r * 128:(r + 1) * 128, co:co + W], wqout.ap[:, 0:W]), 'wstw',
            reads=[wqout])

    def emit_wjob(i, qs=None):
        emit_wjob_pre(i, qs)
        emit_wjob_main(i, qs)

    n_w_per_tile = 1

    def stageA1a(t):
        x = xb[t % 3]
        mv = mvP[t % 3]
        dmarw(p, 'sync', lambda E, t=t, x=x: E.dma_start(
            x.ap[:], d_x_all[t * 128:(t + 1) * 128, :]), f'xinA{t % 3}',
            writes=[x])
        oprw(p, 'vector', lambda v, x=x, mv=mv: v.bn_stats(
            mv.ap[:, 0:6], x.ap[:, 0:512]), reads=[x], writes=[mv])
        oprw(p, 'vector', lambda v, x=x, mv=mv: v.bn_stats(
            mv.ap[:, 6:12], x.ap[:, 512:1024]), reads=[x], writes=[mv])
        oprw(p, 'vector', lambda v, mv=mv: v.bn_aggr(
            mv.ap[:, 12:14], mv.ap[:, 0:12]), writes=[mv])
        oprw(p, 'vector', lambda v, mv=mv: v.tensor_scalar(
            mv.ap[:, 14:15], mv.ap[:, 13:14], 1e-5, None, ALU.add), writes=[mv])
        oprw(p, 'vector', lambda v, mv=mv: v.tensor_copy(
            mv.ap[:, 15:16], mv.ap[:, 13:14]), writes=[mv])
        oprw(p, 'scalar', lambda sc, mv=mv: sc.activation(
            mv.ap[:, 14:16], mv.ap[:, 14:16], AF.Ln), writes=[mv])
        oprw(p, 'scalar', lambda sc, mv=mv: sc.activation(
            mv.ap[:, 14:16], mv.ap[:, 14:16], AF.Exp, scale=-0.5), writes=[mv])

    def stageA1b(t):
        x = xb[t % 3]
        xn1 = xn1P[t % 3]
        mv = mvP[t % 3]
        is_own = (t % cpb) == 0
        o = t // cpb
        if is_own:
            oprw(p, 'vector', lambda v, o=o, mv=mv: v.tensor_copy(
                stats.ap[:, 2 * o:2 * o + 1], mv.ap[:, 12:13]),
                reads=[mv], writes=[stats])
            oprw(p, 'vector', lambda v, o=o, mv=mv: v.tensor_copy(
                stats.ap[:, 2 * o + 1:2 * o + 2], mv.ap[:, 14:15]),
                reads=[mv], writes=[stats])
        oprw(p, 'vector', lambda v, x=x, mv=mv, xn1=xn1: v.tensor_scalar(
            xn1.ap[:], x.ap[:], mv.ap[:, 12:13], mv.ap[:, 15:16],
            ALU.subtract, ALU.mult), reads=[x, mv], writes=[xn1])

    def stageA2(t):
        xn1 = xn1P[t % 3]
        xT = xTP[t % 2]
        ksq = ksqP[t % 2]
        kphi = kphiP[t % 2]
        vext = vextP[t % 2]
        qphi = qphiP[t % 2]
        is_own = (t % cpb) == 0
        o = t // cpb
        for kc in range(8):
            oprw(p, 'tensor', lambda T, kc=kc, xn1=xn1: T.transpose(
                TR.ap[:, kc * 128:(kc + 1) * 128],
                xn1.ap[:, kc * 128:(kc + 1) * 128], ident.ap[:]),
                reads=[xn1, ident], writes=[TR])
        oprw(p, 'scalar' if t < 40 else 'vector',
             lambda e_, xT=xT: e_.copy(xT.ap[:], TR.ap[:]) if hasattr(e_, 'copy')
             else e_.tensor_copy(xT.ap[:], TR.ap[:]),
             reads=[TR], writes=[xT])
        for kc in range(8):
            for hh in range(2):
                oprw(p, 'tensor', lambda T, kc=kc, hh=hh, xT=xT: T.matmul(
                    (PA if hh == 0 else PB).ap[:],
                    xT.ap[:, kc * 128:(kc + 1) * 128],
                    wTsC[kc].ap[:, kc * 3 * C + C + hh * 512: kc * 3 * C + C + (hh + 1) * 512],
                    start=(kc == 0), stop=(kc == 7)),
                    reads=[xT, wTsC[kc]], writes=[PA if hh == 0 else PB])
        for kc in range(8):
            for hh in range(2):
                oprw(p, 'tensor', lambda T, kc=kc, hh=hh, xT=xT: T.matmul(
                    (PC if hh == 0 else PD).ap[:],
                    xT.ap[:, kc * 128:(kc + 1) * 128],
                    wTsC[kc].ap[:, kc * 3 * C + 2 * C + hh * 512: kc * 3 * C + 2 * C + (hh + 1) * 512],
                    start=(kc == 0), stop=(kc == 7)),
                    reads=[xT, wTsC[kc]], writes=[PC if hh == 0 else PD])
        oprw(p, 'scalar', lambda sc, ksq=ksq: sc.activation(
            ksq.ap[:, 0:512], PA.ap[:], AF.Square), reads=[PA], writes=[ksq])
        oprw(p, 'scalar', lambda sc, ksq=ksq: sc.activation(
            ksq.ap[:, 512:1024], PB.ap[:], AF.Square), reads=[PB], writes=[ksq])
        oprw(p, 'scalar', lambda sc, ksq=ksq, kphi=kphi: sc.activation(
            kphi.ap[:], ksq.ap[:], AF.Exp, scale=-0.5), reads=[ksq], writes=[kphi])
        if is_own:
            for kc in range(8):
                for hh in range(2):
                    oprw(p, 'tensor', lambda T, kc=kc, hh=hh, xT=xT: T.matmul(
                        (PA if hh == 0 else PB).ap[:],
                        xT.ap[:, kc * 128:(kc + 1) * 128],
                        wTsC[kc].ap[:, kc * 3 * C + hh * 512: kc * 3 * C + (hh + 1) * 512],
                        start=(kc == 0), stop=(kc == 7)),
                        reads=[xT, wTsC[kc]], writes=[PA if hh == 0 else PB])
        oprw(p, 'scalar', lambda sc, vext=vext: sc.copy(
            vext.ap[:].rearrange("p (h e) -> p h e", e=65)[:, 0:8, 0:64],
            PC.ap[:].rearrange("p (h e) -> p h e", e=64)[:, 0:8, :]),
            reads=[PC], writes=[vext])
        oprw(p, 'scalar', lambda sc, vext=vext: sc.copy(
            vext.ap[:].rearrange("p (h e) -> p h e", e=65)[:, 8:16, 0:64],
            PD.ap[:].rearrange("p (h e) -> p h e", e=64)[:, 0:8, :]),
            reads=[PD], writes=[vext])
        if is_own:
            oprw(p, 'scalar', lambda sc, ksq=ksq: sc.activation(
                ksq.ap[:, 0:512], PA.ap[:], AF.Square), reads=[PA], writes=[ksq])
            oprw(p, 'scalar', lambda sc, ksq=ksq: sc.activation(
                ksq.ap[:, 512:1024], PB.ap[:], AF.Square), reads=[PB], writes=[ksq])
            oprw(p, 'scalar', lambda sc, ksq=ksq, qphi=qphi: sc.activation(
                qphi.ap[:], ksq.ap[:], AF.Exp, scale=-0.5),
                reads=[ksq], writes=[qphi])
            dmarw(p, 'scalar', lambda E, o=o, qphi=qphi: E.dma_start(
                d_qspill[o * 128:(o + 1) * 128, :], qphi.ap[:]), 'qsp',
                reads=[qphi])

    def stageA2kv(t):
        kphi = kphiP[t % 2]
        vext = vextP[t % 2]
        for j in range(8):
            bank = (KV1, KV2, KV3)[j // 3]
            slot = j % 3
            st = (t == 0 and slot == 0)
            sp = (t == T_batch - 1 and j in (2, 5, 7))
            oprw(p, 'tensor', lambda T, j=j, bank=bank, slot=slot, st=st, sp=sp, kphi=kphi, vext=vext: T.matmul(
                bank.ap[:, slot * 130:(slot + 1) * 130],
                kphi.ap[:, j * 128:(j + 1) * 128],
                vext.ap[:, j * 130:(j + 1) * 130],
                start=st, stop=sp),
                reads=[kphi, vext], writes=[bank])

    stageA1a(0)
    stageA1b(0)
    stageA1a(1)
    w_pre = 0
    w_load = 0
    for t in range(T_batch):
        if t + 2 < T_batch:
            stageA1a(t + 2)
        if t + 1 < T_batch:
            stageA1b(t + 1)
        if (t < 34 or t % 2 == 0) and w_load < min(len(wjobs), 38):
            emit_wjob_load(w_load)
            w_load += 1
            if w_pre < w_load - 1:
                emit_wjob_prerest(w_pre)
                w_pre += 1
        stageA2(t)
        if t > 0:
            stageA2kv(t - 1)
        if wj_i < w_pre - 1:
            emit_wjob_main(wj_i)
            wj_i += 1
    stageA2kv(T_batch - 1)
    while w_pre < w_load:
        emit_wjob_prerest(w_pre)
        w_pre += 1
    while wj_i < w_pre:
        emit_wjob_main(wj_i)
        wj_i += 1

    for j in range(8):
        bank = (KV1, KV2, KV3)[j // 3]
        slot = j % 3
        oprw(p, 'vector', lambda v, j=j, bank=bank, slot=slot: v.tensor_copy(
            kv_b.ap[0:64, j * 130:j * 130 + 65],
            bank.ap[0:64, slot * 130:slot * 130 + 65]),
            reads=[bank], writes=[kv_b])
        oprw(p, 'vector', lambda v, j=j, bank=bank, slot=slot: v.tensor_copy(
            kv_b.ap[64:128, j * 130 + 65:j * 130 + 130],
            bank.ap[64:128, slot * 130 + 65:slot * 130 + 130]),
            reads=[bank], writes=[kv_b])

    p.barrier()
    esA.close()

    projTgh = esB.enter_context(nc.sbuf_tensor('projTg', [128, 8 * C], BF16))
    projTgC = [Buf(projTgh, f'projTg{kc}') for kc in range(8)]
    wldB = SB('wldB', [128, HID], F32, stack=esB)
    for kc in range(8):
        dmarw(p, 'sync', lambda E, kc=kc: E.dma_start(
            wldB.ap[:, 0:C],
            d_projT[kc * 128:(kc + 1) * 128, :]), 'wB', writes=[wldB])
        oprw(p, 'vector', lambda v, kc=kc: v.tensor_tensor(
            projTgC[kc].ap[:, kc * C:(kc + 1) * C],
            wldB.ap[:, 0:C], gammar.ap[:], ALU.mult),
            reads=[wldB, gammar], writes=[projTgC[kc]])

    qTallh = esB.enter_context(nc.sbuf_tensor('qTall', [128, 8 * S_core], BF16))
    qTallC = [Buf(qTallh, f'qTall{j}') for j in range(8)]
    for j in range(8):
        dmarw(p, 'sync', lambda E, j=j: E.dma_start(
            qTallC[j].ap[:, j * S_core:(j + 1) * S_core],
            d_qspill[:, j * 128:(j + 1) * 128],
            transpose=True), f'qrd{j % 2}', writes=[qTallC[j]])
    den = SB('den', [128, NH], F32, stack=esB)
    zz = SB('zz', [128, NH], F32, stack=esB)
    outzP = [SB('outza', [128, C], BF16, stack=esB), SB('outzb', [128, C], BF16, stack=esB)]
    ozTP = [SB('ozTa', [128, C], BF16, stack=esB), SB('ozTb', [128, C], BF16, stack=esB)]
    xn2P = [SB('xn2a', [128, C], BF16, stack=esB), SB('xn2b', [128, C], BF16, stack=esB)]
    xnBP = [SB('xnBa', [128, C], F32, stack=esB), SB('xnBb', [128, C], F32, stack=esB)]
    oaP = [SB('oaa', [128, C], F32, stack=esB), SB('oab', [128, C], F32, stack=esB)]
    xbB = [SB('xbBa', [128, C], F32, stack=esB), SB('xbBb', [128, C], F32, stack=esB)]
    x1P = [SB('x1a', [128, C], F32, stack=esB), SB('x1b', [128, C], F32, stack=esB)]
    xqBP = [SB('xqBa', [128, C], BF16, stack=esB), SB('xqBb', [128, C], BF16, stack=esB)]

    OUTB = (KV1, KV2, KV3)

    def stageBa(t):
        pr = t % 2
        x = xbB[pr]
        dmarw(p, 'sync', lambda E, t=t, x=x: E.dma_start(
            x.ap[:], d_x_own[t * 128:(t + 1) * 128, :]), f'xin{t % 2}',
            writes=[x])

    def stageBb(t):
        pr = t % 2
        x = xbB[pr]
        outz = outzP[pr]
        ozT = ozTP[pr]
        xn = xnBP[pr]
        x1 = x1P[pr]
        oacc = oaP[pr]
        xn2 = xn2P[pr]
        mv = mvP[pr]
        oprw(p, 'vector', lambda v, x=x, xn=xn, t=t: v.tensor_scalar(
            xn.ap[:], x.ap[:], stats.ap[:, 2 * t:2 * t + 1],
            stats.ap[:, 2 * t + 1:2 * t + 2], ALU.subtract, ALU.mult),
            reads=[x, stats], writes=[xn])
        for j in range(8):
            bank = OUTB[j // 3]
            slot = j % 3
            oprw(p, 'tensor', lambda T, j=j, bank=bank, slot=slot, t=t: T.matmul(
                bank.ap[:, slot * 130:(slot + 1) * 130],
                qTallC[j].ap[:, j * S_core + t * 128:j * S_core + (t + 1) * 128],
                kv_b.ap[:, j * 130:(j + 1) * 130],
                start=True, stop=True),
                reads=[qTallC[j], kv_b], writes=[bank])
        oprw(p, 'vector', lambda v: v.tensor_copy(
            den.ap[:, 0:6],
            OUTB[0].ap[:, 0:390].rearrange("p (h e) -> p h e", e=65)[:, :, 64]),
            reads=[OUTB[0]], writes=[den])
        oprw(p, 'vector', lambda v: v.tensor_copy(
            den.ap[:, 6:12],
            OUTB[1].ap[:, 0:390].rearrange("p (h e) -> p h e", e=65)[:, :, 64]),
            reads=[OUTB[1]], writes=[den])
        oprw(p, 'vector', lambda v: v.tensor_copy(
            den.ap[:, 12:16],
            OUTB[2].ap[:, 0:260].rearrange("p (h e) -> p h e", e=65)[:, :, 64]),
            reads=[OUTB[2]], writes=[den])
        oprw(p, 'vector', lambda v: v.tensor_scalar(
            den.ap[:], den.ap[:], 1e-6, None, ALU.add), writes=[den])
        oprw(p, 'vector', lambda v: v.reciprocal(zz.ap[:], den.ap[:]),
             reads=[den], writes=[zz])
        for bi, (b0, h0, nh) in enumerate(((OUTB[0], 0, 6), (OUTB[1], 6, 6),
                                           (OUTB[2], 12, 4))):
            oprw(p, 'vector', lambda v, b0=b0, h0=h0, nh=nh, outz=outz: v.tensor_tensor(
                outz.ap[:, h0 * HD:(h0 + nh) * HD].rearrange(
                    "p (h e) -> p h e", e=HD),
                b0.ap[:, 0:nh * 65].rearrange("p (h e) -> p h e", e=65)[:, :, 0:HD],
                zz.ap[:, h0:h0 + nh].unsqueeze(2).broadcast_to([128, nh, HD]),
                ALU.mult), reads=[b0, zz], writes=[outz])
        for kc in range(8):
            oprw(p, 'tensor', lambda T, kc=kc, outz=outz: T.transpose(
                TR.ap[:, kc * 128:(kc + 1) * 128],
                outz.ap[:, kc * 128:(kc + 1) * 128], ident.ap[:]),
                reads=[outz, ident], writes=[TR])
        oprw(p, 'scalar', lambda sc, ozT=ozT: sc.copy(ozT.ap[:], TR.ap[:]),
             reads=[TR], writes=[ozT])
        prA, prB = (PA, PB) if t % 2 == 0 else (PC, PD)
        for hh in range(2):
            pbk = prA if hh == 0 else prB
            for kc in range(8):
                oprw(p, 'tensor', lambda T, kc=kc, hh=hh, pbk=pbk, ozT=ozT: T.matmul(
                    pbk.ap[:], ozT.ap[:, kc * 128:(kc + 1) * 128],
                    projTgC[kc].ap[:, kc * C + hh * 512: kc * C + (hh + 1) * 512],
                    start=(kc == 0), stop=False),
                    reads=[ozT, projTgC[kc]], writes=[pbk])
            oprw(p, 'tensor', lambda T, hh=hh, pbk=pbk: T.matmul(
                pbk.ap[:], ones1.ap[:],
                gpb.ap[:, hh * 512:(hh + 1) * 512], start=False, stop=True),
                reads=[ones1, gpb], writes=[pbk])
    def stageBb2(t):
        pr = t % 2
        x = xbB[pr]
        xn = xnBP[pr]
        x1 = x1P[pr]
        oacc = oaP[pr]
        xn2 = xn2P[pr]
        mv = mvP[pr]
        prA, prB = (PA, PB) if t % 2 == 0 else (PC, PD)
        for hh, pbk in ((0, prA), (1, prB)):
            oprw(p, 'vector', lambda v, hh=hh, pbk=pbk, x1=x1, xn=xn: v.tensor_tensor(
                x1.ap[:, hh * 512:(hh + 1) * 512], pbk.ap[:],
                xn.ap[:, hh * 512:(hh + 1) * 512], ALU.add),
                reads=[pbk, xn], writes=[x1])
            oprw(p, 'vector', lambda v, hh=hh, oacc=oacc, x1=x1, x=x: v.tensor_tensor(
                oacc.ap[:, hh * 512:(hh + 1) * 512],
                x1.ap[:, hh * 512:(hh + 1) * 512],
                x.ap[:, hh * 512:(hh + 1) * 512], ALU.add),
                reads=[x1, x], writes=[oacc])
        dmarw(p, 'scalar', lambda E, t=t, oacc=oacc: E.dma_start(
            d_oaspill[t * 128:(t + 1) * 128, :], oacc.ap[:]), 'oas', reads=[oacc])
        oprw(p, 'vector', lambda v, mv=mv, x1=x1: v.bn_stats(
            mv.ap[:, 0:6], x1.ap[:, 0:512]), reads=[x1], writes=[mv])
        oprw(p, 'vector', lambda v, mv=mv, x1=x1: v.bn_stats(
            mv.ap[:, 6:12], x1.ap[:, 512:1024]), reads=[x1], writes=[mv])
        oprw(p, 'vector', lambda v, mv=mv: v.bn_aggr(
            mv.ap[:, 12:14], mv.ap[:, 0:12]), writes=[mv])
        oprw(p, 'vector', lambda v, mv=mv: v.tensor_scalar(
            mv.ap[:, 14:15], mv.ap[:, 12:13], mv.ap[:, 12:13], None,
            ALU.mult), writes=[mv])
        oprw(p, 'vector', lambda v, mv=mv: v.tensor_tensor(
            mv.ap[:, 14:15], mv.ap[:, 14:15], mv.ap[:, 13:14], ALU.add),
            writes=[mv])
        oprw(p, 'scalar', lambda sc, mv=mv: sc.activation(
            mv.ap[:, 14:15], mv.ap[:, 14:15], AF.Ln, scale=1024.0), writes=[mv])
        oprw(p, 'scalar', lambda sc, mv=mv: sc.activation(
            mv.ap[:, 14:15], mv.ap[:, 14:15], AF.Exp, scale=-0.5), writes=[mv])
        oprw(p, 'vector', lambda v, mv=mv, x1=x1, xn2=xn2: v.tensor_scalar(
            xn2.ap[:], x1.ap[:], mv.ap[:, 14:15], 32.0, ALU.mult, ALU.mult),
            reads=[x1, mv], writes=[xn2])
        dmarw(p, 'scalar', lambda E, t=t, xn2=xn2: E.dma_start(
            d_xn2spill[t * 128:(t + 1) * 128, :], xn2.ap[:]), 'xn2s', reads=[xn2])
        if t > 0:
            xqp = xqBP[(t - 1) % 2]
            emit_quant_main(xqp, 0, C, QSET0)
            dmarw(p, 'scalar', lambda E, t=t, xqp=xqp: E.dma_start(
                d_xqspill[(t - 1) * 128:t * 128, :], xqp.ap[:]), 'xqs',
                reads=[xqp])
        emit_quant_pre(xn, 0, C, QSET0)

    stageBa(0)
    for t in range(T_own):
        stageBb(t)
        if t > 0:
            stageBb2(t - 1)
        if t + 1 < T_own:
            stageBa(t + 1)
        if t % 3 == 2 and wj_i < min(len(wjobs), 40):
            emit_wjob(wj_i, QSET1)
            wj_i += 1
    stageBb2(T_own - 1)
    xqp = xqBP[(T_own - 1) % 2]
    emit_quant_main(xqp, 0, C, QSET0)
    dmarw(p, 'scalar', lambda E, xqp=xqp: E.dma_start(
        d_xqspill[(T_own - 1) * 128:T_own * 128, :], xqp.ap[:]), 'xqs',
        reads=[xqp])

    p.barrier()
    esB.close()

    c1Tsh = esB2.enter_context(nc.sbuf_tensor('c1Ts', [128, 8 * HID], BF16))
    c1TsC = [Buf(c1Tsh, f'c1Ts{kc}') for kc in range(8)]
    c2Tgh = esB2.enter_context(nc.sbuf_tensor('c2Tg', [128, 16 * C], BF16))
    c2TgC = [Buf(c2Tgh, f'c2Tg{kc}') for kc in range(16)]
    wldB2 = SB('wldB2', [128, C], F32, stack=esB2)
    for kc in range(8):
        for hf in range(2):
            dmarw(p, 'sync', lambda E, kc=kc, hf=hf: E.dma_start(
                wldB2.ap[:],
                d_c1T[kc * 128:(kc + 1) * 128, hf * C:(hf + 1) * C]), 'wB',
                writes=[wldB2])
            oprw(p, 'vector', lambda v, kc=kc, hf=hf: v.tensor_scalar(
                c1TsC[kc].ap[:, kc * HID + hf * C:kc * HID + (hf + 1) * C],
                wldB2.ap[:],
                s2col.ap[:, kc:kc + 1], None, ALU.mult),
                reads=[wldB2, s2col], writes=[c1TsC[kc]])
    for kc in range(16):
        dmarw(p, 'sync', lambda E, kc=kc: E.dma_start(
            wldB2.ap[:, 0:C],
            d_c2T[kc * 128:(kc + 1) * 128, :]), 'wB', writes=[wldB2])
        oprw(p, 'vector', lambda v, kc=kc: v.tensor_tensor(
            c2TgC[kc].ap[:, kc * C:(kc + 1) * C],
            wldB2.ap[:, 0:C], gammar.ap[:], ALU.mult),
            reads=[wldB2, gammar], writes=[c2TgC[kc]])

    x2Tallh = esB2.enter_context(nc.sbuf_tensor('x2Tall', [128, 8 * S_core], BF16))
    x2TallC = [Buf(x2Tallh, f'x2Tall{j}') for j in range(8)]
    for j in range(8):
        dmarw(p, 'sync', lambda E, j=j: E.dma_start(
            x2TallC[j].ap[:, j * S_core:(j + 1) * S_core],
            d_xn2spill[:, j * 128:(j + 1) * 128],
            transpose=True), f'xn2r{j % 2}', writes=[x2TallC[j]])
    hddP = [SB('hdda', [128, HID], BF16, stack=esB2), SB('hddb', [128, HID], BF16, stack=esB2)]
    hdTP = [SB('hdTa', [128, HID], BF16, stack=esB2), SB('hdTb', [128, HID], BF16, stack=esB2)]
    oaiP = [SB('oaia', [128, C], F32, stack=esB2), SB('oaib', [128, C], F32, stack=esB2)]

    def stageB2oai(t):
        oai = oaiP[t % 2]
        dmarw(p, 'sync', lambda E, t=t, oai=oai: E.dma_start(
            oai.ap[:], d_oaspill[t * 128:(t + 1) * 128, :]), 'oar', writes=[oai])

    def stageB2m(t):
        pr = t % 2
        hdd = hddP[pr]
        C1B = (PA, PB, PC, PD)
        for hh in range(4):
            pbk = C1B[hh]
            for kc in range(8):
                oprw(p, 'tensor', lambda T, kc=kc, hh=hh, pbk=pbk, t=t: T.matmul(
                    pbk.ap[:], x2TallC[kc].ap[:, kc * S_core + t * 128:kc * S_core + (t + 1) * 128],
                    c1TsC[kc].ap[:, kc * HID + hh * 512: kc * HID + (hh + 1) * 512],
                    start=(kc == 0), stop=False),
                    reads=[x2TallC[kc], c1TsC[kc]], writes=[pbk])
            oprw(p, 'tensor', lambda T, hh=hh, pbk=pbk: T.matmul(
                pbk.ap[:], ones1.ap[:],
                c1brow.ap[:, hh * 512:(hh + 1) * 512], start=False, stop=True),
                reads=[ones1, c1brow], writes=[pbk])
        for hh in range(4):
            pbk = C1B[hh]
            oprw(p, 'scalar', lambda sc, hh=hh, pbk=pbk, hdd=hdd: sc.activation(
                hdd.ap[:, hh * 512:(hh + 1) * 512], pbk.ap[:], AF.Gelu),
                reads=[pbk], writes=[hdd])

    def stageB2c(t):
        pr = t % 2
        hdd = hddP[pr]
        hdT = hdTP[pr]
        oai = oaiP[pr]
        yb = oai
        for rnd in range(2):
            for kc in range(rnd * 8, rnd * 8 + 8):
                oprw(p, 'tensor', lambda T, kc=kc, rnd=rnd, hdd=hdd: T.transpose(
                    TR.ap[:, (kc - rnd * 8) * 128:(kc - rnd * 8 + 1) * 128],
                    hdd.ap[:, kc * 128:(kc + 1) * 128], ident.ap[:]),
                    reads=[hdd, ident], writes=[TR])
            oprw(p, 'scalar', lambda sc, rnd=rnd, hdT=hdT: sc.copy(
                hdT.ap[:, rnd * 1024:(rnd + 1) * 1024], TR.ap[:]),
                reads=[TR], writes=[hdT])
        for hh in range(2):
            pbk = KV1 if hh == 0 else KV2
            for kc in range(16):
                oprw(p, 'tensor', lambda T, kc=kc, hh=hh, pbk=pbk, hdT=hdT: T.matmul(
                    pbk.ap[:], hdT.ap[:, kc * 128:(kc + 1) * 128],
                    c2TgC[kc].ap[:, kc * C + hh * 512: kc * C + (hh + 1) * 512],
                    start=(kc == 0), stop=False),
                    reads=[hdT, c2TgC[kc]], writes=[pbk])
            oprw(p, 'tensor', lambda T, hh=hh, pbk=pbk: T.matmul(
                pbk.ap[:], ones1.ap[:],
                gc2b.ap[:, hh * 512:(hh + 1) * 512], start=False, stop=True),
                reads=[ones1, gc2b], writes=[pbk])
        for hh, pbk in ((0, KV1), (1, KV2)):
            oprw(p, 'vector', lambda v, hh=hh, pbk=pbk, yb=yb, oai=oai: v.tensor_tensor(
                yb.ap[:, hh * 512:(hh + 1) * 512], pbk.ap[:],
                oai.ap[:, hh * 512:(hh + 1) * 512], ALU.add),
                reads=[pbk, oai], writes=[yb])
        dmarw(p, 'sync', lambda E, t=t, yb=yb: E.dma_start(
            d_y1spill[t * 128:(t + 1) * 128, :], yb.ap[:]), 'y1s', reads=[yb])

    stageB2oai(0)
    for t in range(T_own):
        stageB2m(t)
        if t > 0:
            stageB2c(t - 1)
        if t + 1 < T_own:
            stageB2oai(t + 1)
        if wj_i < len(wjobs):
            emit_wjob(wj_i)
            wj_i += 1
    stageB2c(T_own - 1)

    while wj_i < len(wjobs):
        emit_wjob(wj_i)
        wj_i += 1

    p.barrier()
    esB2.close()
    esW.close()


    wgTh = esF1.enter_context(nc.sbuf_tensor('wgT', [128, 8 * IFF], BF16))
    wgTC = [Buf(wgTh, f'wgT{kc}') for kc in range(8)]
    wuTh = esF1.enter_context(nc.sbuf_tensor('wuT', [128, 8 * IFF], BF16))
    wuTC = [Buf(wuTh, f'wuT{kc}') for kc in range(8)]
    for kc in range(8):
        dmarw(p, 'sync', lambda E, kc=kc: E.dma_start(
            wgTC[kc].ap[:, kc * IFF:(kc + 1) * IFF],
            d_wgq[:, kc * 128:(kc + 1) * 128], transpose=True), f'wgt{kc % 2}',
            writes=[wgTC[kc]])
        dmarw(p, 'sync', lambda E, kc=kc: E.dma_start(
            wuTC[kc].ap[:, kc * IFF:(kc + 1) * IFF],
            d_wuq[:, kc * 128:(kc + 1) * 128], transpose=True), f'wut{kc % 2}',
            writes=[wuTC[kc]])

    xqTallh = esF1.enter_context(nc.sbuf_tensor('xqTall', [128, 8 * S_core], BF16))
    xqTallC = [Buf(xqTallh, f'xqTall{j}') for j in range(8)]
    for j in range(8):
        dmarw(p, 'sync', lambda E, j=j: E.dma_start(
            xqTallC[j].ap[:, j * S_core:(j + 1) * S_core],
            d_xqspill[:, j * 128:(j + 1) * 128],
            transpose=True), f'xqr{j % 2}', writes=[xqTallC[j]])
    hbufP = [SB('hbufa', [128, IFF], F32, stack=esF1), SB('hbufb', [128, IFF], F32, stack=esF1)]
    hsilP = [SB('hsila', [128, 512], F32, stack=esF1), SB('hsilb', [128, 512], F32, stack=esF1)]
    hq1h = esF1.enter_context(nc.sbuf_tensor('hq1', [128, IFF], BF16))
    hqC = [Buf(hq1h, f'hq{c}') for c in range(3)]

    def stageF1m(t):
        hbuf = hbufP[t % 2]
        for occ in range(6):
            w = 512 if occ < 5 else IFF - 5 * 512
            pg = PA if occ % 2 == 0 else PC
            pu = PB if occ % 2 == 0 else PD
            for kc in range(8):
                oprw(p, 'tensor', lambda T, kc=kc, occ=occ, w=w, pg=pg, t=t: T.matmul(
                    pg.ap[:, 0:w],
                    xqTallC[kc].ap[:, kc * S_core + t * 128:kc * S_core + (t + 1) * 128],
                    wgTC[kc].ap[:, kc * IFF + occ * 512: kc * IFF + occ * 512 + w],
                    start=(kc == 0), stop=(kc == 7)),
                    reads=[xqTallC[kc], wgTC[kc]], writes=[pg])
            for kc in range(8):
                oprw(p, 'tensor', lambda T, kc=kc, occ=occ, w=w, pu=pu, t=t: T.matmul(
                    pu.ap[:, 0:w],
                    xqTallC[kc].ap[:, kc * S_core + t * 128:kc * S_core + (t + 1) * 128],
                    wuTC[kc].ap[:, kc * IFF + occ * 512: kc * IFF + occ * 512 + w],
                    start=(kc == 0), stop=(kc == 7)),
                    reads=[xqTallC[kc], wuTC[kc]], writes=[pu])
            hsil = hsilP[occ % 2]
            oprw(p, 'scalar', lambda sc, occ=occ, w=w, pg=pg, hsil=hsil: sc.activation(
                hsil.ap[:, 0:w], pg.ap[:, 0:w], AF.Silu), reads=[pg], writes=[hsil])
            oprw(p, 'scalar', lambda sc, occ=occ, w=w, pu=pu, hbuf=hbuf: sc.copy(
                hbuf.ap[:, occ * 512:occ * 512 + w], pu.ap[:, 0:w]),
                reads=[pu], writes=[hbuf])
            oprw(p, 'gpsimd', lambda g, occ=occ, w=w, hsil=hsil, hbuf=hbuf: g.tensor_tensor(
                hbuf.ap[:, occ * 512:occ * 512 + w], hsil.ap[:, 0:w],
                hbuf.ap[:, occ * 512:occ * 512 + w], ALU.mult),
                reads=[hsil], writes=[hbuf])

    def stageF1q(t):
        hbuf = hbufP[t % 2]
        emit_quant_pre_core(hbuf, 0, 1024, QSET0)
        emit_quant_pre_pool(1024, QSET0, u_eng='vector')
        emit_quant_pre_core(hbuf, 1024, 1024, QSET1)
        emit_quant_pre_pool(1024, QSET1, u_eng='vector')
        emit_quant_main(hqC[0], 0, 1024, QSET0)
        dmarw(p, 'scalar', lambda E, t=t: E.dma_start(
            d_hqspill[t * 128:(t + 1) * 128, 0:1024], hq1h[:, 0:1024]),
            'hqs', reads=[hqC[0]])
        emit_quant_pre_core(hbuf, 2048, IFF - 2048, QSET0)
        emit_quant_pre_pool(IFF - 2048, QSET0, u_eng='vector')
        emit_quant_main(hqC[1], 1024, 1024, QSET1)
        dmarw(p, 'scalar', lambda E, t=t: E.dma_start(
            d_hqspill[t * 128:(t + 1) * 128, 1024:2048], hq1h[:, 1024:2048]),
            'hqs', reads=[hqC[1]])
        emit_quant_main(hqC[2], 2048, IFF - 2048, QSET0)
        dmarw(p, 'scalar', lambda E, t=t: E.dma_start(
            d_hqspill[t * 128:(t + 1) * 128, 2048:IFF], hq1h[:, 2048:IFF]),
            'hqs', reads=[hqC[2]])

    for t in range(T_own):
        stageF1m(t)
        if t > 0:
            stageF1q(t - 1)
    stageF1q(T_own - 1)

    p.barrier()
    esF1.close()
    esQ.close()

    wdTh = esF2.enter_context(nc.sbuf_tensor('wdT', [128, 22 * C], BF16))
    wdTC = [Buf(wdTh, f'wdT{kc}') for kc in range(22)]
    for kc in range(22):
        dmarw(p, 'sync', lambda E, kc=kc: E.dma_start(
            wdTC[kc].ap[:, kc * C:(kc + 1) * C],
            d_wdq[:, kc * 128:(kc + 1) * 128], transpose=True), f'wdt{kc % 3}',
            writes=[wdTC[kc]])
    hqTallh = esF2.enter_context(nc.sbuf_tensor('hqTall', [128, 22 * S_core], BF16))
    hqTallC = [Buf(hqTallh, f'hqTall{j}') for j in range(22)]
    for j in range(22):
        dmarw(p, 'sync', lambda E, j=j: E.dma_start(
            hqTallC[j].ap[:, j * S_core:(j + 1) * S_core],
            d_hqspill[:, j * 128:(j + 1) * 128],
            transpose=True), f'hqr{j % 3}', writes=[hqTallC[j]])
    fbufP = [SB('fbufa', [128, C], F32, stack=esF2), SB('fbufb', [128, C], F32, stack=esF2)]
    xbF = [SB('xbFa', [128, C], F32, stack=esF2), SB('xbFb', [128, C], F32, stack=esF2)]

    def stageF2a(t):
        xfb = xbF[t % 2]
        dmarw(p, 'sync', lambda E, t=t, xfb=xfb: E.dma_start(
            xfb.ap[:], d_y1spill[t * 128:(t + 1) * 128, :]), f'xin{t % 2}',
            writes=[xfb])

    def stageF2b(t):
        fbuf = fbufP[t % 2]
        xfb = xbF[t % 2]
        pa, pb = (PA, PB) if t % 2 == 0 else (PC, PD)
        for kc in range(22):
            for hh in range(2):
                oprw(p, 'tensor', lambda T, kc=kc, hh=hh, t=t, pa=pa, pb=pb: T.matmul(
                    (pa if hh == 0 else pb).ap[:],
                    hqTallC[kc].ap[:, kc * S_core + t * 128:kc * S_core + (t + 1) * 128],
                    wdTC[kc].ap[:, kc * C + hh * 512: kc * C + (hh + 1) * 512],
                    start=(kc == 0), stop=(kc == 21)),
                    reads=[hqTallC[kc], wdTC[kc]], writes=[pa if hh == 0 else pb])
        oprw(p, 'vector', lambda v, xfb=xfb, fbuf=fbuf, pa=pa: v.tensor_tensor(
            fbuf.ap[:, 0:512], pa.ap[:], xfb.ap[:, 0:512], ALU.add),
            reads=[pa, xfb], writes=[fbuf])
        oprw(p, 'vector', lambda v, xfb=xfb, fbuf=fbuf, pb=pb: v.tensor_tensor(
            fbuf.ap[:, 512:1024], pb.ap[:], xfb.ap[:, 512:1024], ALU.add),
            reads=[pb, xfb], writes=[fbuf])
        dmarw(p, 'sync', lambda E, t=t, fbuf=fbuf: E.dma_start(
            d_y[t * 128:(t + 1) * 128, :], fbuf.ap[:]), 'yout', reads=[fbuf])

    stageF2a(0)
    for t in range(T_own):
        if t + 1 < T_own:
            stageF2a(t + 1)
        stageF2b(t)

    p.barrier()
    esF2.close()

    p.wait('sync', [('D', 'yout', p.dma_sems['yout'][1])])

    esB.close()
    with nc.Block() as block:
        p.finalize(block)
    es.close()
    return nc


def interleave_x(xb_full, q, S_core, cpb):
    S, C_ = xb_full.shape
    nb = S // 128
    T_own = S_core // 128
    blocks = xb_full.reshape(nb, 128, C_)
    own_ids = list(range(q * T_own, (q + 1) * T_own))
    own_set = set(own_ids)
    other_ids = [i for i in range(nb) if i not in own_set]
    order = []
    oi = ri = 0
    for t in range(nb):
        if t % cpb == 0:
            order.append(own_ids[oi])
            oi += 1
        else:
            order.append(other_ids[ri])
            ri += 1
    return np.ascontiguousarray(blocks[order].reshape(S, C_))


_NC_CACHE = {}


def _get_nc(S_batch, S_core):
    key = (S_batch, S_core)
    if key not in _NC_CACHE:
        _NC_CACHE[key] = build_nc(S_batch, S_core, S_batch // S_core)
    return _NC_CACHE[key]


def kernel(L0_attn_x, L0_attn_scale1, L0_attn_scale2, L0_attn_gamma,
           L0_attn_qkv_w, L0_attn_proj_w, L0_attn_proj_b, L0_attn_conv1_w,
           L0_attn_conv1_b, L0_attn_conv2_w, L0_attn_conv2_b, L0_ffn_w_gate,
           L0_ffn_w_up, L0_ffn_w_down):
    x = np.ascontiguousarray(np.asarray(L0_attn_x, np.float32))
    B, S, _ = x.shape
    n_cores = 8
    cpb = n_cores // B
    S_core = S // cpb
    nc = _get_nc(S, S_core)

    row = lambda a: np.ascontiguousarray(np.asarray(a, np.float32)).reshape(1, -1)
    wT = lambda w: np.ascontiguousarray(np.asarray(w, np.float32).T)
    nat = lambda w: np.ascontiguousarray(np.asarray(w, np.float32))
    common = {
        'scale1': row(L0_attn_scale1), 'scale2': row(L0_attn_scale2),
        'gamma': row(L0_attn_gamma), 'proj_b': row(L0_attn_proj_b),
        'conv1_b': row(L0_attn_conv1_b), 'conv2_b': row(L0_attn_conv2_b),
        'qkv_wT': wT(L0_attn_qkv_w), 'proj_wT': wT(L0_attn_proj_w),
        'conv1_wT': wT(L0_attn_conv1_w), 'conv2_wT': wT(L0_attn_conv2_w),
        'w_gate': nat(L0_ffn_w_gate), 'w_up': nat(L0_ffn_w_up),
        'w_down': nat(L0_ffn_w_down),
        'ident': np.eye(128, dtype=BF),
        'ones1': np.ones((1, 128), dtype=BF),
    }
    in_maps = []
    for c in range(n_cores):
        b = c // cpb
        q = c % cpb
        xb_full = x[b]
        own = np.ascontiguousarray(xb_full[q * S_core:(q + 1) * S_core])
        m = dict(common)
        m['x_all'] = interleave_x(xb_full, q, S_core, cpb)
        m['x_own'] = own
        in_maps.append(m)

    res = run_bass_kernel_spmd(nc, in_maps, core_ids=list(range(n_cores)))
    shards = [res.results[c]['y'] for c in range(n_cores)]
    out = np.empty((B, S, C), np.float32)
    for c in range(n_cores):
        b = c // cpb
        q = c % cpb
        out[b, q * S_core:(q + 1) * S_core] = shards[c]
    return out


# version 59 (final)
# speedup vs baseline: 1.0241x; 1.0007x over previous
import numpy as np
import ml_dtypes
from contextlib import ExitStack

import concourse.bass as bass
import concourse.mybir as mybir
from concourse.bass_utils import run_bass_kernel_spmd

F32 = mybir.dt.float32
BF16 = mybir.dt.bfloat16
AF = mybir.ActivationFunctionType
ALU = mybir.AluOpType
AX = mybir.AxisListType
BF = ml_dtypes.bfloat16

C = 1024
HID = 2048
IFF = 2816
NH = 16
HD = 64
QB = 16
C6 = 0.16666667163372039794921875
LN32 = 3.4657359027997265

ENGINES = ('sync', 'scalar', 'vector', 'tensor', 'gpsimd')


class Prog:

    def __init__(self, nc, es):
        self.nc = nc
        self.items = {e: [] for e in ENGINES}
        self.ecounts = {e: 0 for e in ENGINES}
        self.progsem = {e: es.enter_context(nc.semaphore(name=f'pg_{e}'))
                        for e in ENGINES}
        self.dma_sems = {}
        self.dma_chain = {}
        self.es = es

    def _dep_list(self, deps):
        out = []
        for d in deps:
            if d is None:
                continue
            if isinstance(d, (list, tuple)) and d and isinstance(d[0], (list, tuple)):
                out.extend(x for x in d if x is not None)
            else:
                out.append(d)
        return out

    def op(self, eng, emit, deps=()):
        idx = self.ecounts[eng]
        self.ecounts[eng] += 1
        self.items[eng].append({'emit': emit, 'deps': self._dep_list(deps),
                                'dma': None})
        return ('E', eng, idx)

    def dma(self, eng, emit, sem, deps=()):
        if sem not in self.dma_sems:
            self.dma_sems[sem] = [self.es.enter_context(
                self.nc.semaphore(name=f'dm_{sem}')), 0]
        self.dma_sems[sem][1] += 1
        k = self.dma_sems[sem][1]
        dl = self._dep_list(deps)
        prev = self.dma_chain.get(sem)
        if prev is not None:
            dl.append(prev)
        tok = ('D', sem, k)
        self.dma_chain[sem] = tok
        self.items[eng].append({'emit': emit, 'deps': dl, 'dma': (sem, k)})
        return tok

    def wait(self, eng, deps):
        self.items[eng].append({'emit': None, 'deps': self._dep_list(deps),
                                'dma': None})
        return None

    def barrier(self, engines=ENGINES):
        toks = []
        for e in engines:
            n = self.ecounts[e]
            if n:
                toks.append(('E', e, n - 1))
        for s, (h, cnt) in self.dma_sems.items():
            if cnt:
                toks.append(('D', s, cnt))
        for e in engines:
            self.wait(e, toks)
        return toks

    def finalize(self, block):
        referenced = set()
        for eng in ENGINES:
            for it in self.items[eng]:
                for d in it['deps']:
                    if d[0] == 'E' and d[1] == 'tensor' and eng != 'tensor':
                        referenced.add(d[2])
        tidx_sorted = sorted(referenced)
        import bisect
        tinc_of = {}
        for n_, i_ in enumerate(tidx_sorted):
            tinc_of[i_] = n_ + 1
        self._tensor_ref = referenced
        last_t = self.ecounts['tensor'] - 1
        total_tincs = len(tidx_sorted) + (0 if last_t in referenced else 1)

        def tensor_wait_count(i2):
            j = bisect.bisect_left(tidx_sorted, i2)
            if j >= len(tidx_sorted):
                return total_tincs
            return tinc_of[tidx_sorted[j]]

        for eng in ENGINES:
            items = self.items[eng]
            if not items:
                continue
            section = getattr(block, eng)

            def make_body(items=items, eng=eng):
                def body(E):
                    water = {}
                    tcount = 0
                    tseen = -1
                    for it in items:
                        for d in it['deps']:
                            kind = d[0]
                            if kind == 'E':
                                _, e2, i2 = d
                                if e2 == 'tensor':
                                    if eng == 'tensor':
                                        continue
                                    need = tensor_wait_count(i2)
                                else:
                                    need = i2 + 1
                                key = ('E', e2)
                                if water.get(key, 0) < need:
                                    E.wait_ge(self.progsem[e2], need)
                                    water[key] = need
                            else:
                                _, sname, k = d
                                need = 16 * k
                                key = ('D', sname)
                                if water.get(key, 0) < need:
                                    E.wait_ge(self.dma_sems[sname][0], need)
                                    water[key] = need
                        if it['emit'] is None:
                            continue
                        inst = it['emit'](E)
                        if it['dma'] is not None:
                            inst.then_inc(self.dma_sems[it['dma'][0]][0], 16)
                        elif eng == 'tensor':
                            tseen += 1
                            if tseen in self._tensor_ref or tseen == self.ecounts['tensor'] - 1:
                                inst.then_inc(self.progsem[eng], 1)
                                tcount += 1
                        else:
                            inst.then_inc(self.progsem[eng], 1)
                return body
            section(make_body())


class Buf:
    def __init__(self, ap, name=''):
        self.ap = ap
        self.name = name
        self.w = None
        self.readers = []

    def rdep(self):
        return [self.w] if self.w is not None else []

    def wdep(self):
        d = list(self.readers)
        if self.w is not None:
            d.append(self.w)
        return d

    def note_read(self, tok):
        if tok is not None:
            self.readers.append(tok)

    def note_write(self, tok):
        self.w = tok
        self.readers = []


def oprw(p, eng, emit, reads=(), writes=(), extra=()):
    deps = list(extra)
    for b in reads:
        deps += b.rdep()
    for b in writes:
        deps += b.wdep()
    tok = p.op(eng, emit, deps)
    for b in reads:
        b.note_read(tok)
    for b in writes:
        b.note_write(tok)
    return tok


def dmarw(p, eng, emit, sem, reads=(), writes=(), extra=()):
    deps = list(extra)
    for b in reads:
        deps += b.rdep()
    for b in writes:
        deps += b.wdep()
    tok = p.dma(eng, emit, sem, deps)
    for b in reads:
        b.note_read(tok)
    for b in writes:
        b.note_write(tok)
    return tok



def build_nc(S_batch, S_core, n_cores_per_batch):
    nc = bass.Bass('TRN2', target_bir_lowering=False)
    T_batch = S_batch // 128
    T_own = S_core // 128
    cpb = n_cores_per_batch

    d_x_all = nc.dram_tensor('x_all', [S_batch, C], F32, kind='ExternalInput').ap()
    d_x_own = nc.dram_tensor('x_own', [S_core, C], F32, kind='ExternalInput').ap()
    d_scale1 = nc.dram_tensor('scale1', [1, C], F32, kind='ExternalInput').ap()
    d_scale2 = nc.dram_tensor('scale2', [1, C], F32, kind='ExternalInput').ap()
    d_gamma = nc.dram_tensor('gamma', [1, C], F32, kind='ExternalInput').ap()
    d_pb = nc.dram_tensor('proj_b', [1, C], F32, kind='ExternalInput').ap()
    d_c1b = nc.dram_tensor('conv1_b', [1, HID], F32, kind='ExternalInput').ap()
    d_c2b = nc.dram_tensor('conv2_b', [1, C], F32, kind='ExternalInput').ap()
    d_qkvT = nc.dram_tensor('qkv_wT', [C, 3 * C], F32, kind='ExternalInput').ap()
    d_projT = nc.dram_tensor('proj_wT', [C, C], F32, kind='ExternalInput').ap()
    d_c1T = nc.dram_tensor('conv1_wT', [C, HID], F32, kind='ExternalInput').ap()
    d_c2T = nc.dram_tensor('conv2_wT', [HID, C], F32, kind='ExternalInput').ap()
    d_wg = nc.dram_tensor('w_gate', [IFF, C], F32, kind='ExternalInput').ap()
    d_wu = nc.dram_tensor('w_up', [IFF, C], F32, kind='ExternalInput').ap()
    d_wd = nc.dram_tensor('w_down', [C, IFF], F32, kind='ExternalInput').ap()
    d_ident = nc.dram_tensor('ident', [128, 128], BF16, kind='ExternalInput').ap()
    d_ones1 = nc.dram_tensor('ones1', [1, 128], BF16, kind='ExternalInput').ap()
    d_y = nc.dram_tensor('y', [S_core, C], F32, kind='ExternalOutput').ap()
    d_qspill = nc.dram_tensor('qspill', [S_core, C], BF16, kind='Internal').ap()
    d_hqspill = nc.dram_tensor('hqspill', [S_core, IFF], BF16, kind='Internal').ap()
    d_xqspill = nc.dram_tensor('xqspill', [S_core, C], BF16, kind='Internal').ap()
    d_y1spill = nc.dram_tensor('y1spill', [S_core, C], F32, kind='Internal').ap()
    d_xn2spill = nc.dram_tensor('xn2spill', [S_core, C], BF16, kind='Internal').ap()
    d_oaspill = nc.dram_tensor('oaspill', [S_core, C], F32, kind='Internal').ap()
    d_wgq = nc.dram_tensor('wgq', [IFF, C], BF16, kind='Internal').ap()
    d_wuq = nc.dram_tensor('wuq', [IFF, C], BF16, kind='Internal').ap()
    d_wdq = nc.dram_tensor('wdq', [C, IFF], BF16, kind='Internal').ap()

    es = ExitStack()
    esA = ExitStack()
    esQ = ExitStack()
    esW = ExitStack()
    esF1 = ExitStack()
    esF2 = ExitStack()
    esB = ExitStack()
    esB2 = ExitStack()

    def SB(name, shape, dt, stack=None):
        h = (stack or es).enter_context(nc.sbuf_tensor(name, shape, dt))
        return Buf(h, name)

    def PSUM(name, shape, dt):
        h = es.enter_context(nc.psum_tensor(name, shape, dt))
        return Buf(h, name)

    p = Prog(nc, es)

    ident = SB('identsb', [128, 128], BF16)
    ones1 = SB('ones1sb', [1, 128], BF16)
    s1col = SB('s1col', [128, 8], F32)
    s2col = SB('s2col', [128, 8], F32)
    gammar = SB('gammar', [128, C], F32)
    gpb = SB('gpb', [1, C], BF16)
    gc2b = SB('gc2b', [1, C], BF16)
    c1brow = SB('c1brow', [1, HID], BF16)
    stats = SB('stats', [128, 2 * T_own], F32)
    kv_b = SB('kv_b', [128, 8 * 130], BF16)

    PA = PSUM('PA', [128, 512], F32)
    PB = PSUM('PB', [128, 512], F32)
    PC = PSUM('PC', [128, 512], F32)
    PD = PSUM('PD', [128, 512], F32)
    KV1 = PSUM('KV1', [128, 512], F32)
    KV2 = PSUM('KV2', [128, 512], F32)
    KV3 = PSUM('KV3', [128, 512], F32)
    TR = PSUM('TR', [128, 1024], BF16)

    mvP = [SB('mva', [128, 16], F32), SB('mvb', [128, 16], F32), SB('mvc', [128, 16], F32)]

    q_sgn = SB('q_sgn', [128, 1024], F32, stack=esQ)
    q_ss = SB('q_ss', [128, 1024], F32, stack=esQ)
    q_lt = SB('q_lt', [128, 1024], BF16, stack=esQ)
    q_m1 = SB('q_m1', [128, 1024], BF16, stack=esQ)
    q_m2 = SB('q_m2', [128, 1024], BF16, stack=esQ)
    q_m3 = SB('q_m3', [128, 1024], BF16, stack=esQ)
    q_g2 = SB('q_g2', [128, 1024], BF16, stack=esQ)
    q_a = SB('q_a', [128, 1024], F32, stack=esQ)
    q2_a = SB('q2_a', [128, 1024], F32, stack=esQ)
    q_amax = SB('q_amax', [128, 64], F32, stack=esQ)
    q_scale = SB('q_scale', [128, 64], F32, stack=esQ)
    q_tmp = SB('q_tmp', [128, 64], F32, stack=esQ)
    q_hs = SB('q_hs', [128, 64], F32, stack=esQ)
    q_inv = SB('q_inv', [128, 64], F32, stack=esQ)
    q2_sgn = SB('q2_sgn', [128, 1024], F32, stack=esQ)
    q2_ss = SB('q2_ss', [128, 1024], F32, stack=esQ)
    q2_amax = SB('q2_amax', [128, 64], F32, stack=esQ)
    q2_scale = SB('q2_scale', [128, 64], F32, stack=esQ)
    q2_tmp = SB('q2_tmp', [128, 64], F32, stack=esQ)
    q2_hs = SB('q2_hs', [128, 64], F32, stack=esQ)
    q2_inv = SB('q2_inv', [128, 64], F32, stack=esQ)
    WJW = 1408
    wqld = [SB('wqld1', [128, WJW], F32, stack=esW),
            SB('wqld2', [128, WJW], F32, stack=esW)]
    wqout = SB('wqout', [128, WJW], BF16, stack=esW)
    rowtmp = SB('rowtmp', [1, C], F32, stack=esA)
    rowtmp2 = SB('rowtmp2', [1, C], F32, stack=esA)

    QSET0 = (q_sgn, q_amax, q_scale, q_tmp, q_hs, q_inv, q_ss, q_a)
    QSET1 = (q2_sgn, q2_amax, q2_scale, q2_tmp, q2_hs, q2_inv, q2_ss, q2_a)

    def emit_quant_pre_core(src, soff, W, qs, tok_extra=()):
        sgn, amax, scale, tmp, hs, inv, ssb, qa = qs
        NB = W // QB
        sa = src.ap[:, soff:soff + W]
        oprw(p, 'scalar', lambda sc, sa=sa, W=W, qa=qa: sc.activation(
            qa.ap[:, 0:W], sa, AF.Abs), reads=[src], writes=[qa], extra=tok_extra)
        oprw(p, 'scalar', lambda sc, sa=sa, W=W, sgn=sgn: sc.activation(
            sgn.ap[:, 0:W], sa, AF.Sign), reads=[src], writes=[sgn], extra=tok_extra)
        oprw(p, 'vector', lambda v, sa=sa, NB=NB, amax=amax: v.tensor_reduce(
            amax.ap[:, 0:NB].unsqueeze(2),
            sa.rearrange("p (a b) -> p a b", b=QB), AX.X, ALU.max,
            apply_absolute_value=True), reads=[src], writes=[amax], extra=tok_extra)
        oprw(p, 'vector', lambda v, NB=NB, amax=amax, scale=scale: v.tensor_scalar(
            scale.ap[:, 0:NB], amax.ap[:, 0:NB], C6, None, ALU.mult),
            reads=[amax], writes=[scale])
        oprw(p, 'vector', lambda v, NB=NB, amax=amax, scale=scale, tmp=tmp: v.scalar_tensor_tensor(
            tmp.ap[:, 0:NB], scale.ap[:, 0:NB], -6.0, amax.ap[:, 0:NB],
            ALU.mult, ALU.add), reads=[scale, amax], writes=[tmp])
        oprw(p, 'vector', lambda v, NB=NB, scale=scale, tmp=tmp: v.scalar_tensor_tensor(
            scale.ap[:, 0:NB], tmp.ap[:, 0:NB], C6, scale.ap[:, 0:NB],
            ALU.mult, ALU.add), reads=[tmp], writes=[scale])
        oprw(p, 'vector', lambda v, NB=NB, scale=scale: v.tensor_scalar(
            scale.ap[:, 0:NB], scale.ap[:, 0:NB], 1e-12, None, ALU.max),
            writes=[scale])
        oprw(p, 'vector', lambda v, NB=NB, scale=scale, hs=hs: v.tensor_scalar(
            hs.ap[:, 0:NB], scale.ap[:, 0:NB], 0.5, None, ALU.mult),
            reads=[scale], writes=[hs])
        oprw(p, 'vector', lambda v, NB=NB, scale=scale, inv=inv: v.reciprocal(
            inv.ap[:, 0:NB], scale.ap[:, 0:NB]), reads=[scale], writes=[inv])

    def emit_quant_pre_pool(W, qs, u_eng='gpsimd'):
        sgn, amax, scale, tmp, hs, inv, ssb, qa = qs
        NB = W // QB
        oprw(p, 'gpsimd', lambda g, NB=NB, W=W, sgn=sgn, hs=hs, ssb=ssb: g.tensor_tensor(
            ssb.ap[:, 0:W].rearrange("p (a b) -> p a b", b=QB),
            sgn.ap[:, 0:W].rearrange("p (a b) -> p a b", b=QB),
            hs.ap[:, 0:NB].unsqueeze(2).broadcast_to([128, NB, QB]),
            ALU.mult), reads=[sgn, hs], writes=[ssb])
        oprw(p, u_eng, lambda g, NB=NB, W=W, inv=inv, qa=qa: g.tensor_tensor(
            qa.ap[:, 0:W].rearrange("p (a b) -> p a b", b=QB),
            qa.ap[:, 0:W].rearrange("p (a b) -> p a b", b=QB),
            inv.ap[:, 0:NB].unsqueeze(2).broadcast_to([128, NB, QB]),
            ALU.mult), reads=[inv], writes=[qa])

    def emit_quant_pre(src, soff, W, qs, tok_extra=()):
        emit_quant_pre_core(src, soff, W, qs, tok_extra)
        emit_quant_pre_pool(W, qs)

    def emit_quant_main(dst, doff, W, qs):
        sgn, amax, scale, tmp, hs, inv, ssb, q_a = qs
        NB = W // QB
        oprw(p, 'vector', lambda g, W=W, q_a=q_a: g.tensor_scalar(
            q_lt.ap[:, 0:W], q_a.ap[:, 0:W], 0.25, None, ALU.is_gt),
            reads=[q_a], writes=[q_lt])
        oprw(p, 'vector', lambda g, W=W: g.tensor_scalar(
            q_m1.ap[:, 0:W], q_a.ap[:, 0:W], 0.75, None, ALU.is_gt),
            reads=[q_a], writes=[q_m1])
        oprw(p, 'vector', lambda g, W=W: g.tensor_scalar(
            q_m2.ap[:, 0:W], q_a.ap[:, 0:W], 1.25, None, ALU.is_gt),
            reads=[q_a], writes=[q_m2])
        oprw(p, 'vector', lambda g, W=W: g.tensor_scalar(
            q_m3.ap[:, 0:W], q_a.ap[:, 0:W], 1.75, None, ALU.is_gt),
            reads=[q_a], writes=[q_m3])
        oprw(p, 'vector', lambda g, W=W: g.tensor_tensor(
            q_lt.ap[:, 0:W], q_lt.ap[:, 0:W], q_m1.ap[:, 0:W], ALU.add),
            reads=[q_m1], writes=[q_lt])
        oprw(p, 'vector', lambda g, W=W: g.tensor_scalar(
            q_m1.ap[:, 0:W], q_a.ap[:, 0:W], 2.5, 2.0, ALU.is_gt, ALU.mult),
            reads=[q_a], writes=[q_m1])
        oprw(p, 'vector', lambda g, W=W: g.tensor_tensor(
            q_m2.ap[:, 0:W], q_m2.ap[:, 0:W], q_m3.ap[:, 0:W], ALU.add),
            reads=[q_m3], writes=[q_m2])
        oprw(p, 'vector', lambda g, W=W: g.tensor_scalar(
            q_m3.ap[:, 0:W], q_a.ap[:, 0:W], 3.5, 2.0, ALU.is_gt, ALU.mult),
            reads=[q_a], writes=[q_m3])
        oprw(p, 'vector', lambda g, W=W: g.tensor_tensor(
            q_lt.ap[:, 0:W], q_lt.ap[:, 0:W], q_m2.ap[:, 0:W], ALU.add),
            reads=[q_m2], writes=[q_lt])
        oprw(p, 'vector', lambda g, W=W: g.tensor_scalar(
            q_m2.ap[:, 0:W], q_a.ap[:, 0:W], 5.0, 4.0, ALU.is_gt, ALU.mult),
            reads=[q_a], writes=[q_m2])
        oprw(p, 'vector', lambda g, W=W: g.tensor_tensor(
            q_m1.ap[:, 0:W], q_m1.ap[:, 0:W], q_m3.ap[:, 0:W], ALU.add),
            reads=[q_m3], writes=[q_m1])
        oprw(p, 'vector', lambda g, W=W: g.tensor_tensor(
            q_m1.ap[:, 0:W], q_m1.ap[:, 0:W], q_m2.ap[:, 0:W], ALU.add),
            reads=[q_m2], writes=[q_m1])
        oprw(p, 'vector', lambda g, W=W: g.tensor_tensor(
            q_g2.ap[:, 0:W], q_lt.ap[:, 0:W], q_m1.ap[:, 0:W], ALU.add),
            reads=[q_lt, q_m1], writes=[q_g2])
        return oprw(p, 'vector', lambda v, W=W, doff=doff, ssb=ssb: v.tensor_tensor(
            dst.ap[:, doff:doff + W], q_g2.ap[:, 0:W], ssb.ap[:, 0:W],
            ALU.mult), reads=[q_g2, ssb], writes=[dst])

    def emit_quant(src, soff, dst, doff, W, tok_extra=(), eng='vector'):
        emit_quant_pre(src, soff, W, QSET0, tok_extra)
        return emit_quant_main(dst, doff, W, QSET0)

    dmarw(p, 'sync', lambda E: E.dma_start(ident.ap[:], d_ident[:]), 'win',
          writes=[ident])
    dmarw(p, 'sync', lambda E: E.dma_start(ones1.ap[:], d_ones1[:]), 'win',
          writes=[ones1])
    dmarw(p, 'sync', lambda E: E.dma_start(
        s1col.ap[:], d_scale1.rearrange("o (a p) -> (o p) a", p=128),
        allow_slow_non_contiguous=True), 'win', writes=[s1col])
    dmarw(p, 'sync', lambda E: E.dma_start(
        s2col.ap[:], d_scale2.rearrange("o (a p) -> (o p) a", p=128),
        allow_slow_non_contiguous=True), 'win', writes=[s2col])
    dmarw(p, 'sync', lambda E: E.dma_start(
        gammar.ap[:], d_gamma.partition_broadcast(128)), 'win', writes=[gammar])
    dmarw(p, 'sync', lambda E: E.dma_start(rowtmp.ap[:], d_pb[:]), 'win',
          writes=[rowtmp])
    oprw(p, 'vector', lambda v: v.tensor_tensor(
        gpb.ap[:], rowtmp.ap[:], gammar.ap[0:1, :], ALU.mult),
        reads=[rowtmp, gammar], writes=[gpb])
    dmarw(p, 'sync', lambda E: E.dma_start(rowtmp2.ap[:], d_c2b[:]), 'win',
          writes=[rowtmp2])
    oprw(p, 'vector', lambda v: v.tensor_tensor(
        gc2b.ap[:], rowtmp2.ap[:], gammar.ap[0:1, :], ALU.mult),
        reads=[rowtmp2, gammar], writes=[gc2b])
    dmarw(p, 'gpsimd', lambda E: E.dma_start(c1brow.ap[:], d_c1b[:]), 'gwin',
          writes=[c1brow])
    wTsh = esA.enter_context(nc.sbuf_tensor('wTs', [128, 8 * 3 * C], BF16))
    wTsC = [Buf(wTsh, f'wTs{kc}') for kc in range(8)]
    wld = SB('wld', [128, 2 * 3 * C], F32, stack=esA)
    for kc in range(8):
        half = kc % 2
        ld = dmarw(p, 'sync', lambda E, kc=kc, half=half: E.dma_start(
            wld.ap[:, half * 3 * C:(half + 1) * 3 * C],
            d_qkvT[kc * 128:(kc + 1) * 128, :]), 'wqkv', writes=[wld])
        oprw(p, 'vector', lambda v, kc=kc, half=half: v.tensor_scalar(
            wTsC[kc].ap[:, kc * 3 * C:(kc + 1) * 3 * C],
            wld.ap[:, half * 3 * C:(half + 1) * 3 * C],
            s1col.ap[:, kc:kc + 1], None, ALU.mult),
            reads=[wld, s1col], writes=[wTsC[kc]])

    xb = [SB('xb0', [128, C], F32, stack=esA), SB('xb1', [128, C], F32, stack=esA), SB('xb2', [128, C], F32, stack=esA)]
    xn1P = [SB('xn1a', [128, C], BF16, stack=esA), SB('xn1b', [128, C], BF16, stack=esA), SB('xn1c', [128, C], BF16, stack=esA)]
    xTP = [SB('xTa', [128, C], BF16, stack=esA), SB('xTb', [128, C], BF16, stack=esA)]
    ksq1 = SB('ksq1', [128, C], F32, stack=esA)
    ksqP = [ksq1, ksq1]
    kphiP = [SB('kphia', [128, C], BF16, stack=esA), SB('kphib', [128, C], BF16, stack=esA)]
    vextP = [SB('vexta', [128, NH * 65], BF16, stack=esA), SB('vextb', [128, NH * 65], BF16, stack=esA)]
    qphiP = [SB('qphia', [128, C], BF16, stack=esA), SB('qphib', [128, C], BF16, stack=esA)]

    for _ve in vextP:
        oprw(p, 'vector', lambda v, _ve=_ve: v.memset(_ve.ap[:], 1.0), writes=[_ve])
    oprw(p, 'vector', lambda v: v.memset(kv_b.ap[:], 0.0), writes=[kv_b])

    wjobs = []
    for r in range(IFF // 128):
        wjobs.append((d_wg, d_wgq, r, 0, C))
    for r in range(IFF // 128):
        wjobs.append((d_wu, d_wuq, r, 0, C))
    for r in range(C // 128):
        wjobs.append((d_wd, d_wdq, r, 0, WJW))
        wjobs.append((d_wd, d_wdq, r, WJW, WJW))
    wj_i = 0

    def emit_wjob_load(i):
        src, dst, r, co, W = wjobs[i]
        buf = wqld[i % 2]
        dmarw(p, 'gpsimd', lambda E, src=src, r=r, co=co, W=W, buf=buf: E.dma_start(
            buf.ap[:, 0:W], src[r * 128:(r + 1) * 128, co:co + W]),
            f'wstr{i % 2}', writes=[buf])

    def emit_wjob_prerest(i, qs=None):
        src, dst, r, co, W = wjobs[i]
        buf = wqld[i % 2]
        qs = qs or (QSET1 if i % 2 else QSET0)
        emit_quant_pre(buf, 0, min(1024, W), qs)

    def emit_wjob_pre(i, qs=None):
        emit_wjob_load(i)
        emit_wjob_prerest(i, qs)

    def emit_wjob_main(i, qs=None):
        src, dst, r, co, W = wjobs[i]
        buf = wqld[i % 2]
        qs = qs or (QSET1 if i % 2 else QSET0)
        emit_quant_main(wqout, 0, min(1024, W), qs)
        for off in range(1024, W, 1024):
            w = min(1024, W - off)
            emit_quant_pre(buf, off, w, qs)
            emit_quant_main(wqout, off, w, qs)
        dmarw(p, 'gpsimd', lambda E, dst=dst, r=r, co=co, W=W: E.dma_start(
            dst[r * 128:(r + 1) * 128, co:co + W], wqout.ap[:, 0:W]), 'wstw',
            reads=[wqout])

    def emit_wjob(i, qs=None):
        emit_wjob_pre(i, qs)
        emit_wjob_main(i, qs)

    n_w_per_tile = 1

    def stageA1a(t):
        x = xb[t % 3]
        mv = mvP[t % 3]
        dmarw(p, 'sync', lambda E, t=t, x=x: E.dma_start(
            x.ap[:], d_x_all[t * 128:(t + 1) * 128, :]), f'xinA{t % 3}',
            writes=[x])
        oprw(p, 'vector', lambda v, x=x, mv=mv: v.bn_stats(
            mv.ap[:, 0:6], x.ap[:, 0:512]), reads=[x], writes=[mv])
        oprw(p, 'vector', lambda v, x=x, mv=mv: v.bn_stats(
            mv.ap[:, 6:12], x.ap[:, 512:1024]), reads=[x], writes=[mv])
        oprw(p, 'vector', lambda v, mv=mv: v.bn_aggr(
            mv.ap[:, 12:14], mv.ap[:, 0:12]), writes=[mv])
        oprw(p, 'vector', lambda v, mv=mv: v.tensor_scalar(
            mv.ap[:, 14:15], mv.ap[:, 13:14], 1e-5, None, ALU.add), writes=[mv])
        oprw(p, 'vector', lambda v, mv=mv: v.tensor_copy(
            mv.ap[:, 15:16], mv.ap[:, 13:14]), writes=[mv])
        oprw(p, 'scalar', lambda sc, mv=mv: sc.activation(
            mv.ap[:, 14:16], mv.ap[:, 14:16], AF.Ln), writes=[mv])
        oprw(p, 'scalar', lambda sc, mv=mv: sc.activation(
            mv.ap[:, 14:16], mv.ap[:, 14:16], AF.Exp, scale=-0.5), writes=[mv])

    def stageA1b(t):
        x = xb[t % 3]
        xn1 = xn1P[t % 3]
        mv = mvP[t % 3]
        is_own = (t % cpb) == 0
        o = t // cpb
        if is_own:
            oprw(p, 'vector', lambda v, o=o, mv=mv: v.tensor_copy(
                stats.ap[:, 2 * o:2 * o + 1], mv.ap[:, 12:13]),
                reads=[mv], writes=[stats])
            oprw(p, 'vector', lambda v, o=o, mv=mv: v.tensor_copy(
                stats.ap[:, 2 * o + 1:2 * o + 2], mv.ap[:, 14:15]),
                reads=[mv], writes=[stats])
        oprw(p, 'vector', lambda v, x=x, mv=mv, xn1=xn1: v.tensor_scalar(
            xn1.ap[:], x.ap[:], mv.ap[:, 12:13], mv.ap[:, 15:16],
            ALU.subtract, ALU.mult), reads=[x, mv], writes=[xn1])

    def stageA2(t):
        xn1 = xn1P[t % 3]
        xT = xTP[t % 2]
        ksq = ksqP[t % 2]
        kphi = kphiP[t % 2]
        vext = vextP[t % 2]
        qphi = qphiP[t % 2]
        is_own = (t % cpb) == 0
        o = t // cpb
        for kc in range(8):
            oprw(p, 'tensor', lambda T, kc=kc, xn1=xn1: T.transpose(
                TR.ap[:, kc * 128:(kc + 1) * 128],
                xn1.ap[:, kc * 128:(kc + 1) * 128], ident.ap[:]),
                reads=[xn1, ident], writes=[TR])
        oprw(p, 'scalar' if t < 40 else 'vector',
             lambda e_, xT=xT: e_.copy(xT.ap[:], TR.ap[:]) if hasattr(e_, 'copy')
             else e_.tensor_copy(xT.ap[:], TR.ap[:]),
             reads=[TR], writes=[xT])
        for kc in range(8):
            for hh in range(2):
                oprw(p, 'tensor', lambda T, kc=kc, hh=hh, xT=xT: T.matmul(
                    (PA if hh == 0 else PB).ap[:],
                    xT.ap[:, kc * 128:(kc + 1) * 128],
                    wTsC[kc].ap[:, kc * 3 * C + C + hh * 512: kc * 3 * C + C + (hh + 1) * 512],
                    start=(kc == 0), stop=(kc == 7)),
                    reads=[xT, wTsC[kc]], writes=[PA if hh == 0 else PB])
        for kc in range(8):
            for hh in range(2):
                oprw(p, 'tensor', lambda T, kc=kc, hh=hh, xT=xT: T.matmul(
                    (PC if hh == 0 else PD).ap[:],
                    xT.ap[:, kc * 128:(kc + 1) * 128],
                    wTsC[kc].ap[:, kc * 3 * C + 2 * C + hh * 512: kc * 3 * C + 2 * C + (hh + 1) * 512],
                    start=(kc == 0), stop=(kc == 7)),
                    reads=[xT, wTsC[kc]], writes=[PC if hh == 0 else PD])
        oprw(p, 'scalar', lambda sc, ksq=ksq: sc.activation(
            ksq.ap[:, 0:512], PA.ap[:], AF.Square), reads=[PA], writes=[ksq])
        oprw(p, 'scalar', lambda sc, ksq=ksq: sc.activation(
            ksq.ap[:, 512:1024], PB.ap[:], AF.Square), reads=[PB], writes=[ksq])
        oprw(p, 'scalar', lambda sc, ksq=ksq, kphi=kphi: sc.activation(
            kphi.ap[:], ksq.ap[:], AF.Exp, scale=-0.5), reads=[ksq], writes=[kphi])
        if is_own:
            for kc in range(8):
                for hh in range(2):
                    oprw(p, 'tensor', lambda T, kc=kc, hh=hh, xT=xT: T.matmul(
                        (PA if hh == 0 else PB).ap[:],
                        xT.ap[:, kc * 128:(kc + 1) * 128],
                        wTsC[kc].ap[:, kc * 3 * C + hh * 512: kc * 3 * C + (hh + 1) * 512],
                        start=(kc == 0), stop=(kc == 7)),
                        reads=[xT, wTsC[kc]], writes=[PA if hh == 0 else PB])
        oprw(p, 'scalar', lambda sc, vext=vext: sc.copy(
            vext.ap[:].rearrange("p (h e) -> p h e", e=65)[:, 0:8, 0:64],
            PC.ap[:].rearrange("p (h e) -> p h e", e=64)[:, 0:8, :]),
            reads=[PC], writes=[vext])
        oprw(p, 'scalar', lambda sc, vext=vext: sc.copy(
            vext.ap[:].rearrange("p (h e) -> p h e", e=65)[:, 8:16, 0:64],
            PD.ap[:].rearrange("p (h e) -> p h e", e=64)[:, 0:8, :]),
            reads=[PD], writes=[vext])
        if is_own:
            oprw(p, 'scalar', lambda sc, ksq=ksq: sc.activation(
                ksq.ap[:, 0:512], PA.ap[:], AF.Square), reads=[PA], writes=[ksq])
            oprw(p, 'scalar', lambda sc, ksq=ksq: sc.activation(
                ksq.ap[:, 512:1024], PB.ap[:], AF.Square), reads=[PB], writes=[ksq])
            oprw(p, 'scalar', lambda sc, ksq=ksq, qphi=qphi: sc.activation(
                qphi.ap[:], ksq.ap[:], AF.Exp, scale=-0.5),
                reads=[ksq], writes=[qphi])
            dmarw(p, 'scalar', lambda E, o=o, qphi=qphi: E.dma_start(
                d_qspill[o * 128:(o + 1) * 128, :], qphi.ap[:]), 'qsp',
                reads=[qphi])

    def stageA2kv(t):
        kphi = kphiP[t % 2]
        vext = vextP[t % 2]
        for j in range(8):
            bank = (KV1, KV2, KV3)[j // 3]
            slot = j % 3
            st = (t == 0 and slot == 0)
            sp = (t == T_batch - 1 and j in (2, 5, 7))
            oprw(p, 'tensor', lambda T, j=j, bank=bank, slot=slot, st=st, sp=sp, kphi=kphi, vext=vext: T.matmul(
                bank.ap[:, slot * 130:(slot + 1) * 130],
                kphi.ap[:, j * 128:(j + 1) * 128],
                vext.ap[:, j * 130:(j + 1) * 130],
                start=st, stop=sp),
                reads=[kphi, vext], writes=[bank])

    stageA1a(0)
    stageA1b(0)
    stageA1a(1)
    w_pre = 0
    w_load = 0
    for t in range(T_batch):
        if t + 2 < T_batch:
            stageA1a(t + 2)
        if t + 1 < T_batch:
            stageA1b(t + 1)
        if (t < 34 or t % 2 == 0) and w_load < min(len(wjobs), 38):
            emit_wjob_load(w_load)
            w_load += 1
            if w_pre < w_load - 1:
                emit_wjob_prerest(w_pre)
                w_pre += 1
        stageA2(t)
        if t > 0:
            stageA2kv(t - 1)
        if wj_i < w_pre - 1:
            emit_wjob_main(wj_i)
            wj_i += 1
    stageA2kv(T_batch - 1)
    while w_pre < w_load:
        emit_wjob_prerest(w_pre)
        w_pre += 1
    while wj_i < w_pre:
        emit_wjob_main(wj_i)
        wj_i += 1

    for j in range(8):
        bank = (KV1, KV2, KV3)[j // 3]
        slot = j % 3
        oprw(p, 'vector', lambda v, j=j, bank=bank, slot=slot: v.tensor_copy(
            kv_b.ap[0:64, j * 130:j * 130 + 65],
            bank.ap[0:64, slot * 130:slot * 130 + 65]),
            reads=[bank], writes=[kv_b])
        oprw(p, 'vector', lambda v, j=j, bank=bank, slot=slot: v.tensor_copy(
            kv_b.ap[64:128, j * 130 + 65:j * 130 + 130],
            bank.ap[64:128, slot * 130 + 65:slot * 130 + 130]),
            reads=[bank], writes=[kv_b])

    p.barrier()
    esA.close()

    projTgh = esB.enter_context(nc.sbuf_tensor('projTg', [128, 8 * C], BF16))
    projTgC = [Buf(projTgh, f'projTg{kc}') for kc in range(8)]
    wldB = SB('wldB', [128, HID], F32, stack=esB)
    for kc in range(8):
        dmarw(p, 'sync', lambda E, kc=kc: E.dma_start(
            wldB.ap[:, 0:C],
            d_projT[kc * 128:(kc + 1) * 128, :]), 'wB', writes=[wldB])
        oprw(p, 'vector', lambda v, kc=kc: v.tensor_tensor(
            projTgC[kc].ap[:, kc * C:(kc + 1) * C],
            wldB.ap[:, 0:C], gammar.ap[:], ALU.mult),
            reads=[wldB, gammar], writes=[projTgC[kc]])

    qTallh = esB.enter_context(nc.sbuf_tensor('qTall', [128, 8 * S_core], BF16))
    qTallC = [Buf(qTallh, f'qTall{j}') for j in range(8)]
    for j in range(8):
        dmarw(p, 'sync', lambda E, j=j: E.dma_start(
            qTallC[j].ap[:, j * S_core:(j + 1) * S_core],
            d_qspill[:, j * 128:(j + 1) * 128],
            transpose=True), f'qrd{j % 2}', writes=[qTallC[j]])
    den = SB('den', [128, NH], F32, stack=esB)
    zz = SB('zz', [128, NH], F32, stack=esB)
    outzP = [SB('outza', [128, C], BF16, stack=esB), SB('outzb', [128, C], BF16, stack=esB)]
    ozTP = [SB('ozTa', [128, C], BF16, stack=esB), SB('ozTb', [128, C], BF16, stack=esB)]
    xn2P = [SB('xn2a', [128, C], BF16, stack=esB), SB('xn2b', [128, C], BF16, stack=esB)]
    xnBP = [SB('xnBa', [128, C], F32, stack=esB), SB('xnBb', [128, C], F32, stack=esB)]
    oaP = [SB('oaa', [128, C], F32, stack=esB), SB('oab', [128, C], F32, stack=esB)]
    xbB = [SB('xbBa', [128, C], F32, stack=esB), SB('xbBb', [128, C], F32, stack=esB)]
    x1P = [SB('x1a', [128, C], F32, stack=esB), SB('x1b', [128, C], F32, stack=esB)]
    xqBP = [SB('xqBa', [128, C], BF16, stack=esB), SB('xqBb', [128, C], BF16, stack=esB)]

    OUTB = (KV1, KV2, KV3)

    def stageBa(t):
        pr = t % 2
        x = xbB[pr]
        dmarw(p, 'sync', lambda E, t=t, x=x: E.dma_start(
            x.ap[:], d_x_own[t * 128:(t + 1) * 128, :]), f'xin{t % 2}',
            writes=[x])

    def stageBb(t):
        pr = t % 2
        x = xbB[pr]
        outz = outzP[pr]
        ozT = ozTP[pr]
        xn = xnBP[pr]
        x1 = x1P[pr]
        oacc = oaP[pr]
        xn2 = xn2P[pr]
        mv = mvP[pr]
        oprw(p, 'vector', lambda v, x=x, xn=xn, t=t: v.tensor_scalar(
            xn.ap[:], x.ap[:], stats.ap[:, 2 * t:2 * t + 1],
            stats.ap[:, 2 * t + 1:2 * t + 2], ALU.subtract, ALU.mult),
            reads=[x, stats], writes=[xn])
        for j in range(8):
            bank = OUTB[j // 3]
            slot = j % 3
            oprw(p, 'tensor', lambda T, j=j, bank=bank, slot=slot, t=t: T.matmul(
                bank.ap[:, slot * 130:(slot + 1) * 130],
                qTallC[j].ap[:, j * S_core + t * 128:j * S_core + (t + 1) * 128],
                kv_b.ap[:, j * 130:(j + 1) * 130],
                start=True, stop=True),
                reads=[qTallC[j], kv_b], writes=[bank])
        oprw(p, 'scalar', lambda sc: sc.copy(
            den.ap[:, 0:6],
            OUTB[0].ap[:, 0:390].rearrange("p (h e) -> p h e", e=65)[:, :, 64]),
            reads=[OUTB[0]], writes=[den])
        oprw(p, 'scalar', lambda sc: sc.copy(
            den.ap[:, 6:12],
            OUTB[1].ap[:, 0:390].rearrange("p (h e) -> p h e", e=65)[:, :, 64]),
            reads=[OUTB[1]], writes=[den])
        oprw(p, 'scalar', lambda sc: sc.copy(
            den.ap[:, 12:16],
            OUTB[2].ap[:, 0:260].rearrange("p (h e) -> p h e", e=65)[:, :, 64]),
            reads=[OUTB[2]], writes=[den])
        oprw(p, 'vector', lambda v: v.tensor_scalar(
            den.ap[:], den.ap[:], 1e-6, None, ALU.add), writes=[den])
        oprw(p, 'vector', lambda v: v.reciprocal(zz.ap[:], den.ap[:]),
             reads=[den], writes=[zz])
        for bi, (b0, h0, nh) in enumerate(((OUTB[0], 0, 6), (OUTB[1], 6, 6),
                                           (OUTB[2], 12, 4))):
            oprw(p, 'vector', lambda v, b0=b0, h0=h0, nh=nh, outz=outz: v.tensor_tensor(
                outz.ap[:, h0 * HD:(h0 + nh) * HD].rearrange(
                    "p (h e) -> p h e", e=HD),
                b0.ap[:, 0:nh * 65].rearrange("p (h e) -> p h e", e=65)[:, :, 0:HD],
                zz.ap[:, h0:h0 + nh].unsqueeze(2).broadcast_to([128, nh, HD]),
                ALU.mult), reads=[b0, zz], writes=[outz])
        for kc in range(8):
            oprw(p, 'tensor', lambda T, kc=kc, outz=outz: T.transpose(
                TR.ap[:, kc * 128:(kc + 1) * 128],
                outz.ap[:, kc * 128:(kc + 1) * 128], ident.ap[:]),
                reads=[outz, ident], writes=[TR])
        oprw(p, 'scalar', lambda sc, ozT=ozT: sc.copy(ozT.ap[:], TR.ap[:]),
             reads=[TR], writes=[ozT])
        prA, prB = (PA, PB) if t % 2 == 0 else (PC, PD)
        for hh in range(2):
            pbk = prA if hh == 0 else prB
            for kc in range(8):
                oprw(p, 'tensor', lambda T, kc=kc, hh=hh, pbk=pbk, ozT=ozT: T.matmul(
                    pbk.ap[:], ozT.ap[:, kc * 128:(kc + 1) * 128],
                    projTgC[kc].ap[:, kc * C + hh * 512: kc * C + (hh + 1) * 512],
                    start=(kc == 0), stop=False),
                    reads=[ozT, projTgC[kc]], writes=[pbk])
            oprw(p, 'tensor', lambda T, hh=hh, pbk=pbk: T.matmul(
                pbk.ap[:], ones1.ap[:],
                gpb.ap[:, hh * 512:(hh + 1) * 512], start=False, stop=True),
                reads=[ones1, gpb], writes=[pbk])
    def stageBb2(t):
        pr = t % 2
        x = xbB[pr]
        xn = xnBP[pr]
        x1 = x1P[pr]
        oacc = oaP[pr]
        xn2 = xn2P[pr]
        mv = mvP[pr]
        prA, prB = (PA, PB) if t % 2 == 0 else (PC, PD)
        for hh, pbk in ((0, prA), (1, prB)):
            oprw(p, 'vector', lambda v, hh=hh, pbk=pbk, x1=x1, xn=xn: v.tensor_tensor(
                x1.ap[:, hh * 512:(hh + 1) * 512], pbk.ap[:],
                xn.ap[:, hh * 512:(hh + 1) * 512], ALU.add),
                reads=[pbk, xn], writes=[x1])
            oprw(p, 'vector', lambda v, hh=hh, oacc=oacc, x1=x1, x=x: v.tensor_tensor(
                oacc.ap[:, hh * 512:(hh + 1) * 512],
                x1.ap[:, hh * 512:(hh + 1) * 512],
                x.ap[:, hh * 512:(hh + 1) * 512], ALU.add),
                reads=[x1, x], writes=[oacc])
        dmarw(p, 'scalar', lambda E, t=t, oacc=oacc: E.dma_start(
            d_oaspill[t * 128:(t + 1) * 128, :], oacc.ap[:]), 'oas', reads=[oacc])
        oprw(p, 'vector', lambda v, mv=mv, x1=x1: v.bn_stats(
            mv.ap[:, 0:6], x1.ap[:, 0:512]), reads=[x1], writes=[mv])
        oprw(p, 'vector', lambda v, mv=mv, x1=x1: v.bn_stats(
            mv.ap[:, 6:12], x1.ap[:, 512:1024]), reads=[x1], writes=[mv])
        oprw(p, 'vector', lambda v, mv=mv: v.bn_aggr(
            mv.ap[:, 12:14], mv.ap[:, 0:12]), writes=[mv])
        oprw(p, 'vector', lambda v, mv=mv: v.tensor_scalar(
            mv.ap[:, 14:15], mv.ap[:, 12:13], mv.ap[:, 12:13], None,
            ALU.mult), writes=[mv])
        oprw(p, 'vector', lambda v, mv=mv: v.tensor_tensor(
            mv.ap[:, 14:15], mv.ap[:, 14:15], mv.ap[:, 13:14], ALU.add),
            writes=[mv])
        oprw(p, 'scalar', lambda sc, mv=mv: sc.activation(
            mv.ap[:, 14:15], mv.ap[:, 14:15], AF.Ln, scale=1024.0), writes=[mv])
        oprw(p, 'scalar', lambda sc, mv=mv: sc.activation(
            mv.ap[:, 14:15], mv.ap[:, 14:15], AF.Exp, scale=-0.5), writes=[mv])
        oprw(p, 'vector', lambda v, mv=mv, x1=x1, xn2=xn2: v.tensor_scalar(
            xn2.ap[:], x1.ap[:], mv.ap[:, 14:15], 32.0, ALU.mult, ALU.mult),
            reads=[x1, mv], writes=[xn2])
        dmarw(p, 'scalar', lambda E, t=t, xn2=xn2: E.dma_start(
            d_xn2spill[t * 128:(t + 1) * 128, :], xn2.ap[:]), 'xn2s', reads=[xn2])
        if t > 0:
            xqp = xqBP[(t - 1) % 2]
            emit_quant_main(xqp, 0, C, QSET0)
            dmarw(p, 'scalar', lambda E, t=t, xqp=xqp: E.dma_start(
                d_xqspill[(t - 1) * 128:t * 128, :], xqp.ap[:]), 'xqs',
                reads=[xqp])
        emit_quant_pre(xn, 0, C, QSET0)

    stageBa(0)
    for t in range(T_own):
        stageBb(t)
        if t > 0:
            stageBb2(t - 1)
        if t + 1 < T_own:
            stageBa(t + 1)
        if t % 3 == 2 and wj_i < min(len(wjobs), 40):
            emit_wjob(wj_i, QSET1)
            wj_i += 1
    stageBb2(T_own - 1)
    xqp = xqBP[(T_own - 1) % 2]
    emit_quant_main(xqp, 0, C, QSET0)
    dmarw(p, 'scalar', lambda E, xqp=xqp: E.dma_start(
        d_xqspill[(T_own - 1) * 128:T_own * 128, :], xqp.ap[:]), 'xqs',
        reads=[xqp])

    p.barrier()
    esB.close()

    c1Tsh = esB2.enter_context(nc.sbuf_tensor('c1Ts', [128, 8 * HID], BF16))
    c1TsC = [Buf(c1Tsh, f'c1Ts{kc}') for kc in range(8)]
    c2Tgh = esB2.enter_context(nc.sbuf_tensor('c2Tg', [128, 16 * C], BF16))
    c2TgC = [Buf(c2Tgh, f'c2Tg{kc}') for kc in range(16)]
    wldB2 = SB('wldB2', [128, C], F32, stack=esB2)
    for kc in range(8):
        for hf in range(2):
            dmarw(p, 'sync', lambda E, kc=kc, hf=hf: E.dma_start(
                wldB2.ap[:],
                d_c1T[kc * 128:(kc + 1) * 128, hf * C:(hf + 1) * C]), 'wB',
                writes=[wldB2])
            oprw(p, 'vector', lambda v, kc=kc, hf=hf: v.tensor_scalar(
                c1TsC[kc].ap[:, kc * HID + hf * C:kc * HID + (hf + 1) * C],
                wldB2.ap[:],
                s2col.ap[:, kc:kc + 1], None, ALU.mult),
                reads=[wldB2, s2col], writes=[c1TsC[kc]])
    for kc in range(16):
        dmarw(p, 'sync', lambda E, kc=kc: E.dma_start(
            wldB2.ap[:, 0:C],
            d_c2T[kc * 128:(kc + 1) * 128, :]), 'wB', writes=[wldB2])
        oprw(p, 'vector', lambda v, kc=kc: v.tensor_tensor(
            c2TgC[kc].ap[:, kc * C:(kc + 1) * C],
            wldB2.ap[:, 0:C], gammar.ap[:], ALU.mult),
            reads=[wldB2, gammar], writes=[c2TgC[kc]])

    x2Tallh = esB2.enter_context(nc.sbuf_tensor('x2Tall', [128, 8 * S_core], BF16))
    x2TallC = [Buf(x2Tallh, f'x2Tall{j}') for j in range(8)]
    for j in range(8):
        dmarw(p, 'sync', lambda E, j=j: E.dma_start(
            x2TallC[j].ap[:, j * S_core:(j + 1) * S_core],
            d_xn2spill[:, j * 128:(j + 1) * 128],
            transpose=True), f'xn2r{j % 2}', writes=[x2TallC[j]])
    hddP = [SB('hdda', [128, HID], BF16, stack=esB2), SB('hddb', [128, HID], BF16, stack=esB2)]
    hdTP = [SB('hdTa', [128, HID], BF16, stack=esB2), SB('hdTb', [128, HID], BF16, stack=esB2)]
    oaiP = [SB('oaia', [128, C], F32, stack=esB2), SB('oaib', [128, C], F32, stack=esB2)]

    def stageB2oai(t):
        oai = oaiP[t % 2]
        dmarw(p, 'sync', lambda E, t=t, oai=oai: E.dma_start(
            oai.ap[:], d_oaspill[t * 128:(t + 1) * 128, :]), 'oar', writes=[oai])

    def stageB2m(t):
        pr = t % 2
        hdd = hddP[pr]
        C1B = (PA, PB, PC, PD)
        for hh in range(4):
            pbk = C1B[hh]
            for kc in range(8):
                oprw(p, 'tensor', lambda T, kc=kc, hh=hh, pbk=pbk, t=t: T.matmul(
                    pbk.ap[:], x2TallC[kc].ap[:, kc * S_core + t * 128:kc * S_core + (t + 1) * 128],
                    c1TsC[kc].ap[:, kc * HID + hh * 512: kc * HID + (hh + 1) * 512],
                    start=(kc == 0), stop=False),
                    reads=[x2TallC[kc], c1TsC[kc]], writes=[pbk])
            oprw(p, 'tensor', lambda T, hh=hh, pbk=pbk: T.matmul(
                pbk.ap[:], ones1.ap[:],
                c1brow.ap[:, hh * 512:(hh + 1) * 512], start=False, stop=True),
                reads=[ones1, c1brow], writes=[pbk])
        for hh in range(4):
            pbk = C1B[hh]
            oprw(p, 'scalar', lambda sc, hh=hh, pbk=pbk, hdd=hdd: sc.activation(
                hdd.ap[:, hh * 512:(hh + 1) * 512], pbk.ap[:], AF.Gelu),
                reads=[pbk], writes=[hdd])

    def stageB2c(t):
        pr = t % 2
        hdd = hddP[pr]
        hdT = hdTP[pr]
        oai = oaiP[pr]
        yb = oai
        for rnd in range(2):
            for kc in range(rnd * 8, rnd * 8 + 8):
                oprw(p, 'tensor', lambda T, kc=kc, rnd=rnd, hdd=hdd: T.transpose(
                    TR.ap[:, (kc - rnd * 8) * 128:(kc - rnd * 8 + 1) * 128],
                    hdd.ap[:, kc * 128:(kc + 1) * 128], ident.ap[:]),
                    reads=[hdd, ident], writes=[TR])
            oprw(p, 'scalar', lambda sc, rnd=rnd, hdT=hdT: sc.copy(
                hdT.ap[:, rnd * 1024:(rnd + 1) * 1024], TR.ap[:]),
                reads=[TR], writes=[hdT])
        for hh in range(2):
            pbk = KV1 if hh == 0 else KV2
            for kc in range(16):
                oprw(p, 'tensor', lambda T, kc=kc, hh=hh, pbk=pbk, hdT=hdT: T.matmul(
                    pbk.ap[:], hdT.ap[:, kc * 128:(kc + 1) * 128],
                    c2TgC[kc].ap[:, kc * C + hh * 512: kc * C + (hh + 1) * 512],
                    start=(kc == 0), stop=False),
                    reads=[hdT, c2TgC[kc]], writes=[pbk])
            oprw(p, 'tensor', lambda T, hh=hh, pbk=pbk: T.matmul(
                pbk.ap[:], ones1.ap[:],
                gc2b.ap[:, hh * 512:(hh + 1) * 512], start=False, stop=True),
                reads=[ones1, gc2b], writes=[pbk])
        for hh, pbk in ((0, KV1), (1, KV2)):
            oprw(p, 'vector', lambda v, hh=hh, pbk=pbk, yb=yb, oai=oai: v.tensor_tensor(
                yb.ap[:, hh * 512:(hh + 1) * 512], pbk.ap[:],
                oai.ap[:, hh * 512:(hh + 1) * 512], ALU.add),
                reads=[pbk, oai], writes=[yb])
        dmarw(p, 'sync', lambda E, t=t, yb=yb: E.dma_start(
            d_y1spill[t * 128:(t + 1) * 128, :], yb.ap[:]), 'y1s', reads=[yb])

    stageB2oai(0)
    for t in range(T_own):
        stageB2m(t)
        if t > 0:
            stageB2c(t - 1)
        if t + 1 < T_own:
            stageB2oai(t + 1)
        if wj_i < len(wjobs):
            emit_wjob(wj_i)
            wj_i += 1
    stageB2c(T_own - 1)

    while wj_i < len(wjobs):
        emit_wjob(wj_i)
        wj_i += 1

    p.barrier()
    esB2.close()
    esW.close()


    wgTh = esF1.enter_context(nc.sbuf_tensor('wgT', [128, 8 * IFF], BF16))
    wgTC = [Buf(wgTh, f'wgT{kc}') for kc in range(8)]
    wuTh = esF1.enter_context(nc.sbuf_tensor('wuT', [128, 8 * IFF], BF16))
    wuTC = [Buf(wuTh, f'wuT{kc}') for kc in range(8)]
    for kc in range(8):
        dmarw(p, 'sync', lambda E, kc=kc: E.dma_start(
            wgTC[kc].ap[:, kc * IFF:(kc + 1) * IFF],
            d_wgq[:, kc * 128:(kc + 1) * 128], transpose=True), f'wgt{kc % 2}',
            writes=[wgTC[kc]])
        dmarw(p, 'sync', lambda E, kc=kc: E.dma_start(
            wuTC[kc].ap[:, kc * IFF:(kc + 1) * IFF],
            d_wuq[:, kc * 128:(kc + 1) * 128], transpose=True), f'wut{kc % 2}',
            writes=[wuTC[kc]])

    xqTallh = esF1.enter_context(nc.sbuf_tensor('xqTall', [128, 8 * S_core], BF16))
    xqTallC = [Buf(xqTallh, f'xqTall{j}') for j in range(8)]
    for j in range(8):
        dmarw(p, 'sync', lambda E, j=j: E.dma_start(
            xqTallC[j].ap[:, j * S_core:(j + 1) * S_core],
            d_xqspill[:, j * 128:(j + 1) * 128],
            transpose=True), f'xqr{j % 2}', writes=[xqTallC[j]])
    hbufP = [SB('hbufa', [128, IFF], F32, stack=esF1), SB('hbufb', [128, IFF], F32, stack=esF1)]
    hsilP = [SB('hsila', [128, 512], F32, stack=esF1), SB('hsilb', [128, 512], F32, stack=esF1)]
    hq1 = SB('hq1', [128, IFF], BF16, stack=esF1)
    hqP = [hq1, hq1]

    def stageF1m(t):
        hbuf = hbufP[t % 2]
        for occ in range(6):
            w = 512 if occ < 5 else IFF - 5 * 512
            pg = PA if occ % 2 == 0 else PC
            pu = PB if occ % 2 == 0 else PD
            for kc in range(8):
                oprw(p, 'tensor', lambda T, kc=kc, occ=occ, w=w, pg=pg, t=t: T.matmul(
                    pg.ap[:, 0:w],
                    xqTallC[kc].ap[:, kc * S_core + t * 128:kc * S_core + (t + 1) * 128],
                    wgTC[kc].ap[:, kc * IFF + occ * 512: kc * IFF + occ * 512 + w],
                    start=(kc == 0), stop=(kc == 7)),
                    reads=[xqTallC[kc], wgTC[kc]], writes=[pg])
            for kc in range(8):
                oprw(p, 'tensor', lambda T, kc=kc, occ=occ, w=w, pu=pu, t=t: T.matmul(
                    pu.ap[:, 0:w],
                    xqTallC[kc].ap[:, kc * S_core + t * 128:kc * S_core + (t + 1) * 128],
                    wuTC[kc].ap[:, kc * IFF + occ * 512: kc * IFF + occ * 512 + w],
                    start=(kc == 0), stop=(kc == 7)),
                    reads=[xqTallC[kc], wuTC[kc]], writes=[pu])
            hsil = hsilP[occ % 2]
            oprw(p, 'scalar', lambda sc, occ=occ, w=w, pg=pg, hsil=hsil: sc.activation(
                hsil.ap[:, 0:w], pg.ap[:, 0:w], AF.Silu), reads=[pg], writes=[hsil])
            oprw(p, 'scalar', lambda sc, occ=occ, w=w, pu=pu, hbuf=hbuf: sc.copy(
                hbuf.ap[:, occ * 512:occ * 512 + w], pu.ap[:, 0:w]),
                reads=[pu], writes=[hbuf])
            oprw(p, 'gpsimd', lambda g, occ=occ, w=w, hsil=hsil, hbuf=hbuf: g.tensor_tensor(
                hbuf.ap[:, occ * 512:occ * 512 + w], hsil.ap[:, 0:w],
                hbuf.ap[:, occ * 512:occ * 512 + w], ALU.mult),
                reads=[hsil], writes=[hbuf])

    def stageF1q(t):
        hbuf = hbufP[t % 2]
        hq = hqP[t % 2]
        emit_quant_pre_core(hbuf, 0, 1024, QSET0)
        emit_quant_pre_pool(1024, QSET0, u_eng='vector')
        emit_quant_pre_core(hbuf, 1024, 1024, QSET1)
        emit_quant_pre_pool(1024, QSET1, u_eng='vector')
        emit_quant_main(hq, 0, 1024, QSET0)
        emit_quant_pre_core(hbuf, 2048, IFF - 2048, QSET0)
        emit_quant_pre_pool(IFF - 2048, QSET0, u_eng='vector')
        emit_quant_main(hq, 1024, 1024, QSET1)
        emit_quant_main(hq, 2048, IFF - 2048, QSET0)
        dmarw(p, 'scalar', lambda E, t=t, hq=hq: E.dma_start(
            d_hqspill[t * 128:(t + 1) * 128, :], hq.ap[:]), 'hqs', reads=[hq])

    for t in range(T_own):
        stageF1m(t)
        if t > 0:
            stageF1q(t - 1)
    stageF1q(T_own - 1)

    p.barrier()
    esF1.close()
    esQ.close()

    wdTh = esF2.enter_context(nc.sbuf_tensor('wdT', [128, 22 * C], BF16))
    wdTC = [Buf(wdTh, f'wdT{kc}') for kc in range(22)]
    for kc in range(22):
        dmarw(p, 'sync', lambda E, kc=kc: E.dma_start(
            wdTC[kc].ap[:, kc * C:(kc + 1) * C],
            d_wdq[:, kc * 128:(kc + 1) * 128], transpose=True), f'wdt{kc % 3}',
            writes=[wdTC[kc]])
    hqTallh = esF2.enter_context(nc.sbuf_tensor('hqTall', [128, 22 * S_core], BF16))
    hqTallC = [Buf(hqTallh, f'hqTall{j}') for j in range(22)]
    for j in range(22):
        dmarw(p, 'sync', lambda E, j=j: E.dma_start(
            hqTallC[j].ap[:, j * S_core:(j + 1) * S_core],
            d_hqspill[:, j * 128:(j + 1) * 128],
            transpose=True), f'hqr{j % 3}', writes=[hqTallC[j]])
    fbufP = [SB('fbufa', [128, C], F32, stack=esF2), SB('fbufb', [128, C], F32, stack=esF2)]
    xbF = [SB('xbFa', [128, C], F32, stack=esF2), SB('xbFb', [128, C], F32, stack=esF2)]

    def stageF2a(t):
        xfb = xbF[t % 2]
        dmarw(p, 'sync', lambda E, t=t, xfb=xfb: E.dma_start(
            xfb.ap[:], d_y1spill[t * 128:(t + 1) * 128, :]), f'xin{t % 2}',
            writes=[xfb])

    def stageF2b(t):
        fbuf = fbufP[t % 2]
        xfb = xbF[t % 2]
        pa, pb = (PA, PB) if t % 2 == 0 else (PC, PD)
        for kc in range(22):
            for hh in range(2):
                oprw(p, 'tensor', lambda T, kc=kc, hh=hh, t=t, pa=pa, pb=pb: T.matmul(
                    (pa if hh == 0 else pb).ap[:],
                    hqTallC[kc].ap[:, kc * S_core + t * 128:kc * S_core + (t + 1) * 128],
                    wdTC[kc].ap[:, kc * C + hh * 512: kc * C + (hh + 1) * 512],
                    start=(kc == 0), stop=(kc == 21)),
                    reads=[hqTallC[kc], wdTC[kc]], writes=[pa if hh == 0 else pb])
        oprw(p, 'vector', lambda v, xfb=xfb, fbuf=fbuf, pa=pa: v.tensor_tensor(
            fbuf.ap[:, 0:512], pa.ap[:], xfb.ap[:, 0:512], ALU.add),
            reads=[pa, xfb], writes=[fbuf])
        oprw(p, 'vector', lambda v, xfb=xfb, fbuf=fbuf, pb=pb: v.tensor_tensor(
            fbuf.ap[:, 512:1024], pb.ap[:], xfb.ap[:, 512:1024], ALU.add),
            reads=[pb, xfb], writes=[fbuf])
        dmarw(p, 'sync', lambda E, t=t, fbuf=fbuf: E.dma_start(
            d_y[t * 128:(t + 1) * 128, :], fbuf.ap[:]), 'yout', reads=[fbuf])

    stageF2a(0)
    for t in range(T_own):
        if t + 1 < T_own:
            stageF2a(t + 1)
        stageF2b(t)

    p.barrier()
    esF2.close()

    p.wait('sync', [('D', 'yout', p.dma_sems['yout'][1])])

    esB.close()
    with nc.Block() as block:
        p.finalize(block)
    es.close()
    return nc


def interleave_x(xb_full, q, S_core, cpb):
    S, C_ = xb_full.shape
    nb = S // 128
    T_own = S_core // 128
    blocks = xb_full.reshape(nb, 128, C_)
    own_ids = list(range(q * T_own, (q + 1) * T_own))
    own_set = set(own_ids)
    other_ids = [i for i in range(nb) if i not in own_set]
    order = []
    oi = ri = 0
    for t in range(nb):
        if t % cpb == 0:
            order.append(own_ids[oi])
            oi += 1
        else:
            order.append(other_ids[ri])
            ri += 1
    return np.ascontiguousarray(blocks[order].reshape(S, C_))


_NC_CACHE = {}


def _get_nc(S_batch, S_core):
    key = (S_batch, S_core)
    if key not in _NC_CACHE:
        _NC_CACHE[key] = build_nc(S_batch, S_core, S_batch // S_core)
    return _NC_CACHE[key]


def kernel(L0_attn_x, L0_attn_scale1, L0_attn_scale2, L0_attn_gamma,
           L0_attn_qkv_w, L0_attn_proj_w, L0_attn_proj_b, L0_attn_conv1_w,
           L0_attn_conv1_b, L0_attn_conv2_w, L0_attn_conv2_b, L0_ffn_w_gate,
           L0_ffn_w_up, L0_ffn_w_down):
    x = np.ascontiguousarray(np.asarray(L0_attn_x, np.float32))
    B, S, _ = x.shape
    n_cores = 8
    cpb = n_cores // B
    S_core = S // cpb
    nc = _get_nc(S, S_core)

    row = lambda a: np.ascontiguousarray(np.asarray(a, np.float32)).reshape(1, -1)
    wT = lambda w: np.ascontiguousarray(np.asarray(w, np.float32).T)
    nat = lambda w: np.ascontiguousarray(np.asarray(w, np.float32))
    common = {
        'scale1': row(L0_attn_scale1), 'scale2': row(L0_attn_scale2),
        'gamma': row(L0_attn_gamma), 'proj_b': row(L0_attn_proj_b),
        'conv1_b': row(L0_attn_conv1_b), 'conv2_b': row(L0_attn_conv2_b),
        'qkv_wT': wT(L0_attn_qkv_w), 'proj_wT': wT(L0_attn_proj_w),
        'conv1_wT': wT(L0_attn_conv1_w), 'conv2_wT': wT(L0_attn_conv2_w),
        'w_gate': nat(L0_ffn_w_gate), 'w_up': nat(L0_ffn_w_up),
        'w_down': nat(L0_ffn_w_down),
        'ident': np.eye(128, dtype=BF),
        'ones1': np.ones((1, 128), dtype=BF),
    }
    in_maps = []
    for c in range(n_cores):
        b = c // cpb
        q = c % cpb
        xb_full = x[b]
        own = np.ascontiguousarray(xb_full[q * S_core:(q + 1) * S_core])
        m = dict(common)
        m['x_all'] = interleave_x(xb_full, q, S_core, cpb)
        m['x_own'] = own
        in_maps.append(m)

    res = run_bass_kernel_spmd(nc, in_maps, core_ids=list(range(n_cores)))
    shards = [res.results[c]['y'] for c in range(n_cores)]
    out = np.empty((B, S, C), np.float32)
    for c in range(n_cores):
        b = c // cpb
        q = c % cpb
        out[b, q * S_core:(q + 1) * S_core] = shards[c]
    return out


# version 60
# speedup vs baseline: 1.0269x; 1.0027x over previous
import numpy as np
import ml_dtypes
from contextlib import ExitStack

import concourse.bass as bass
import concourse.mybir as mybir
from concourse.bass_utils import run_bass_kernel_spmd

F32 = mybir.dt.float32
BF16 = mybir.dt.bfloat16
AF = mybir.ActivationFunctionType
ALU = mybir.AluOpType
AX = mybir.AxisListType
BF = ml_dtypes.bfloat16

C = 1024
HID = 2048
IFF = 2816
NH = 16
HD = 64
QB = 16
C6 = 0.16666667163372039794921875
LN32 = 3.4657359027997265

ENGINES = ('sync', 'scalar', 'vector', 'tensor', 'gpsimd')


class Prog:

    def __init__(self, nc, es):
        self.nc = nc
        self.items = {e: [] for e in ENGINES}
        self.ecounts = {e: 0 for e in ENGINES}
        self.progsem = {e: es.enter_context(nc.semaphore(name=f'pg_{e}'))
                        for e in ENGINES}
        self.dma_sems = {}
        self.dma_chain = {}
        self.es = es

    def _dep_list(self, deps):
        out = []
        for d in deps:
            if d is None:
                continue
            if isinstance(d, (list, tuple)) and d and isinstance(d[0], (list, tuple)):
                out.extend(x for x in d if x is not None)
            else:
                out.append(d)
        return out

    def op(self, eng, emit, deps=()):
        idx = self.ecounts[eng]
        self.ecounts[eng] += 1
        self.items[eng].append({'emit': emit, 'deps': self._dep_list(deps),
                                'dma': None})
        return ('E', eng, idx)

    def dma(self, eng, emit, sem, deps=()):
        if sem not in self.dma_sems:
            self.dma_sems[sem] = [self.es.enter_context(
                self.nc.semaphore(name=f'dm_{sem}')), 0]
        self.dma_sems[sem][1] += 1
        k = self.dma_sems[sem][1]
        dl = self._dep_list(deps)
        prev = self.dma_chain.get(sem)
        if prev is not None:
            dl.append(prev)
        tok = ('D', sem, k)
        self.dma_chain[sem] = tok
        self.items[eng].append({'emit': emit, 'deps': dl, 'dma': (sem, k)})
        return tok

    def wait(self, eng, deps):
        self.items[eng].append({'emit': None, 'deps': self._dep_list(deps),
                                'dma': None})
        return None

    def barrier(self, engines=ENGINES):
        toks = []
        for e in engines:
            n = self.ecounts[e]
            if n:
                toks.append(('E', e, n - 1))
        for s, (h, cnt) in self.dma_sems.items():
            if cnt:
                toks.append(('D', s, cnt))
        for e in engines:
            self.wait(e, toks)
        return toks

    def finalize(self, block):
        referenced = set()
        for eng in ENGINES:
            for it in self.items[eng]:
                for d in it['deps']:
                    if d[0] == 'E' and d[1] == 'tensor' and eng != 'tensor':
                        referenced.add(d[2])
        tidx_sorted = sorted(referenced)
        import bisect
        tinc_of = {}
        for n_, i_ in enumerate(tidx_sorted):
            tinc_of[i_] = n_ + 1
        self._tensor_ref = referenced
        last_t = self.ecounts['tensor'] - 1
        total_tincs = len(tidx_sorted) + (0 if last_t in referenced else 1)

        def tensor_wait_count(i2):
            j = bisect.bisect_left(tidx_sorted, i2)
            if j >= len(tidx_sorted):
                return total_tincs
            return tinc_of[tidx_sorted[j]]

        for eng in ENGINES:
            items = self.items[eng]
            if not items:
                continue
            section = getattr(block, eng)

            def make_body(items=items, eng=eng):
                def body(E):
                    water = {}
                    tcount = 0
                    tseen = -1
                    for it in items:
                        for d in it['deps']:
                            kind = d[0]
                            if kind == 'E':
                                _, e2, i2 = d
                                if e2 == 'tensor':
                                    if eng == 'tensor':
                                        continue
                                    need = tensor_wait_count(i2)
                                else:
                                    need = i2 + 1
                                key = ('E', e2)
                                if water.get(key, 0) < need:
                                    E.wait_ge(self.progsem[e2], need)
                                    water[key] = need
                            else:
                                _, sname, k = d
                                need = 16 * k
                                key = ('D', sname)
                                if water.get(key, 0) < need:
                                    E.wait_ge(self.dma_sems[sname][0], need)
                                    water[key] = need
                        if it['emit'] is None:
                            continue
                        inst = it['emit'](E)
                        if it['dma'] is not None:
                            inst.then_inc(self.dma_sems[it['dma'][0]][0], 16)
                        elif eng == 'tensor':
                            tseen += 1
                            if tseen in self._tensor_ref or tseen == self.ecounts['tensor'] - 1:
                                inst.then_inc(self.progsem[eng], 1)
                                tcount += 1
                        else:
                            inst.then_inc(self.progsem[eng], 1)
                return body
            section(make_body())


class Buf:
    def __init__(self, ap, name=''):
        self.ap = ap
        self.name = name
        self.w = None
        self.readers = []

    def rdep(self):
        return [self.w] if self.w is not None else []

    def wdep(self):
        d = list(self.readers)
        if self.w is not None:
            d.append(self.w)
        return d

    def note_read(self, tok):
        if tok is not None:
            self.readers.append(tok)

    def note_write(self, tok):
        self.w = tok
        self.readers = []


def oprw(p, eng, emit, reads=(), writes=(), extra=()):
    deps = list(extra)
    for b in reads:
        deps += b.rdep()
    for b in writes:
        deps += b.wdep()
    tok = p.op(eng, emit, deps)
    for b in reads:
        b.note_read(tok)
    for b in writes:
        b.note_write(tok)
    return tok


def dmarw(p, eng, emit, sem, reads=(), writes=(), extra=()):
    deps = list(extra)
    for b in reads:
        deps += b.rdep()
    for b in writes:
        deps += b.wdep()
    tok = p.dma(eng, emit, sem, deps)
    for b in reads:
        b.note_read(tok)
    for b in writes:
        b.note_write(tok)
    return tok



def build_nc(S_batch, S_core, n_cores_per_batch):
    nc = bass.Bass('TRN2', target_bir_lowering=False)
    T_batch = S_batch // 128
    T_own = S_core // 128
    cpb = n_cores_per_batch

    d_x_all = nc.dram_tensor('x_all', [S_batch, C], F32, kind='ExternalInput').ap()
    d_x_own = nc.dram_tensor('x_own', [S_core, C], F32, kind='ExternalInput').ap()
    d_scale1 = nc.dram_tensor('scale1', [1, C], F32, kind='ExternalInput').ap()
    d_scale2 = nc.dram_tensor('scale2', [1, C], F32, kind='ExternalInput').ap()
    d_gamma = nc.dram_tensor('gamma', [1, C], F32, kind='ExternalInput').ap()
    d_pb = nc.dram_tensor('proj_b', [1, C], F32, kind='ExternalInput').ap()
    d_c1b = nc.dram_tensor('conv1_b', [1, HID], F32, kind='ExternalInput').ap()
    d_c2b = nc.dram_tensor('conv2_b', [1, C], F32, kind='ExternalInput').ap()
    d_qkvT = nc.dram_tensor('qkv_wT', [C, 3 * C], F32, kind='ExternalInput').ap()
    d_projT = nc.dram_tensor('proj_wT', [C, C], F32, kind='ExternalInput').ap()
    d_c1T = nc.dram_tensor('conv1_wT', [C, HID], F32, kind='ExternalInput').ap()
    d_c2T = nc.dram_tensor('conv2_wT', [HID, C], F32, kind='ExternalInput').ap()
    d_wg = nc.dram_tensor('w_gate', [IFF, C], F32, kind='ExternalInput').ap()
    d_wu = nc.dram_tensor('w_up', [IFF, C], F32, kind='ExternalInput').ap()
    d_wd = nc.dram_tensor('w_down', [C, IFF], F32, kind='ExternalInput').ap()
    d_ident = nc.dram_tensor('ident', [128, 128], BF16, kind='ExternalInput').ap()
    d_ones1 = nc.dram_tensor('ones1', [1, 128], BF16, kind='ExternalInput').ap()
    d_y = nc.dram_tensor('y', [S_core, C], F32, kind='ExternalOutput').ap()
    d_qspill = nc.dram_tensor('qspill', [S_core, C], BF16, kind='Internal').ap()
    d_hqspill = nc.dram_tensor('hqspill', [S_core, IFF], BF16, kind='Internal').ap()
    d_xqspill = nc.dram_tensor('xqspill', [S_core, C], BF16, kind='Internal').ap()
    d_y1spill = nc.dram_tensor('y1spill', [S_core, C], F32, kind='Internal').ap()
    d_xn2spill = nc.dram_tensor('xn2spill', [S_core, C], BF16, kind='Internal').ap()
    d_oaspill = nc.dram_tensor('oaspill', [S_core, C], F32, kind='Internal').ap()
    d_wgq = nc.dram_tensor('wgq', [IFF, C], BF16, kind='Internal').ap()
    d_wuq = nc.dram_tensor('wuq', [IFF, C], BF16, kind='Internal').ap()
    d_wdq = nc.dram_tensor('wdq', [C, IFF], BF16, kind='Internal').ap()

    es = ExitStack()
    esA = ExitStack()
    esQ = ExitStack()
    esW = ExitStack()
    esF1 = ExitStack()
    esF2 = ExitStack()
    esB = ExitStack()
    esB2 = ExitStack()

    def SB(name, shape, dt, stack=None):
        h = (stack or es).enter_context(nc.sbuf_tensor(name, shape, dt))
        return Buf(h, name)

    def PSUM(name, shape, dt):
        h = es.enter_context(nc.psum_tensor(name, shape, dt))
        return Buf(h, name)

    p = Prog(nc, es)

    ident = SB('identsb', [128, 128], BF16)
    ones1 = SB('ones1sb', [1, 128], BF16)
    s1col = SB('s1col', [128, 8], F32)
    s2col = SB('s2col', [128, 8], F32)
    gammar = SB('gammar', [128, C], F32)
    gpb = SB('gpb', [1, C], BF16)
    gc2b = SB('gc2b', [1, C], BF16)
    c1brow = SB('c1brow', [1, HID], BF16)
    stats = SB('stats', [128, 2 * T_own], F32)
    kv_b = SB('kv_b', [128, 8 * 130], BF16)

    PA = PSUM('PA', [128, 512], F32)
    PB = PSUM('PB', [128, 512], F32)
    PC = PSUM('PC', [128, 512], F32)
    PD = PSUM('PD', [128, 512], F32)
    KV1 = PSUM('KV1', [128, 512], F32)
    KV2 = PSUM('KV2', [128, 512], F32)
    KV3 = PSUM('KV3', [128, 512], F32)
    TR = PSUM('TR', [128, 1024], BF16)

    mvP = [SB('mva', [128, 16], F32), SB('mvb', [128, 16], F32), SB('mvc', [128, 16], F32)]

    q_sgn = SB('q_sgn', [128, 1024], F32, stack=esQ)
    q_ss = SB('q_ss', [128, 1024], F32, stack=esQ)
    q_lt = SB('q_lt', [128, 1024], BF16, stack=esQ)
    q_m1 = SB('q_m1', [128, 1024], BF16, stack=esQ)
    q_m2 = SB('q_m2', [128, 1024], BF16, stack=esQ)
    q_m3 = SB('q_m3', [128, 1024], BF16, stack=esQ)
    q_g2 = SB('q_g2', [128, 1024], BF16, stack=esQ)
    q_a = SB('q_a', [128, 1024], F32, stack=esQ)
    q2_a = SB('q2_a', [128, 1024], F32, stack=esQ)
    q_amax = SB('q_amax', [128, 64], F32, stack=esQ)
    q_scale = SB('q_scale', [128, 64], F32, stack=esQ)
    q_tmp = SB('q_tmp', [128, 64], F32, stack=esQ)
    q_hs = SB('q_hs', [128, 64], F32, stack=esQ)
    q_inv = SB('q_inv', [128, 64], F32, stack=esQ)
    q2_sgn = SB('q2_sgn', [128, 1024], F32, stack=esQ)
    q2_ss = SB('q2_ss', [128, 1024], F32, stack=esQ)
    q2_amax = SB('q2_amax', [128, 64], F32, stack=esQ)
    q2_scale = SB('q2_scale', [128, 64], F32, stack=esQ)
    q2_tmp = SB('q2_tmp', [128, 64], F32, stack=esQ)
    q2_hs = SB('q2_hs', [128, 64], F32, stack=esQ)
    q2_inv = SB('q2_inv', [128, 64], F32, stack=esQ)
    WJW = 1408
    wqld = [SB('wqld1', [128, WJW], F32, stack=esW),
            SB('wqld2', [128, WJW], F32, stack=esW)]
    wqout = SB('wqout', [128, WJW], BF16, stack=esW)
    rowtmp = SB('rowtmp', [1, C], F32, stack=esA)
    rowtmp2 = SB('rowtmp2', [1, C], F32, stack=esA)

    QSET0 = (q_sgn, q_amax, q_scale, q_tmp, q_hs, q_inv, q_ss, q_a)
    QSET1 = (q2_sgn, q2_amax, q2_scale, q2_tmp, q2_hs, q2_inv, q2_ss, q2_a)

    def emit_quant_pre_core(src, soff, W, qs, tok_extra=()):
        sgn, amax, scale, tmp, hs, inv, ssb, qa = qs
        NB = W // QB
        sa = src.ap[:, soff:soff + W]
        oprw(p, 'scalar', lambda sc, sa=sa, W=W, qa=qa: sc.activation(
            qa.ap[:, 0:W], sa, AF.Abs), reads=[src], writes=[qa], extra=tok_extra)
        oprw(p, 'scalar', lambda sc, sa=sa, W=W, sgn=sgn: sc.activation(
            sgn.ap[:, 0:W], sa, AF.Sign), reads=[src], writes=[sgn], extra=tok_extra)
        oprw(p, 'vector', lambda v, sa=sa, NB=NB, amax=amax: v.tensor_reduce(
            amax.ap[:, 0:NB].unsqueeze(2),
            sa.rearrange("p (a b) -> p a b", b=QB), AX.X, ALU.max,
            apply_absolute_value=True), reads=[src], writes=[amax], extra=tok_extra)
        oprw(p, 'vector', lambda v, NB=NB, amax=amax, scale=scale: v.tensor_scalar(
            scale.ap[:, 0:NB], amax.ap[:, 0:NB], C6, None, ALU.mult),
            reads=[amax], writes=[scale])
        oprw(p, 'vector', lambda v, NB=NB, amax=amax, scale=scale, tmp=tmp: v.scalar_tensor_tensor(
            tmp.ap[:, 0:NB], scale.ap[:, 0:NB], -6.0, amax.ap[:, 0:NB],
            ALU.mult, ALU.add), reads=[scale, amax], writes=[tmp])
        oprw(p, 'vector', lambda v, NB=NB, scale=scale, tmp=tmp: v.scalar_tensor_tensor(
            scale.ap[:, 0:NB], tmp.ap[:, 0:NB], C6, scale.ap[:, 0:NB],
            ALU.mult, ALU.add), reads=[tmp], writes=[scale])
        oprw(p, 'vector', lambda v, NB=NB, scale=scale: v.tensor_scalar(
            scale.ap[:, 0:NB], scale.ap[:, 0:NB], 1e-12, None, ALU.max),
            writes=[scale])
        oprw(p, 'vector', lambda v, NB=NB, scale=scale, hs=hs: v.tensor_scalar(
            hs.ap[:, 0:NB], scale.ap[:, 0:NB], 0.5, None, ALU.mult),
            reads=[scale], writes=[hs])
        oprw(p, 'vector', lambda v, NB=NB, scale=scale, inv=inv: v.reciprocal(
            inv.ap[:, 0:NB], scale.ap[:, 0:NB]), reads=[scale], writes=[inv])

    def emit_quant_pre_pool(W, qs, u_eng='gpsimd'):
        sgn, amax, scale, tmp, hs, inv, ssb, qa = qs
        NB = W // QB
        oprw(p, 'gpsimd', lambda g, NB=NB, W=W, sgn=sgn, hs=hs, ssb=ssb: g.tensor_tensor(
            ssb.ap[:, 0:W].rearrange("p (a b) -> p a b", b=QB),
            sgn.ap[:, 0:W].rearrange("p (a b) -> p a b", b=QB),
            hs.ap[:, 0:NB].unsqueeze(2).broadcast_to([128, NB, QB]),
            ALU.mult), reads=[sgn, hs], writes=[ssb])
        oprw(p, u_eng, lambda g, NB=NB, W=W, inv=inv, qa=qa: g.tensor_tensor(
            qa.ap[:, 0:W].rearrange("p (a b) -> p a b", b=QB),
            qa.ap[:, 0:W].rearrange("p (a b) -> p a b", b=QB),
            inv.ap[:, 0:NB].unsqueeze(2).broadcast_to([128, NB, QB]),
            ALU.mult), reads=[inv], writes=[qa])

    def emit_quant_pre(src, soff, W, qs, tok_extra=()):
        emit_quant_pre_core(src, soff, W, qs, tok_extra)
        emit_quant_pre_pool(W, qs)

    def emit_quant_main(dst, doff, W, qs):
        sgn, amax, scale, tmp, hs, inv, ssb, q_a = qs
        NB = W // QB
        oprw(p, 'vector', lambda g, W=W, q_a=q_a: g.tensor_scalar(
            q_lt.ap[:, 0:W], q_a.ap[:, 0:W], 0.25, None, ALU.is_gt),
            reads=[q_a], writes=[q_lt])
        oprw(p, 'vector', lambda g, W=W: g.tensor_scalar(
            q_m1.ap[:, 0:W], q_a.ap[:, 0:W], 0.75, None, ALU.is_gt),
            reads=[q_a], writes=[q_m1])
        oprw(p, 'vector', lambda g, W=W: g.tensor_scalar(
            q_m2.ap[:, 0:W], q_a.ap[:, 0:W], 1.25, None, ALU.is_gt),
            reads=[q_a], writes=[q_m2])
        oprw(p, 'vector', lambda g, W=W: g.tensor_scalar(
            q_m3.ap[:, 0:W], q_a.ap[:, 0:W], 1.75, None, ALU.is_gt),
            reads=[q_a], writes=[q_m3])
        oprw(p, 'vector', lambda g, W=W: g.tensor_tensor(
            q_lt.ap[:, 0:W], q_lt.ap[:, 0:W], q_m1.ap[:, 0:W], ALU.add),
            reads=[q_m1], writes=[q_lt])
        oprw(p, 'vector', lambda g, W=W: g.tensor_scalar(
            q_m1.ap[:, 0:W], q_a.ap[:, 0:W], 2.5, 2.0, ALU.is_gt, ALU.mult),
            reads=[q_a], writes=[q_m1])
        oprw(p, 'vector', lambda g, W=W: g.tensor_tensor(
            q_m2.ap[:, 0:W], q_m2.ap[:, 0:W], q_m3.ap[:, 0:W], ALU.add),
            reads=[q_m3], writes=[q_m2])
        oprw(p, 'vector', lambda g, W=W: g.tensor_scalar(
            q_m3.ap[:, 0:W], q_a.ap[:, 0:W], 3.5, 2.0, ALU.is_gt, ALU.mult),
            reads=[q_a], writes=[q_m3])
        oprw(p, 'vector', lambda g, W=W: g.tensor_tensor(
            q_lt.ap[:, 0:W], q_lt.ap[:, 0:W], q_m2.ap[:, 0:W], ALU.add),
            reads=[q_m2], writes=[q_lt])
        oprw(p, 'vector', lambda g, W=W: g.tensor_scalar(
            q_m2.ap[:, 0:W], q_a.ap[:, 0:W], 5.0, 4.0, ALU.is_gt, ALU.mult),
            reads=[q_a], writes=[q_m2])
        oprw(p, 'vector', lambda g, W=W: g.tensor_tensor(
            q_m1.ap[:, 0:W], q_m1.ap[:, 0:W], q_m3.ap[:, 0:W], ALU.add),
            reads=[q_m3], writes=[q_m1])
        oprw(p, 'vector', lambda g, W=W: g.tensor_tensor(
            q_m1.ap[:, 0:W], q_m1.ap[:, 0:W], q_m2.ap[:, 0:W], ALU.add),
            reads=[q_m2], writes=[q_m1])
        oprw(p, 'vector', lambda g, W=W: g.tensor_tensor(
            q_g2.ap[:, 0:W], q_lt.ap[:, 0:W], q_m1.ap[:, 0:W], ALU.add),
            reads=[q_lt, q_m1], writes=[q_g2])
        return oprw(p, 'vector', lambda v, W=W, doff=doff, ssb=ssb: v.tensor_tensor(
            dst.ap[:, doff:doff + W], q_g2.ap[:, 0:W], ssb.ap[:, 0:W],
            ALU.mult), reads=[q_g2, ssb], writes=[dst])

    def emit_quant(src, soff, dst, doff, W, tok_extra=(), eng='vector'):
        emit_quant_pre(src, soff, W, QSET0, tok_extra)
        return emit_quant_main(dst, doff, W, QSET0)

    dmarw(p, 'sync', lambda E: E.dma_start(ident.ap[:], d_ident[:]), 'win',
          writes=[ident])
    dmarw(p, 'sync', lambda E: E.dma_start(ones1.ap[:], d_ones1[:]), 'win',
          writes=[ones1])
    dmarw(p, 'sync', lambda E: E.dma_start(
        s1col.ap[:], d_scale1.rearrange("o (a p) -> (o p) a", p=128),
        allow_slow_non_contiguous=True), 'win', writes=[s1col])
    dmarw(p, 'sync', lambda E: E.dma_start(
        s2col.ap[:], d_scale2.rearrange("o (a p) -> (o p) a", p=128),
        allow_slow_non_contiguous=True), 'win', writes=[s2col])
    dmarw(p, 'sync', lambda E: E.dma_start(
        gammar.ap[:], d_gamma.partition_broadcast(128)), 'win', writes=[gammar])
    dmarw(p, 'sync', lambda E: E.dma_start(rowtmp.ap[:], d_pb[:]), 'win',
          writes=[rowtmp])
    oprw(p, 'vector', lambda v: v.tensor_tensor(
        gpb.ap[:], rowtmp.ap[:], gammar.ap[0:1, :], ALU.mult),
        reads=[rowtmp, gammar], writes=[gpb])
    dmarw(p, 'sync', lambda E: E.dma_start(rowtmp2.ap[:], d_c2b[:]), 'win',
          writes=[rowtmp2])
    oprw(p, 'vector', lambda v: v.tensor_tensor(
        gc2b.ap[:], rowtmp2.ap[:], gammar.ap[0:1, :], ALU.mult),
        reads=[rowtmp2, gammar], writes=[gc2b])
    dmarw(p, 'gpsimd', lambda E: E.dma_start(c1brow.ap[:], d_c1b[:]), 'gwin',
          writes=[c1brow])
    wTsh = esA.enter_context(nc.sbuf_tensor('wTs', [128, 8 * 3 * C], BF16))
    wTsC = [Buf(wTsh, f'wTs{kc}') for kc in range(8)]
    wld = SB('wld', [128, 2 * 3 * C], F32, stack=esA)
    for kc in range(8):
        half = kc % 2
        ld = dmarw(p, 'sync', lambda E, kc=kc, half=half: E.dma_start(
            wld.ap[:, half * 3 * C:(half + 1) * 3 * C],
            d_qkvT[kc * 128:(kc + 1) * 128, :]), 'wqkv', writes=[wld])
        oprw(p, 'scalar', lambda sc, kc=kc, half=half: sc.activation(
            wTsC[kc].ap[:, kc * 3 * C:(kc + 1) * 3 * C],
            wld.ap[:, half * 3 * C:(half + 1) * 3 * C],
            AF.Copy, scale=s1col.ap[:, kc:kc + 1]),
            reads=[wld, s1col], writes=[wTsC[kc]])

    xb = [SB('xb0', [128, C], F32, stack=esA), SB('xb1', [128, C], F32, stack=esA), SB('xb2', [128, C], F32, stack=esA)]
    xn1P = [SB('xn1a', [128, C], BF16, stack=esA), SB('xn1b', [128, C], BF16, stack=esA), SB('xn1c', [128, C], BF16, stack=esA)]
    xTP = [SB('xTa', [128, C], BF16, stack=esA), SB('xTb', [128, C], BF16, stack=esA)]
    ksq1 = SB('ksq1', [128, C], F32, stack=esA)
    ksqP = [ksq1, ksq1]
    kphiP = [SB('kphia', [128, C], BF16, stack=esA), SB('kphib', [128, C], BF16, stack=esA)]
    vextP = [SB('vexta', [128, NH * 65], BF16, stack=esA), SB('vextb', [128, NH * 65], BF16, stack=esA)]
    qphiP = [SB('qphia', [128, C], BF16, stack=esA), SB('qphib', [128, C], BF16, stack=esA)]

    for _ve in vextP:
        oprw(p, 'vector', lambda v, _ve=_ve: v.memset(_ve.ap[:], 1.0), writes=[_ve])
    oprw(p, 'vector', lambda v: v.memset(kv_b.ap[:], 0.0), writes=[kv_b])

    wjobs = []
    for r in range(IFF // 128):
        wjobs.append((d_wg, d_wgq, r, 0, C))
    for r in range(IFF // 128):
        wjobs.append((d_wu, d_wuq, r, 0, C))
    for r in range(C // 128):
        wjobs.append((d_wd, d_wdq, r, 0, WJW))
        wjobs.append((d_wd, d_wdq, r, WJW, WJW))
    wj_i = 0

    def emit_wjob_load(i):
        src, dst, r, co, W = wjobs[i]
        buf = wqld[i % 2]
        dmarw(p, 'gpsimd', lambda E, src=src, r=r, co=co, W=W, buf=buf: E.dma_start(
            buf.ap[:, 0:W], src[r * 128:(r + 1) * 128, co:co + W]),
            f'wstr{i % 2}', writes=[buf])

    def emit_wjob_prerest(i, qs=None):
        src, dst, r, co, W = wjobs[i]
        buf = wqld[i % 2]
        qs = qs or (QSET1 if i % 2 else QSET0)
        emit_quant_pre(buf, 0, min(1024, W), qs)

    def emit_wjob_pre(i, qs=None):
        emit_wjob_load(i)
        emit_wjob_prerest(i, qs)

    def emit_wjob_main(i, qs=None):
        src, dst, r, co, W = wjobs[i]
        buf = wqld[i % 2]
        qs = qs or (QSET1 if i % 2 else QSET0)
        emit_quant_main(wqout, 0, min(1024, W), qs)
        for off in range(1024, W, 1024):
            w = min(1024, W - off)
            emit_quant_pre(buf, off, w, qs)
            emit_quant_main(wqout, off, w, qs)
        dmarw(p, 'gpsimd', lambda E, dst=dst, r=r, co=co, W=W: E.dma_start(
            dst[r * 128:(r + 1) * 128, co:co + W], wqout.ap[:, 0:W]), 'wstw',
            reads=[wqout])

    def emit_wjob(i, qs=None):
        emit_wjob_pre(i, qs)
        emit_wjob_main(i, qs)

    n_w_per_tile = 1

    def stageA1a(t):
        x = xb[t % 3]
        mv = mvP[t % 3]
        dmarw(p, 'sync', lambda E, t=t, x=x: E.dma_start(
            x.ap[:], d_x_all[t * 128:(t + 1) * 128, :]), f'xinA{t % 3}',
            writes=[x])
        oprw(p, 'vector', lambda v, x=x, mv=mv: v.bn_stats(
            mv.ap[:, 0:6], x.ap[:, 0:512]), reads=[x], writes=[mv])
        oprw(p, 'vector', lambda v, x=x, mv=mv: v.bn_stats(
            mv.ap[:, 6:12], x.ap[:, 512:1024]), reads=[x], writes=[mv])
        oprw(p, 'vector', lambda v, mv=mv: v.bn_aggr(
            mv.ap[:, 12:14], mv.ap[:, 0:12]), writes=[mv])
        oprw(p, 'vector', lambda v, mv=mv: v.tensor_scalar(
            mv.ap[:, 14:15], mv.ap[:, 13:14], 1e-5, None, ALU.add), writes=[mv])
        oprw(p, 'vector', lambda v, mv=mv: v.tensor_copy(
            mv.ap[:, 15:16], mv.ap[:, 13:14]), writes=[mv])
        oprw(p, 'scalar', lambda sc, mv=mv: sc.activation(
            mv.ap[:, 14:16], mv.ap[:, 14:16], AF.Ln), writes=[mv])
        oprw(p, 'scalar', lambda sc, mv=mv: sc.activation(
            mv.ap[:, 14:16], mv.ap[:, 14:16], AF.Exp, scale=-0.5), writes=[mv])

    def stageA1b(t):
        x = xb[t % 3]
        xn1 = xn1P[t % 3]
        mv = mvP[t % 3]
        is_own = (t % cpb) == 0
        o = t // cpb
        if is_own:
            oprw(p, 'vector', lambda v, o=o, mv=mv: v.tensor_copy(
                stats.ap[:, 2 * o:2 * o + 1], mv.ap[:, 12:13]),
                reads=[mv], writes=[stats])
            oprw(p, 'vector', lambda v, o=o, mv=mv: v.tensor_copy(
                stats.ap[:, 2 * o + 1:2 * o + 2], mv.ap[:, 14:15]),
                reads=[mv], writes=[stats])
        oprw(p, 'vector', lambda v, x=x, mv=mv, xn1=xn1: v.tensor_scalar(
            xn1.ap[:], x.ap[:], mv.ap[:, 12:13], mv.ap[:, 15:16],
            ALU.subtract, ALU.mult), reads=[x, mv], writes=[xn1])

    def stageA2(t):
        xn1 = xn1P[t % 3]
        xT = xTP[t % 2]
        ksq = ksqP[t % 2]
        kphi = kphiP[t % 2]
        vext = vextP[t % 2]
        qphi = qphiP[t % 2]
        is_own = (t % cpb) == 0
        o = t // cpb
        for kc in range(8):
            oprw(p, 'tensor', lambda T, kc=kc, xn1=xn1: T.transpose(
                TR.ap[:, kc * 128:(kc + 1) * 128],
                xn1.ap[:, kc * 128:(kc + 1) * 128], ident.ap[:]),
                reads=[xn1, ident], writes=[TR])
        oprw(p, 'scalar' if t < 40 else 'vector',
             lambda e_, xT=xT: e_.copy(xT.ap[:], TR.ap[:]) if hasattr(e_, 'copy')
             else e_.tensor_copy(xT.ap[:], TR.ap[:]),
             reads=[TR], writes=[xT])
        for kc in range(8):
            for hh in range(2):
                oprw(p, 'tensor', lambda T, kc=kc, hh=hh, xT=xT: T.matmul(
                    (PA if hh == 0 else PB).ap[:],
                    xT.ap[:, kc * 128:(kc + 1) * 128],
                    wTsC[kc].ap[:, kc * 3 * C + C + hh * 512: kc * 3 * C + C + (hh + 1) * 512],
                    start=(kc == 0), stop=(kc == 7)),
                    reads=[xT, wTsC[kc]], writes=[PA if hh == 0 else PB])
        for kc in range(8):
            for hh in range(2):
                oprw(p, 'tensor', lambda T, kc=kc, hh=hh, xT=xT: T.matmul(
                    (PC if hh == 0 else PD).ap[:],
                    xT.ap[:, kc * 128:(kc + 1) * 128],
                    wTsC[kc].ap[:, kc * 3 * C + 2 * C + hh * 512: kc * 3 * C + 2 * C + (hh + 1) * 512],
                    start=(kc == 0), stop=(kc == 7)),
                    reads=[xT, wTsC[kc]], writes=[PC if hh == 0 else PD])
        oprw(p, 'scalar', lambda sc, ksq=ksq: sc.activation(
            ksq.ap[:, 0:512], PA.ap[:], AF.Square), reads=[PA], writes=[ksq])
        oprw(p, 'scalar', lambda sc, ksq=ksq: sc.activation(
            ksq.ap[:, 512:1024], PB.ap[:], AF.Square), reads=[PB], writes=[ksq])
        oprw(p, 'scalar', lambda sc, ksq=ksq, kphi=kphi: sc.activation(
            kphi.ap[:], ksq.ap[:], AF.Exp, scale=-0.5), reads=[ksq], writes=[kphi])
        if is_own:
            for kc in range(8):
                for hh in range(2):
                    oprw(p, 'tensor', lambda T, kc=kc, hh=hh, xT=xT: T.matmul(
                        (PA if hh == 0 else PB).ap[:],
                        xT.ap[:, kc * 128:(kc + 1) * 128],
                        wTsC[kc].ap[:, kc * 3 * C + hh * 512: kc * 3 * C + (hh + 1) * 512],
                        start=(kc == 0), stop=(kc == 7)),
                        reads=[xT, wTsC[kc]], writes=[PA if hh == 0 else PB])
        oprw(p, 'scalar', lambda sc, vext=vext: sc.copy(
            vext.ap[:].rearrange("p (h e) -> p h e", e=65)[:, 0:8, 0:64],
            PC.ap[:].rearrange("p (h e) -> p h e", e=64)[:, 0:8, :]),
            reads=[PC], writes=[vext])
        oprw(p, 'scalar', lambda sc, vext=vext: sc.copy(
            vext.ap[:].rearrange("p (h e) -> p h e", e=65)[:, 8:16, 0:64],
            PD.ap[:].rearrange("p (h e) -> p h e", e=64)[:, 0:8, :]),
            reads=[PD], writes=[vext])
        if is_own:
            oprw(p, 'scalar', lambda sc, ksq=ksq: sc.activation(
                ksq.ap[:, 0:512], PA.ap[:], AF.Square), reads=[PA], writes=[ksq])
            oprw(p, 'scalar', lambda sc, ksq=ksq: sc.activation(
                ksq.ap[:, 512:1024], PB.ap[:], AF.Square), reads=[PB], writes=[ksq])
            oprw(p, 'scalar', lambda sc, ksq=ksq, qphi=qphi: sc.activation(
                qphi.ap[:], ksq.ap[:], AF.Exp, scale=-0.5),
                reads=[ksq], writes=[qphi])
            dmarw(p, 'scalar', lambda E, o=o, qphi=qphi: E.dma_start(
                d_qspill[o * 128:(o + 1) * 128, :], qphi.ap[:]), 'qsp',
                reads=[qphi])

    def stageA2kv(t):
        kphi = kphiP[t % 2]
        vext = vextP[t % 2]
        for j in range(8):
            bank = (KV1, KV2, KV3)[j // 3]
            slot = j % 3
            st = (t == 0 and slot == 0)
            sp = (t == T_batch - 1 and j in (2, 5, 7))
            oprw(p, 'tensor', lambda T, j=j, bank=bank, slot=slot, st=st, sp=sp, kphi=kphi, vext=vext: T.matmul(
                bank.ap[:, slot * 130:(slot + 1) * 130],
                kphi.ap[:, j * 128:(j + 1) * 128],
                vext.ap[:, j * 130:(j + 1) * 130],
                start=st, stop=sp),
                reads=[kphi, vext], writes=[bank])

    stageA1a(0)
    stageA1b(0)
    stageA1a(1)
    w_pre = 0
    w_load = 0
    for t in range(T_batch):
        if t + 2 < T_batch:
            stageA1a(t + 2)
        if t + 1 < T_batch:
            stageA1b(t + 1)
        if (t < 34 or t % 2 == 0) and w_load < min(len(wjobs), 38):
            emit_wjob_load(w_load)
            w_load += 1
            if w_pre < w_load - 1:
                emit_wjob_prerest(w_pre)
                w_pre += 1
        stageA2(t)
        if t > 0:
            stageA2kv(t - 1)
        if wj_i < w_pre - 1:
            emit_wjob_main(wj_i)
            wj_i += 1
    stageA2kv(T_batch - 1)
    while w_pre < w_load:
        emit_wjob_prerest(w_pre)
        w_pre += 1
    while wj_i < w_pre:
        emit_wjob_main(wj_i)
        wj_i += 1

    for j in range(8):
        bank = (KV1, KV2, KV3)[j // 3]
        slot = j % 3
        oprw(p, 'vector', lambda v, j=j, bank=bank, slot=slot: v.tensor_copy(
            kv_b.ap[0:64, j * 130:j * 130 + 65],
            bank.ap[0:64, slot * 130:slot * 130 + 65]),
            reads=[bank], writes=[kv_b])
        oprw(p, 'vector', lambda v, j=j, bank=bank, slot=slot: v.tensor_copy(
            kv_b.ap[64:128, j * 130 + 65:j * 130 + 130],
            bank.ap[64:128, slot * 130 + 65:slot * 130 + 130]),
            reads=[bank], writes=[kv_b])

    p.barrier()
    esA.close()

    projTgh = esB.enter_context(nc.sbuf_tensor('projTg', [128, 8 * C], BF16))
    projTgC = [Buf(projTgh, f'projTg{kc}') for kc in range(8)]
    wldB = SB('wldB', [128, HID], F32, stack=esB)
    for kc in range(8):
        dmarw(p, 'sync', lambda E, kc=kc: E.dma_start(
            wldB.ap[:, 0:C],
            d_projT[kc * 128:(kc + 1) * 128, :]), 'wB', writes=[wldB])
        oprw(p, 'vector', lambda v, kc=kc: v.tensor_tensor(
            projTgC[kc].ap[:, kc * C:(kc + 1) * C],
            wldB.ap[:, 0:C], gammar.ap[:], ALU.mult),
            reads=[wldB, gammar], writes=[projTgC[kc]])

    qTallh = esB.enter_context(nc.sbuf_tensor('qTall', [128, 8 * S_core], BF16))
    qTallC = [Buf(qTallh, f'qTall{j}') for j in range(8)]
    for j in range(8):
        dmarw(p, 'sync', lambda E, j=j: E.dma_start(
            qTallC[j].ap[:, j * S_core:(j + 1) * S_core],
            d_qspill[:, j * 128:(j + 1) * 128],
            transpose=True), f'qrd{j % 2}', writes=[qTallC[j]])
    den = SB('den', [128, NH], F32, stack=esB)
    zz = SB('zz', [128, NH], F32, stack=esB)
    outzP = [SB('outza', [128, C], BF16, stack=esB), SB('outzb', [128, C], BF16, stack=esB)]
    ozTP = [SB('ozTa', [128, C], BF16, stack=esB), SB('ozTb', [128, C], BF16, stack=esB)]
    xn2P = [SB('xn2a', [128, C], BF16, stack=esB), SB('xn2b', [128, C], BF16, stack=esB)]
    xnBP = [SB('xnBa', [128, C], F32, stack=esB), SB('xnBb', [128, C], F32, stack=esB)]
    oaP = [SB('oaa', [128, C], F32, stack=esB), SB('oab', [128, C], F32, stack=esB)]
    xbB = [SB('xbBa', [128, C], F32, stack=esB), SB('xbBb', [128, C], F32, stack=esB)]
    x1P = [SB('x1a', [128, C], F32, stack=esB), SB('x1b', [128, C], F32, stack=esB)]
    xqBP = [SB('xqBa', [128, C], BF16, stack=esB), SB('xqBb', [128, C], BF16, stack=esB)]

    OUTB = (KV1, KV2, KV3)

    def stageBa(t):
        pr = t % 2
        x = xbB[pr]
        dmarw(p, 'sync', lambda E, t=t, x=x: E.dma_start(
            x.ap[:], d_x_own[t * 128:(t + 1) * 128, :]), f'xin{t % 2}',
            writes=[x])

    def stageBb(t):
        pr = t % 2
        x = xbB[pr]
        outz = outzP[pr]
        ozT = ozTP[pr]
        xn = xnBP[pr]
        x1 = x1P[pr]
        oacc = oaP[pr]
        xn2 = xn2P[pr]
        mv = mvP[pr]
        oprw(p, 'vector', lambda v, x=x, xn=xn, t=t: v.tensor_scalar(
            xn.ap[:], x.ap[:], stats.ap[:, 2 * t:2 * t + 1],
            stats.ap[:, 2 * t + 1:2 * t + 2], ALU.subtract, ALU.mult),
            reads=[x, stats], writes=[xn])
        for j in range(8):
            bank = OUTB[j // 3]
            slot = j % 3
            oprw(p, 'tensor', lambda T, j=j, bank=bank, slot=slot, t=t: T.matmul(
                bank.ap[:, slot * 130:(slot + 1) * 130],
                qTallC[j].ap[:, j * S_core + t * 128:j * S_core + (t + 1) * 128],
                kv_b.ap[:, j * 130:(j + 1) * 130],
                start=True, stop=True),
                reads=[qTallC[j], kv_b], writes=[bank])
        oprw(p, 'scalar', lambda sc: sc.copy(
            den.ap[:, 0:6],
            OUTB[0].ap[:, 0:390].rearrange("p (h e) -> p h e", e=65)[:, :, 64]),
            reads=[OUTB[0]], writes=[den])
        oprw(p, 'scalar', lambda sc: sc.copy(
            den.ap[:, 6:12],
            OUTB[1].ap[:, 0:390].rearrange("p (h e) -> p h e", e=65)[:, :, 64]),
            reads=[OUTB[1]], writes=[den])
        oprw(p, 'scalar', lambda sc: sc.copy(
            den.ap[:, 12:16],
            OUTB[2].ap[:, 0:260].rearrange("p (h e) -> p h e", e=65)[:, :, 64]),
            reads=[OUTB[2]], writes=[den])
        oprw(p, 'vector', lambda v: v.tensor_scalar(
            den.ap[:], den.ap[:], 1e-6, None, ALU.add), writes=[den])
        oprw(p, 'vector', lambda v: v.reciprocal(zz.ap[:], den.ap[:]),
             reads=[den], writes=[zz])
        for bi, (b0, h0, nh) in enumerate(((OUTB[0], 0, 6), (OUTB[1], 6, 6),
                                           (OUTB[2], 12, 4))):
            oprw(p, 'vector', lambda v, b0=b0, h0=h0, nh=nh, outz=outz: v.tensor_tensor(
                outz.ap[:, h0 * HD:(h0 + nh) * HD].rearrange(
                    "p (h e) -> p h e", e=HD),
                b0.ap[:, 0:nh * 65].rearrange("p (h e) -> p h e", e=65)[:, :, 0:HD],
                zz.ap[:, h0:h0 + nh].unsqueeze(2).broadcast_to([128, nh, HD]),
                ALU.mult), reads=[b0, zz], writes=[outz])
        for kc in range(8):
            oprw(p, 'tensor', lambda T, kc=kc, outz=outz: T.transpose(
                TR.ap[:, kc * 128:(kc + 1) * 128],
                outz.ap[:, kc * 128:(kc + 1) * 128], ident.ap[:]),
                reads=[outz, ident], writes=[TR])
        oprw(p, 'scalar', lambda sc, ozT=ozT: sc.copy(ozT.ap[:], TR.ap[:]),
             reads=[TR], writes=[ozT])
        prA, prB = (PA, PB) if t % 2 == 0 else (PC, PD)
        for hh in range(2):
            pbk = prA if hh == 0 else prB
            for kc in range(8):
                oprw(p, 'tensor', lambda T, kc=kc, hh=hh, pbk=pbk, ozT=ozT: T.matmul(
                    pbk.ap[:], ozT.ap[:, kc * 128:(kc + 1) * 128],
                    projTgC[kc].ap[:, kc * C + hh * 512: kc * C + (hh + 1) * 512],
                    start=(kc == 0), stop=False),
                    reads=[ozT, projTgC[kc]], writes=[pbk])
            oprw(p, 'tensor', lambda T, hh=hh, pbk=pbk: T.matmul(
                pbk.ap[:], ones1.ap[:],
                gpb.ap[:, hh * 512:(hh + 1) * 512], start=False, stop=True),
                reads=[ones1, gpb], writes=[pbk])
    def stageBb2(t):
        pr = t % 2
        x = xbB[pr]
        xn = xnBP[pr]
        x1 = x1P[pr]
        oacc = oaP[pr]
        xn2 = xn2P[pr]
        mv = mvP[pr]
        prA, prB = (PA, PB) if t % 2 == 0 else (PC, PD)
        for hh, pbk in ((0, prA), (1, prB)):
            oprw(p, 'vector', lambda v, hh=hh, pbk=pbk, x1=x1, xn=xn: v.tensor_tensor(
                x1.ap[:, hh * 512:(hh + 1) * 512], pbk.ap[:],
                xn.ap[:, hh * 512:(hh + 1) * 512], ALU.add),
                reads=[pbk, xn], writes=[x1])
            oprw(p, 'vector', lambda v, hh=hh, oacc=oacc, x1=x1, x=x: v.tensor_tensor(
                oacc.ap[:, hh * 512:(hh + 1) * 512],
                x1.ap[:, hh * 512:(hh + 1) * 512],
                x.ap[:, hh * 512:(hh + 1) * 512], ALU.add),
                reads=[x1, x], writes=[oacc])
        dmarw(p, 'scalar', lambda E, t=t, oacc=oacc: E.dma_start(
            d_oaspill[t * 128:(t + 1) * 128, :], oacc.ap[:]), 'oas', reads=[oacc])
        oprw(p, 'vector', lambda v, mv=mv, x1=x1: v.bn_stats(
            mv.ap[:, 0:6], x1.ap[:, 0:512]), reads=[x1], writes=[mv])
        oprw(p, 'vector', lambda v, mv=mv, x1=x1: v.bn_stats(
            mv.ap[:, 6:12], x1.ap[:, 512:1024]), reads=[x1], writes=[mv])
        oprw(p, 'vector', lambda v, mv=mv: v.bn_aggr(
            mv.ap[:, 12:14], mv.ap[:, 0:12]), writes=[mv])
        oprw(p, 'vector', lambda v, mv=mv: v.tensor_scalar(
            mv.ap[:, 14:15], mv.ap[:, 12:13], mv.ap[:, 12:13], None,
            ALU.mult), writes=[mv])
        oprw(p, 'vector', lambda v, mv=mv: v.tensor_tensor(
            mv.ap[:, 14:15], mv.ap[:, 14:15], mv.ap[:, 13:14], ALU.add),
            writes=[mv])
        oprw(p, 'scalar', lambda sc, mv=mv: sc.activation(
            mv.ap[:, 14:15], mv.ap[:, 14:15], AF.Ln, scale=1024.0), writes=[mv])
        oprw(p, 'scalar', lambda sc, mv=mv: sc.activation(
            mv.ap[:, 14:15], mv.ap[:, 14:15], AF.Exp, scale=-0.5), writes=[mv])
        oprw(p, 'vector', lambda v, mv=mv, x1=x1, xn2=xn2: v.tensor_scalar(
            xn2.ap[:], x1.ap[:], mv.ap[:, 14:15], 32.0, ALU.mult, ALU.mult),
            reads=[x1, mv], writes=[xn2])
        dmarw(p, 'scalar', lambda E, t=t, xn2=xn2: E.dma_start(
            d_xn2spill[t * 128:(t + 1) * 128, :], xn2.ap[:]), 'xn2s', reads=[xn2])
        if t > 0:
            xqp = xqBP[(t - 1) % 2]
            emit_quant_main(xqp, 0, C, QSET0)
            dmarw(p, 'scalar', lambda E, t=t, xqp=xqp: E.dma_start(
                d_xqspill[(t - 1) * 128:t * 128, :], xqp.ap[:]), 'xqs',
                reads=[xqp])
        emit_quant_pre(xn, 0, C, QSET0)

    stageBa(0)
    for t in range(T_own):
        stageBb(t)
        if t > 0:
            stageBb2(t - 1)
        if t + 1 < T_own:
            stageBa(t + 1)
        if t % 3 == 2 and wj_i < min(len(wjobs), 40):
            emit_wjob(wj_i, QSET1)
            wj_i += 1
    stageBb2(T_own - 1)
    xqp = xqBP[(T_own - 1) % 2]
    emit_quant_main(xqp, 0, C, QSET0)
    dmarw(p, 'scalar', lambda E, xqp=xqp: E.dma_start(
        d_xqspill[(T_own - 1) * 128:T_own * 128, :], xqp.ap[:]), 'xqs',
        reads=[xqp])

    p.barrier()
    esB.close()

    c1Tsh = esB2.enter_context(nc.sbuf_tensor('c1Ts', [128, 8 * HID], BF16))
    c1TsC = [Buf(c1Tsh, f'c1Ts{kc}') for kc in range(8)]
    c2Tgh = esB2.enter_context(nc.sbuf_tensor('c2Tg', [128, 16 * C], BF16))
    c2TgC = [Buf(c2Tgh, f'c2Tg{kc}') for kc in range(16)]
    wldB2 = SB('wldB2', [128, C], F32, stack=esB2)
    for kc in range(8):
        for hf in range(2):
            dmarw(p, 'sync', lambda E, kc=kc, hf=hf: E.dma_start(
                wldB2.ap[:],
                d_c1T[kc * 128:(kc + 1) * 128, hf * C:(hf + 1) * C]), 'wB',
                writes=[wldB2])
            oprw(p, 'vector', lambda v, kc=kc, hf=hf: v.tensor_scalar(
                c1TsC[kc].ap[:, kc * HID + hf * C:kc * HID + (hf + 1) * C],
                wldB2.ap[:],
                s2col.ap[:, kc:kc + 1], None, ALU.mult),
                reads=[wldB2, s2col], writes=[c1TsC[kc]])
    for kc in range(16):
        dmarw(p, 'sync', lambda E, kc=kc: E.dma_start(
            wldB2.ap[:, 0:C],
            d_c2T[kc * 128:(kc + 1) * 128, :]), 'wB', writes=[wldB2])
        oprw(p, 'vector', lambda v, kc=kc: v.tensor_tensor(
            c2TgC[kc].ap[:, kc * C:(kc + 1) * C],
            wldB2.ap[:, 0:C], gammar.ap[:], ALU.mult),
            reads=[wldB2, gammar], writes=[c2TgC[kc]])

    x2Tallh = esB2.enter_context(nc.sbuf_tensor('x2Tall', [128, 8 * S_core], BF16))
    x2TallC = [Buf(x2Tallh, f'x2Tall{j}') for j in range(8)]
    for j in range(8):
        dmarw(p, 'sync', lambda E, j=j: E.dma_start(
            x2TallC[j].ap[:, j * S_core:(j + 1) * S_core],
            d_xn2spill[:, j * 128:(j + 1) * 128],
            transpose=True), f'xn2r{j % 2}', writes=[x2TallC[j]])
    hddP = [SB('hdda', [128, HID], BF16, stack=esB2), SB('hddb', [128, HID], BF16, stack=esB2)]
    hdTP = [SB('hdTa', [128, HID], BF16, stack=esB2), SB('hdTb', [128, HID], BF16, stack=esB2)]
    oaiP = [SB('oaia', [128, C], F32, stack=esB2), SB('oaib', [128, C], F32, stack=esB2)]

    def stageB2oai(t):
        oai = oaiP[t % 2]
        dmarw(p, 'sync', lambda E, t=t, oai=oai: E.dma_start(
            oai.ap[:], d_oaspill[t * 128:(t + 1) * 128, :]), 'oar', writes=[oai])

    def stageB2m(t):
        pr = t % 2
        hdd = hddP[pr]
        C1B = (PA, PB, PC, PD)
        for hh in range(4):
            pbk = C1B[hh]
            for kc in range(8):
                oprw(p, 'tensor', lambda T, kc=kc, hh=hh, pbk=pbk, t=t: T.matmul(
                    pbk.ap[:], x2TallC[kc].ap[:, kc * S_core + t * 128:kc * S_core + (t + 1) * 128],
                    c1TsC[kc].ap[:, kc * HID + hh * 512: kc * HID + (hh + 1) * 512],
                    start=(kc == 0), stop=False),
                    reads=[x2TallC[kc], c1TsC[kc]], writes=[pbk])
            oprw(p, 'tensor', lambda T, hh=hh, pbk=pbk: T.matmul(
                pbk.ap[:], ones1.ap[:],
                c1brow.ap[:, hh * 512:(hh + 1) * 512], start=False, stop=True),
                reads=[ones1, c1brow], writes=[pbk])
        for hh in range(4):
            pbk = C1B[hh]
            oprw(p, 'scalar', lambda sc, hh=hh, pbk=pbk, hdd=hdd: sc.activation(
                hdd.ap[:, hh * 512:(hh + 1) * 512], pbk.ap[:], AF.Gelu),
                reads=[pbk], writes=[hdd])

    def stageB2c(t):
        pr = t % 2
        hdd = hddP[pr]
        hdT = hdTP[pr]
        oai = oaiP[pr]
        yb = oai
        for rnd in range(2):
            for kc in range(rnd * 8, rnd * 8 + 8):
                oprw(p, 'tensor', lambda T, kc=kc, rnd=rnd, hdd=hdd: T.transpose(
                    TR.ap[:, (kc - rnd * 8) * 128:(kc - rnd * 8 + 1) * 128],
                    hdd.ap[:, kc * 128:(kc + 1) * 128], ident.ap[:]),
                    reads=[hdd, ident], writes=[TR])
            oprw(p, 'scalar', lambda sc, rnd=rnd, hdT=hdT: sc.copy(
                hdT.ap[:, rnd * 1024:(rnd + 1) * 1024], TR.ap[:]),
                reads=[TR], writes=[hdT])
        for hh in range(2):
            pbk = KV1 if hh == 0 else KV2
            for kc in range(16):
                oprw(p, 'tensor', lambda T, kc=kc, hh=hh, pbk=pbk, hdT=hdT: T.matmul(
                    pbk.ap[:], hdT.ap[:, kc * 128:(kc + 1) * 128],
                    c2TgC[kc].ap[:, kc * C + hh * 512: kc * C + (hh + 1) * 512],
                    start=(kc == 0), stop=False),
                    reads=[hdT, c2TgC[kc]], writes=[pbk])
            oprw(p, 'tensor', lambda T, hh=hh, pbk=pbk: T.matmul(
                pbk.ap[:], ones1.ap[:],
                gc2b.ap[:, hh * 512:(hh + 1) * 512], start=False, stop=True),
                reads=[ones1, gc2b], writes=[pbk])
        for hh, pbk in ((0, KV1), (1, KV2)):
            oprw(p, 'vector', lambda v, hh=hh, pbk=pbk, yb=yb, oai=oai: v.tensor_tensor(
                yb.ap[:, hh * 512:(hh + 1) * 512], pbk.ap[:],
                oai.ap[:, hh * 512:(hh + 1) * 512], ALU.add),
                reads=[pbk, oai], writes=[yb])
        dmarw(p, 'sync', lambda E, t=t, yb=yb: E.dma_start(
            d_y1spill[t * 128:(t + 1) * 128, :], yb.ap[:]), 'y1s', reads=[yb])

    stageB2oai(0)
    for t in range(T_own):
        stageB2m(t)
        if t > 0:
            stageB2c(t - 1)
        if t + 1 < T_own:
            stageB2oai(t + 1)
        if wj_i < len(wjobs):
            emit_wjob(wj_i)
            wj_i += 1
    stageB2c(T_own - 1)

    while wj_i < len(wjobs):
        emit_wjob(wj_i)
        wj_i += 1

    p.barrier()
    esB2.close()
    esW.close()


    wgTh = esF1.enter_context(nc.sbuf_tensor('wgT', [128, 8 * IFF], BF16))
    wgTC = [Buf(wgTh, f'wgT{kc}') for kc in range(8)]
    wuTh = esF1.enter_context(nc.sbuf_tensor('wuT', [128, 8 * IFF], BF16))
    wuTC = [Buf(wuTh, f'wuT{kc}') for kc in range(8)]
    for kc in range(8):
        dmarw(p, 'sync', lambda E, kc=kc: E.dma_start(
            wgTC[kc].ap[:, kc * IFF:(kc + 1) * IFF],
            d_wgq[:, kc * 128:(kc + 1) * 128], transpose=True), f'wgt{kc % 2}',
            writes=[wgTC[kc]])
        dmarw(p, 'sync', lambda E, kc=kc: E.dma_start(
            wuTC[kc].ap[:, kc * IFF:(kc + 1) * IFF],
            d_wuq[:, kc * 128:(kc + 1) * 128], transpose=True), f'wut{kc % 2}',
            writes=[wuTC[kc]])

    xqTallh = esF1.enter_context(nc.sbuf_tensor('xqTall', [128, 8 * S_core], BF16))
    xqTallC = [Buf(xqTallh, f'xqTall{j}') for j in range(8)]
    for j in range(8):
        dmarw(p, 'sync', lambda E, j=j: E.dma_start(
            xqTallC[j].ap[:, j * S_core:(j + 1) * S_core],
            d_xqspill[:, j * 128:(j + 1) * 128],
            transpose=True), f'xqr{j % 2}', writes=[xqTallC[j]])
    hbufP = [SB('hbufa', [128, IFF], F32, stack=esF1), SB('hbufb', [128, IFF], F32, stack=esF1)]
    hsilP = [SB('hsila', [128, 512], F32, stack=esF1), SB('hsilb', [128, 512], F32, stack=esF1)]
    hq1 = SB('hq1', [128, IFF], BF16, stack=esF1)
    hqP = [hq1, hq1]

    def stageF1m(t):
        hbuf = hbufP[t % 2]
        for occ in range(6):
            w = 512 if occ < 5 else IFF - 5 * 512
            pg = PA if occ % 2 == 0 else PC
            pu = PB if occ % 2 == 0 else PD
            for kc in range(8):
                oprw(p, 'tensor', lambda T, kc=kc, occ=occ, w=w, pg=pg, t=t: T.matmul(
                    pg.ap[:, 0:w],
                    xqTallC[kc].ap[:, kc * S_core + t * 128:kc * S_core + (t + 1) * 128],
                    wgTC[kc].ap[:, kc * IFF + occ * 512: kc * IFF + occ * 512 + w],
                    start=(kc == 0), stop=(kc == 7)),
                    reads=[xqTallC[kc], wgTC[kc]], writes=[pg])
            for kc in range(8):
                oprw(p, 'tensor', lambda T, kc=kc, occ=occ, w=w, pu=pu, t=t: T.matmul(
                    pu.ap[:, 0:w],
                    xqTallC[kc].ap[:, kc * S_core + t * 128:kc * S_core + (t + 1) * 128],
                    wuTC[kc].ap[:, kc * IFF + occ * 512: kc * IFF + occ * 512 + w],
                    start=(kc == 0), stop=(kc == 7)),
                    reads=[xqTallC[kc], wuTC[kc]], writes=[pu])
            hsil = hsilP[occ % 2]
            oprw(p, 'scalar', lambda sc, occ=occ, w=w, pg=pg, hsil=hsil: sc.activation(
                hsil.ap[:, 0:w], pg.ap[:, 0:w], AF.Silu), reads=[pg], writes=[hsil])
            oprw(p, 'scalar', lambda sc, occ=occ, w=w, pu=pu, hbuf=hbuf: sc.copy(
                hbuf.ap[:, occ * 512:occ * 512 + w], pu.ap[:, 0:w]),
                reads=[pu], writes=[hbuf])
            oprw(p, 'gpsimd', lambda g, occ=occ, w=w, hsil=hsil, hbuf=hbuf: g.tensor_tensor(
                hbuf.ap[:, occ * 512:occ * 512 + w], hsil.ap[:, 0:w],
                hbuf.ap[:, occ * 512:occ * 512 + w], ALU.mult),
                reads=[hsil], writes=[hbuf])

    def stageF1q(t):
        hbuf = hbufP[t % 2]
        hq = hqP[t % 2]
        emit_quant_pre_core(hbuf, 0, 1024, QSET0)
        emit_quant_pre_pool(1024, QSET0, u_eng='vector')
        emit_quant_pre_core(hbuf, 1024, 1024, QSET1)
        emit_quant_pre_pool(1024, QSET1, u_eng='vector')
        emit_quant_main(hq, 0, 1024, QSET0)
        emit_quant_pre_core(hbuf, 2048, IFF - 2048, QSET0)
        emit_quant_pre_pool(IFF - 2048, QSET0, u_eng='vector')
        emit_quant_main(hq, 1024, 1024, QSET1)
        emit_quant_main(hq, 2048, IFF - 2048, QSET0)
        dmarw(p, 'scalar', lambda E, t=t, hq=hq: E.dma_start(
            d_hqspill[t * 128:(t + 1) * 128, :], hq.ap[:]), 'hqs', reads=[hq])

    for t in range(T_own):
        stageF1m(t)
        if t > 0:
            stageF1q(t - 1)
    stageF1q(T_own - 1)

    p.barrier()
    esF1.close()
    esQ.close()

    wdTh = esF2.enter_context(nc.sbuf_tensor('wdT', [128, 22 * C], BF16))
    wdTC = [Buf(wdTh, f'wdT{kc}') for kc in range(22)]
    for kc in range(22):
        dmarw(p, 'sync', lambda E, kc=kc: E.dma_start(
            wdTC[kc].ap[:, kc * C:(kc + 1) * C],
            d_wdq[:, kc * 128:(kc + 1) * 128], transpose=True), f'wdt{kc % 3}',
            writes=[wdTC[kc]])
    hqTallh = esF2.enter_context(nc.sbuf_tensor('hqTall', [128, 22 * S_core], BF16))
    hqTallC = [Buf(hqTallh, f'hqTall{j}') for j in range(22)]
    for j in range(22):
        dmarw(p, 'sync', lambda E, j=j: E.dma_start(
            hqTallC[j].ap[:, j * S_core:(j + 1) * S_core],
            d_hqspill[:, j * 128:(j + 1) * 128],
            transpose=True), f'hqr{j % 3}', writes=[hqTallC[j]])
    fbufP = [SB('fbufa', [128, C], F32, stack=esF2), SB('fbufb', [128, C], F32, stack=esF2)]
    xbF = [SB('xbFa', [128, C], F32, stack=esF2), SB('xbFb', [128, C], F32, stack=esF2)]

    def stageF2a(t):
        xfb = xbF[t % 2]
        dmarw(p, 'sync', lambda E, t=t, xfb=xfb: E.dma_start(
            xfb.ap[:], d_y1spill[t * 128:(t + 1) * 128, :]), f'xin{t % 2}',
            writes=[xfb])

    def stageF2b(t):
        fbuf = fbufP[t % 2]
        xfb = xbF[t % 2]
        pa, pb = (PA, PB) if t % 2 == 0 else (PC, PD)
        for kc in range(22):
            for hh in range(2):
                oprw(p, 'tensor', lambda T, kc=kc, hh=hh, t=t, pa=pa, pb=pb: T.matmul(
                    (pa if hh == 0 else pb).ap[:],
                    hqTallC[kc].ap[:, kc * S_core + t * 128:kc * S_core + (t + 1) * 128],
                    wdTC[kc].ap[:, kc * C + hh * 512: kc * C + (hh + 1) * 512],
                    start=(kc == 0), stop=(kc == 21)),
                    reads=[hqTallC[kc], wdTC[kc]], writes=[pa if hh == 0 else pb])
        oprw(p, 'vector', lambda v, xfb=xfb, fbuf=fbuf, pa=pa: v.tensor_tensor(
            fbuf.ap[:, 0:512], pa.ap[:], xfb.ap[:, 0:512], ALU.add),
            reads=[pa, xfb], writes=[fbuf])
        oprw(p, 'vector', lambda v, xfb=xfb, fbuf=fbuf, pb=pb: v.tensor_tensor(
            fbuf.ap[:, 512:1024], pb.ap[:], xfb.ap[:, 512:1024], ALU.add),
            reads=[pb, xfb], writes=[fbuf])
        dmarw(p, 'sync', lambda E, t=t, fbuf=fbuf: E.dma_start(
            d_y[t * 128:(t + 1) * 128, :], fbuf.ap[:]), 'yout', reads=[fbuf])

    stageF2a(0)
    for t in range(T_own):
        if t + 1 < T_own:
            stageF2a(t + 1)
        stageF2b(t)

    p.barrier()
    esF2.close()

    p.wait('sync', [('D', 'yout', p.dma_sems['yout'][1])])

    esB.close()
    with nc.Block() as block:
        p.finalize(block)
    es.close()
    return nc


def interleave_x(xb_full, q, S_core, cpb):
    S, C_ = xb_full.shape
    nb = S // 128
    T_own = S_core // 128
    blocks = xb_full.reshape(nb, 128, C_)
    own_ids = list(range(q * T_own, (q + 1) * T_own))
    own_set = set(own_ids)
    other_ids = [i for i in range(nb) if i not in own_set]
    order = []
    oi = ri = 0
    for t in range(nb):
        if t % cpb == 0:
            order.append(own_ids[oi])
            oi += 1
        else:
            order.append(other_ids[ri])
            ri += 1
    return np.ascontiguousarray(blocks[order].reshape(S, C_))


_NC_CACHE = {}


def _get_nc(S_batch, S_core):
    key = (S_batch, S_core)
    if key not in _NC_CACHE:
        _NC_CACHE[key] = build_nc(S_batch, S_core, S_batch // S_core)
    return _NC_CACHE[key]


def kernel(L0_attn_x, L0_attn_scale1, L0_attn_scale2, L0_attn_gamma,
           L0_attn_qkv_w, L0_attn_proj_w, L0_attn_proj_b, L0_attn_conv1_w,
           L0_attn_conv1_b, L0_attn_conv2_w, L0_attn_conv2_b, L0_ffn_w_gate,
           L0_ffn_w_up, L0_ffn_w_down):
    x = np.ascontiguousarray(np.asarray(L0_attn_x, np.float32))
    B, S, _ = x.shape
    n_cores = 8
    cpb = n_cores // B
    S_core = S // cpb
    nc = _get_nc(S, S_core)

    row = lambda a: np.ascontiguousarray(np.asarray(a, np.float32)).reshape(1, -1)
    wT = lambda w: np.ascontiguousarray(np.asarray(w, np.float32).T)
    nat = lambda w: np.ascontiguousarray(np.asarray(w, np.float32))
    common = {
        'scale1': row(L0_attn_scale1), 'scale2': row(L0_attn_scale2),
        'gamma': row(L0_attn_gamma), 'proj_b': row(L0_attn_proj_b),
        'conv1_b': row(L0_attn_conv1_b), 'conv2_b': row(L0_attn_conv2_b),
        'qkv_wT': wT(L0_attn_qkv_w), 'proj_wT': wT(L0_attn_proj_w),
        'conv1_wT': wT(L0_attn_conv1_w), 'conv2_wT': wT(L0_attn_conv2_w),
        'w_gate': nat(L0_ffn_w_gate), 'w_up': nat(L0_ffn_w_up),
        'w_down': nat(L0_ffn_w_down),
        'ident': np.eye(128, dtype=BF),
        'ones1': np.ones((1, 128), dtype=BF),
    }
    in_maps = []
    for c in range(n_cores):
        b = c // cpb
        q = c % cpb
        xb_full = x[b]
        own = np.ascontiguousarray(xb_full[q * S_core:(q + 1) * S_core])
        m = dict(common)
        m['x_all'] = interleave_x(xb_full, q, S_core, cpb)
        m['x_own'] = own
        in_maps.append(m)

    res = run_bass_kernel_spmd(nc, in_maps, core_ids=list(range(n_cores)))
    shards = [res.results[c]['y'] for c in range(n_cores)]
    out = np.empty((B, S, C), np.float32)
    for c in range(n_cores):
        b = c // cpb
        q = c % cpb
        out[b, q * S_core:(q + 1) * S_core] = shards[c]
    return out
